# version 06
# baseline (speedup 1.0000x reference)
import math
import os
import numpy as np
from contextlib import ExitStack

import ml_dtypes
import concourse.bass as bass
import concourse.mybir as mybir
from concourse.bass_utils import run_bass_kernel_spmd
from concourse.bass import IndirectOffsetOnAxis

F32 = mybir.dt.float32
BF16 = mybir.dt.bfloat16
I32 = mybir.dt.int32
AF = mybir.ActivationFunctionType
ALU = mybir.AluOpType
AX = mybir.AxisListType

B, T, D = 2, 8192, 1024
NCORE = 8
NSLOT = 8
NTILE = 2 * NSLOT
TOK = NTILE * 128
SHIFT_A = 20.0
HEADS_A = 16
NEG = -1e30
FFN_E, FFN_H = 8, 4096
EPS_LN = 1e-5
EPS_B = 1e-5
RG8 = [list(range(8))]

ATTN_DBG = 3
A_MAXJJ = 99
A_SLIM = int(os.environ.get("A_SLIM", "0"))
A_SKIP = int(os.environ.get("A_SKIP", "1"))
DBG_TT = 15
ALL_PHASES = ("attn_a", "ffn1", "attn_b", "ffn2")


def _pair_windows():
    sl = _slopes(HEADS_A)
    return [int((97.3 / sl[2 * pr + 1] + 127.0) // 128.0) for pr in range(8)]


def ABW():
    return 2 * sum(8 * i + 7 for i in range(NSLOT)) * 16


def GMAXW():
    return (8 * (NSLOT - 1) + 7) * 16


def _slopes(n):
    start = 2.0 ** (-(8.0 / n))
    return np.asarray([start * start ** i for i in range(n)], dtype=np.float32)


class _EngProxy:
    def __init__(self, prog, name):
        self._p = prog
        self._n = name

    def __getattr__(self, fn):
        def emit(*a, waits=(), inc=None, chan=None, **k):
            return self._p._add(self._n, fn, a, k, waits, inc, chan)
        return emit

    def mark(self):
        return (self._n, self._p.vals[self._n])


class Prog:
    SERIAL = {"v": True, "s": True, "t": False, "sd": False, "g": False}
    STEP = {"v": 1, "s": 1, "t": 1, "sd": 16, "g": 16}

    def __init__(self):
        self.ops = {n: [] for n in self.STEP}
        self.vals = dict.fromkeys(self.STEP, 0)
        self.chan_eng = {n: n for n in self.STEP}
        self.v = _EngProxy(self, "v")
        self.s = _EngProxy(self, "s")
        self.t = _EngProxy(self, "t")
        self.sd = _EngProxy(self, "sd")
        self.g = _EngProxy(self, "g")
        self._max_waited = {}

    def cmark(self, chan):
        return (chan, self.vals.get(chan, 0))

    def _add(self, eng, fn, a, k, waits, inc, chan):
        chan = chan or eng
        if chan not in self.vals:
            self.vals[chan] = 0
        step = self.STEP[eng] if inc is None else inc
        ws = []
        for (pc, pv) in waits:
            if pc == eng and self.SERIAL.get(eng):
                continue
            key = (eng, pc)
            if pv > 0 and pv > self._max_waited.get(key, 0):
                self._max_waited[key] = pv
                ws.append((pc, pv))
        self.ops[eng].append((fn, a, k, ws, step, chan, self.vals[eng] if chan == eng else -1,
                              self.vals[eng]))
        self.vals[chan] += step
        return (chan, self.vals[chan])

    def barrier(self, exclude=("wsh",)):
        snap = {cn: v for cn, v in self.vals.items() if v > 0 and cn not in exclude}
        for e in self.STEP:
            for pc, pv in snap.items():
                key = (e, pc)
                if pv > self._max_waited.get(key, 0):
                    self._max_waited[key] = pv
                    self.ops[e].append(("__wait__", (pc, pv), {}, [], 0, None, -1, 0))

    def finalize(self, block, nc, es):
        sems = {}
        for cn in self.vals:
            sems[cn] = es.enter_context(nc.semaphore("sem_" + cn))
        secmap = {"v": block.vector, "s": block.scalar, "t": block.tensor,
                  "sd": block.sync, "g": block.gpsimd}

        def runner(eng_name):
            def run(e):
                serial = self.SERIAL[eng_name]
                engsem = sems[eng_name]
                for (fn, a, k, ws, step, chan, selfwait_val, _ev) in self.ops[eng_name]:
                    if fn == "__wait__":
                        pc, pv = a
                        e.wait_ge(sems[pc], pv)
                        continue
                    for (pc, pv) in ws:
                        e.wait_ge(sems[pc], pv)
                    r = getattr(e, fn)(*a, **k)
                    if serial and chan == eng_name and selfwait_val > 0:
                        r._wait_ge(engsem, selfwait_val)
                    if step:
                        r.then_inc(sems[chan], step)
            return run
        for n, sec in secmap.items():
            sec(runner(n))


class Ctx:
    pass


def emit_ln_tile(p, c, src_ap, out_ap, extra_dep=()):
    v, s = p.v, p.s
    m = v.tensor_reduce(c.ln_nm[:], src_ap, AX.X, ALU.add, waits=extra_dep)
    v.tensor_scalar(c.ln_nm[:], c.ln_nm[:], -1.0 / D, None, ALU.mult)
    mnm = v.mark()
    s.activation(c.ln_dump[:], src_ap, AF.Square, accum_out=c.ln_ssq[:], waits=[mnm])
    mssq = s.mark()
    v.tensor_mul(c.ln_m2[:], c.ln_nm[:], c.ln_nm[:], waits=[mssq])
    v.tensor_scalar(c.ln_ssq[:], c.ln_ssq[:], 1.0 / D, None, ALU.mult)
    v.tensor_sub(c.ln_ssq[:], c.ln_ssq[:], c.ln_m2[:])
    mvar = v.mark()
    s.activation(c.ln_ssq[:], c.ln_ssq[:], AF.Ln, bias=c.cpool[:, 0:1], scale=1.0, waits=[mvar])
    s.activation(c.ln_ssq[:], c.ln_ssq[:], AF.Exp, scale=-0.5)
    mrstd = s.mark()
    v.tensor_scalar(out_ap, src_ap, c.ln_nm[:, 0:1], c.ln_ssq[:, 0:1],
                    ALU.add, ALU.mult, waits=[mrstd])
    return v.mark()


def emit_transpose8(p, c, src_tile_bf, dst_fn, dep):
    marks = []
    prev_evac = [None, None]
    for dc in range(8):
        bank = dc % 2
        w = [dep]
        if prev_evac[bank] is not None:
            w.append(prev_evac[bank])
        p.t.transpose(c.pT[bank][:], src_tile_bf[:, dc * 128:(dc + 1) * 128],
                      c.ident_bf[:], waits=w)
        mt = p.t.mark()
        p.v.tensor_copy(dst_fn(dc), c.pT[bank][:], waits=[mt])
        prev_evac[bank] = p.v.mark()
        marks.append(prev_evac[bank])
    return marks[-1]


def emit_ffn_prologue(p, c):
    p.sd.dma_start(out=c.f_rw[:].rearrange("p (dc e) -> p dc e", dc=8),
                   in_=c.d_rw.rearrange("(dc p) e -> p dc e", p=128), chan="fpro")
    p.sd.dma_start(out=c.f_rb_row[:], in_=c.d_rb, chan="fpro")
    mrw = p.cmark("fpro")
    mrb = mrw
    p.t.matmul(c.pR8[:], c.ones_row[:], c.f_rb_row[:], start=True, stop=True,
               waits=[mrb])
    mrbb = p.t.mark()
    p.v.tensor_copy(c.f_rb_rep[:], c.pR8[:], waits=[mrbb])
    mrbrep = p.v.mark()

    tp_done = []
    for tt in range(NTILE):
        mln = emit_ln_tile(p, c, c.x_res[:, tt * 1024:(tt + 1) * 1024], c.f_xn[:],
                           extra_dep=())
        p.v.tensor_scalar(c.f_xn32[:], c.x_res[:, tt * 1024:(tt + 1) * 1024],
                          c.ln_nm[:, 0:1], c.ln_ssq[:, 0:1], ALU.add, ALU.mult)
        mln32 = p.v.mark()
        mtp = emit_transpose8(
            p, c, c.f_xn[:],
            lambda dc, tt=tt: c.f_xnT[:, (dc * NTILE + tt) * 128:(dc * NTILE + tt) * 128 + 128],
            mln)
        tp_done.append(mtp)
        prev = [None, None]
        for dc in range(8):
            bk = dc % 2
            w = [mln32]
            if prev[bk] is not None:
                w.append(prev[bk])
            p.t.transpose(c.pT32[bk][:], c.f_xn32[:, dc * 128:(dc + 1) * 128],
                          c.ident32[:], waits=w)
            mtt = p.t.mark()
            p.v.tensor_copy(c.f_xnT32[:, dc * 128:(dc + 1) * 128], c.pT32[bk][:], waits=[mtt])
            prev[bk] = p.v.mark()
        mtp32 = p.v.mark()
        for dc in range(8):
            w = [mtp32, mrw] if dc == 0 else []
            if dc == 0 and tt == 0:
                w.append(mrbrep)
            p.t.matmul(c.pR8[:], c.f_xnT32[:, dc * 128:(dc + 1) * 128],
                       c.f_rw[:, dc * FFN_E:(dc + 1) * FFN_E],
                       start=(dc == 0), stop=(dc == 7), waits=w)
        mrl = p.t.mark()
        p.v.tensor_add(c.f_rlog[:], c.pR8[:], c.f_rb_rep[:], waits=[mrl])
        p.v.reduce_max(c.f_rmax[:], c.f_rlog[:], AX.X)
        p.v.tensor_scalar(c.f_mask[:], c.f_rlog[:], c.f_rmax[:, 0:1], None, ALU.is_equal)
        p.v.tensor_scalar(c.f_rmax[:], c.f_rmax[:], -1.0, None, ALU.mult)
        mrm = p.v.mark()
        p.s.activation(c.f_rlog[:], c.f_rlog[:], AF.Exp, bias=c.f_rmax[:, 0:1],
                       accum_out=c.f_esum[:], waits=[mrm])
        mre = p.s.mark()
        p.v.reciprocal(c.f_esum[:], c.f_esum[:], waits=[mre])
        p.v.tensor_scalar(c.gates[:, tt * FFN_E:(tt + 1) * FFN_E],
                          c.f_mask[:], c.f_esum[:, 0:1], None, ALU.mult)

    return p.v.mark()


def emit_ffn_experts(p, c, m_gates, grp_tiles=4, img_ready=False):
    NG = NTILE // grp_tiles
    w1_use = [[], []]
    w2_use = [[], []]
    pa_use = [None, None]
    pg_use = [None, None]
    y_use = None
    act_ready = {}
    wb1 = {}
    wb2 = {}

    for grp in range(NG):
        for e in range(FFN_E):
            for j in range(8):
                buf = (e * 8 + j) % 2
                w = list(w1_use[buf])
                if img_ready or (e, j) in wb1:
                    if (e, j) in wb1:
                        w.append(wb1[(e, j)])
                    p.g.dma_start(out=c.f_w1a[buf][:], in_=c.d_w1sh[e, j, 0],
                                  waits=w, chan=f"w1_{buf}")
                    p.g.dma_start(out=c.f_w1g[buf][:], in_=c.d_w1sh[e, j, 1],
                                  chan=f"w1_{buf}")
                    mw1 = p.cmark(f"w1_{buf}")
                    w1_use[buf] = []
                else:
                    p.g.dma_start(out=c.f_w1a[buf][:].rearrange("p (dc n) -> p dc n", dc=8),
                                  in_=c.d_w1[e, :, j * 512:(j + 1) * 512]
                                  .rearrange("(dc p) n -> p dc n", p=128), waits=w,
                                  chan=f"w1_{buf}")
                    p.g.dma_start(out=c.f_w1g[buf][:].rearrange("p (dc n) -> p dc n", dc=8),
                                  in_=c.d_w1[e, :, FFN_H + j * 512:FFN_H + (j + 1) * 512]
                                  .rearrange("(dc p) n -> p dc n", p=128), chan=f"w1_{buf}")
                    mw1 = p.cmark(f"w1_{buf}")
                    p.sd.dma_start(out=c.d_w1sh[e, j, 0], in_=c.f_w1a[buf][:],
                                   waits=[mw1], chan="w1wb")
                    p.sd.dma_start(out=c.d_w1sh[e, j, 1], in_=c.f_w1g[buf][:],
                                   chan="w1wb")
                    wb1[(e, j)] = p.cmark("w1wb")
                    w1_use[buf] = [wb1[(e, j)]]
                for hc in range(4):
                    ch = j * 4 + hc
                    pb = ch % 2
                    w = [mw1]
                    if ch < 2 and grp == 0 and e == 0:
                        w.append(m_gates)
                    if pa_use[pb] is not None:
                        w.append(pa_use[pb])
                    if pg_use[pb] is not None:
                        w.append(pg_use[pb])
                    xs = c.f_xnT
                    base = lambda dc: (dc * NTILE + grp * grp_tiles) * 128
                    for dc in range(8):
                        p.t.matmul(c.pA[pb][:],
                                   c.f_w1a[buf][:, dc * 512 + hc * 128: dc * 512 + hc * 128 + 128],
                                   xs[:, base(dc): base(dc) + grp_tiles * 128],
                                   start=(dc == 0), stop=(dc == 7),
                                   waits=w if dc == 0 else [])
                    for dc in range(8):
                        p.t.matmul(c.pG[pb][:],
                                   c.f_w1g[buf][:, dc * 512 + hc * 128: dc * 512 + hc * 128 + 128],
                                   xs[:, base(dc): base(dc) + grp_tiles * 128],
                                   start=(dc == 0), stop=(dc == 7))
                    mag = p.t.mark()
                    w1_use[buf] = ([wb1[(e, j)]] if (e, j) in wb1 else []) + [mag]
                    p.s.activation(c.f_silu[pb][:], c.pA[pb][:], AF.Silu, waits=[mag])
                    pa_use[pb] = p.s.mark()
                    p.v.tensor_mul(c.f_actT[:, ch * (grp_tiles * 128):(ch + 1) * (grp_tiles * 128)],
                                   c.pG[pb][:], c.f_silu[pb][:], waits=[pa_use[pb]])
                    pg_use[pb] = p.v.mark()
                    act_ready[ch] = pg_use[pb]

            GT = grp_tiles
            for nh in range(2):
                for cb in range(4):
                    buf = (nh * 4 + cb) % 2
                    w = list(w2_use[buf])
                    key = (e, nh, cb)
                    if img_ready or key in wb2:
                        if key in wb2:
                            w.append(wb2[key])
                        p.g.dma_start(out=c.f_w2c[buf][:], in_=c.d_w2sh[e, nh, cb],
                                      waits=w, chan=f"w2_{buf}")
                        mw2 = p.cmark(f"w2_{buf}")
                        wbdep2 = []
                    else:
                        p.g.dma_start(out=c.f_w2c[buf][:].rearrange("p (cc n) -> p cc n", cc=8),
                                      in_=c.d_w2[e, cb * 1024:(cb + 1) * 1024,
                                                 nh * 512:(nh + 1) * 512]
                                      .rearrange("(cc p) n -> p cc n", p=128),
                                      waits=w, chan=f"w2_{buf}")
                        mw2 = p.cmark(f"w2_{buf}")
                        p.sd.dma_start(out=c.d_w2sh[e, nh, cb], in_=c.f_w2c[buf][:],
                                       waits=[mw2], chan="w2wb")
                        wb2[key] = p.cmark("w2wb")
                        wbdep2 = [wb2[key]]
                    for ch8 in range(8):
                        ch = cb * 8 + ch8
                        w = [mw2, act_ready[ch]]
                        if ch == 0 and y_use is not None:
                            w.append(y_use)
                        for ti in range(4):
                            p.t.matmul(c.pY[ti][:],
                                       c.f_actT[:, ch * (GT * 128) + ti * 128: ch * (GT * 128) + ti * 128 + 128],
                                       c.f_w2c[buf][:, ch8 * 512:(ch8 + 1) * 512],
                                       start=(ch == 0), stop=(ch == 31),
                                       waits=w if ti == 0 else [])
                        w2_use[buf] = wbdep2 + [p.t.mark()]
                my = p.t.mark()
                for ti in range(4):
                    tt = grp * grp_tiles + ti
                    p.v.scalar_tensor_tensor(
                        c.x_res[:, tt * 1024 + nh * 512: tt * 1024 + nh * 512 + 512],
                        c.pY[ti][:],
                        c.gates[:, tt * FFN_E + e: tt * FFN_E + e + 1],
                        c.x_res[:, tt * 1024 + nh * 512: tt * 1024 + nh * 512 + 512],
                        ALU.mult, ALU.add, waits=[my])
                y_use = p.v.mark()




def emit_attn_a_proj(p, c):
    p.v.memset(c.a_QTz[0][64:128, :], 0.0)
    p.v.memset(c.a_QTz[1][0:64, :], 0.0)
    xst_use = [None, None]
    for tt in range(NTILE):
        xb = tt % 2
        w = [] if xst_use[xb] is None else [xst_use[xb]]
        p.sd.dma_start(out=c.a_xst[xb][:], in_=c.x_in[tt * 128:(tt + 1) * 128, :],
                       waits=w, chan=f"xst{xb}")
        mx = p.cmark(f"xst{xb}")
        mln = emit_ln_tile(p, c, c.a_xst[xb][:], c.a_xn[:], extra_dep=[mx])
        emit_transpose8(
            p, c, c.a_xn[:],
            lambda dc, tt=tt: c.a_xnT[:, (dc * NTILE + tt) * 128:(dc * NTILE + tt) * 128 + 128],
            mln)
        xst_use[xb] = p.s.mark()
    mtp_all = p.v.mark()

    wb_use = [None, None]
    pq_use = [None, None]
    pkv_use = None
    ktmp_use = None
    ktst_use = None
    for cg in range(6):
        buf = cg % 2
        w = []
        if wb_use[buf] is not None:
            w.append(wb_use[buf])
        p.g.dma_start(out=c.a_wbuf[buf][:].rearrange("p (dc n) -> p dc n", dc=8),
                      in_=c.d_wqkv[:, cg * 512:(cg + 1) * 512]
                      .rearrange("(dc p) n -> p dc n", p=128), waits=w, chan=f"wq{buf}")
        mw = p.cmark(f"wq{buf}")
        if cg < 2:
            QW = min(512, TOK)
            NQG = TOK // QW
            for hh in range(8):
                h = cg * 8 + hh
                pr, odd = h // 2, h % 2
                for tg in range(NQG):
                    pb = (hh * NQG + tg) % 2
                    w = [mw]
                    if cg == 0 and hh == 0 and tg == 0:
                        w.append(mtp_all)
                    if pq_use[pb] is not None:
                        w.append(pq_use[pb])
                    for dc in range(8):
                        p.t.matmul(c.pQ[pb][:],
                                   c.a_wbuf[buf][:, dc * 512 + hh * 64: dc * 512 + hh * 64 + 64],
                                   c.a_xnT[:, dc * NTILE * 128 + tg * QW: dc * NTILE * 128 + tg * QW + QW],
                                   start=(dc == 0), stop=(dc == 7),
                                   waits=w if dc == 0 else [])
                    mq = p.t.mark()
                    wb_use[buf] = mq
                    dst = c.a_QTz[odd][odd * 64:(odd + 1) * 64,
                                       pr * TOK + tg * QW: pr * TOK + tg * QW + QW]
                    wq2 = [mq]
                    p.v.tensor_scalar(dst, c.pQ[pb][:], 0.125, None, ALU.mult, waits=wq2)
                    pq_use[pb] = p.v.mark()
        else:
            for tt in range(NTILE):
                w = [mw]
                if cg == 2 and tt == 0:
                    w.append(mtp_all)
                if pkv_use is not None:
                    w.append(pkv_use)
                for dc in range(8):
                    p.t.matmul(c.pKV[:],
                               c.a_xnT[:, (dc * NTILE + tt) * 128:(dc * NTILE + tt) * 128 + 128],
                               c.a_wbuf[buf][:, dc * 512:(dc + 1) * 512],
                               start=(dc == 0), stop=(dc == 7),
                               waits=w if dc == 0 else [])
                mkv = p.t.mark()
                wb_use[buf] = mkv
                ww = [mkv]
                if ktmp_use is not None:
                    ww.append(ktmp_use)
                p.v.tensor_copy(c.a_ktmp[:], c.pKV[:], waits=ww)
                pkv_use = p.v.mark()
                if cg >= 4:
                    p.sd.dma_start(out=c.cc_v_in[tt * 128:(tt + 1) * 128,
                                                 (cg - 4) * 512:(cg - 3) * 512],
                                   in_=c.a_ktmp[:], waits=[pkv_use], chan="kvst")
                    ktmp_use = p.cmark("kvst")
                else:
                    prevq = [None, None]
                    for qq in range(4):
                        wq = [pkv_use]
                        if prevq[qq % 2] is not None:
                            wq.append(prevq[qq % 2])
                        p.t.transpose(c.pT[qq % 2][:],
                                      c.a_ktmp[:, qq * 128:(qq + 1) * 128],
                                      c.ident_bf[:], waits=wq)
                        mtt2 = p.t.mark()
                        p.v.tensor_copy(c.a_ktT[:, qq * 128:(qq + 1) * 128],
                                        c.pT[qq % 2][:], waits=[mtt2])
                        prevq[qq % 2] = p.v.mark()
                    mktT = p.v.mark()
                    ktmp_use = mktT
                    wst = [mktT]
                    if ktst_use is not None:
                        wst.append(ktst_use)
                    p.sd.dma_start(
                        out=c.cc_kt_in[tt * 1024 + (cg - 2) * 512:
                                       tt * 1024 + (cg - 2) * 512 + 512, :]
                        .rearrange("(pr d) kp -> d pr kp", d=128),
                        in_=c.a_ktT[:].rearrange("p (pr kp) -> p pr kp", pr=4),
                        waits=wst, chan="ktst")
                    ktst_use = p.cmark("ktst")
    c.m_proj_s = p.s.mark()
    return (p.cmark("kvst"), p.cmark("ktst"))


def emit_attn_a(p, c, m_store):
    NR = 8
    m_store_kv, m_store_kt = m_store
    xst_use = [c.m_proj_s, c.m_proj_s]
    p.g.dma_start(out=c.a_wout[:].rearrange("p (dc n) -> p dc n", dc=8),
                  in_=c.d_wout.rearrange("(dc p) n -> p dc n", p=128), chan="wout",
                  waits=[p.t.mark(), p.v.mark(), c.m_proj_s, m_store_kv, m_store_kt])
    m_wout = p.cmark("wout")
    p.g.collective_compute("AllGather", ALU.bypass, RG8,
                           [c.cc_v_in.opt()], [c.cc_v_all.opt()],
                           waits=[m_store_kv], inc=1, chan="cc")
    p.g.collective_compute("AllGather", ALU.bypass, RG8,
                           [c.cc_kt_in.opt()], [c.cc_kt_all.opt()],
                           waits=[m_store_kt], inc=1, chan="cc")
    m_ag = p.cmark("cc")

    for r in range(NR):
        p.v.memset(c.a_vring[r].rearrange("p (k s) -> p k s", s=66)[:, :, 64:65], 1.0,
                   waits=[m_store_kv, m_store_kt] if r == 0 else [])
    m_ones = p.v.mark()

    m_proj_t = p.t.mark()
    ring_use = [m_proj_t] * NR
    bias_use = [[m_store_kv, m_store_kt], [m_store_kv, m_store_kt]]
    pl_use = [None, None]
    psb_use = {}
    psb_ready = {}
    po_flush = {}
    pb2 = [0]
    pout_use = None
    abias_off = 0
    pb = 0

    if ATTN_DBG < 1:
        return
    CH = 4
    acc_first = {}
    accO_use = None
    A_WIN = _pair_windows()
    for tt in range(NTILE):
        bnum, i = tt // NSLOT, tt % NSLOT
        gmax = 8 * i + 7

        def active(jj, pr):
            if not A_SKIP:
                return True
            return jj == 0 or (8 * i - (jj - 1)) <= A_WIN[pr]

        def prmin_of(jj):
            for pr in range(8):
                if active(jj, pr):
                    return pr
            return 8
        bb = tt % 2
        bu = bias_use[bb]
        w = [] if bu is None else (list(bu) if isinstance(bu, list) else [bu])
        p.sd.dma_start(out=c.a_bias[bb][:, 0:gmax * 16],
                       in_=c.d_abias[:, abias_off:abias_off + gmax * 16], waits=w,
                       chan=f"ab{bb}")
        m_bias = p.cmark(f"ab{bb}")
        abias_off += gmax * 16

        njj = min(gmax + 1, A_MAXJJ)
        nch = (njj + CH - 1) // CH
        for chi in range(nch):
            jlo, jhi = chi * CH, min(njj, chi * CH + CH)
            for jj in range(jlo, jhi):
                r = jj % NR
                sl = jj % (2 * CH)
                if jj == 0:
                    vbuf, vrow = c.cc_v_in, tt * 128
                    ktbuf, ktrow = c.cc_kt_in, tt * 1024
                else:
                    j = jj - 1
                    vbuf = c.cc_v_all
                    vrow = (j % 8) * TOK + (bnum * NSLOT + j // 8) * 128
                    ktbuf = c.cc_kt_all
                    ktrow = (j % 8) * TOK * 8 + (bnum * NSLOT + j // 8) * 1024
                w = [] if ring_use[r] is None else [ring_use[r]]
                if jj == 1:
                    w.append(m_ag)
                if jj == 0:
                    w.append(m_store_kv)
                    w.append(m_store_kt)
                pm = prmin_of(jj)
                pml = pm if A_SLIM else 0
                p.sd.dma_start(
                    out=c.a_kring[r][:, pml * 128:].rearrange("p (pr kp) -> p pr kp", pr=8 - pml),
                    in_=ktbuf[ktrow + pml * 128:ktrow + 1024, :]
                    .rearrange("(pr d) kp -> d pr kp", d=128),
                    waits=w, chan=f"kv{r}")
                for hh in range(2):
                    p.g.dma_start(
                        out=c.a_vring[r].rearrange("p (k m) -> p k m", m=132)[:, pml:8, hh * 66:hh * 66 + 64],
                        in_=vbuf[vrow:vrow + 128, :]
                        .rearrange("p (k n) -> p k n", n=128)[:, pml:8, hh * 64:hh * 64 + 64],
                        waits=w if hh == 0 else [], chan=f"kv{r}")
                m_ring = p.cmark(f"kv{r}")
                m_ring_w = [m_ring, m_ones] if jj == 0 else [m_ring]

                if ATTN_DBG < 2.2:
                    ring_use[r] = m_ring
                    continue
                for pri, pr in enumerate(range(pm, 8)):
                    w1 = list(m_ring_w) if pri == 0 else []
                    if pl_use[pb] is not None:
                        w1.append(pl_use[pb])
                    for odd in range(2):
                        p.t.matmul(c.pL[pb][:, odd * 128:(odd + 1) * 128],
                                   c.a_kring[r][:, pr * 128:(pr + 1) * 128],
                                   c.a_QTz[odd][:, pr * TOK + tt * 128: pr * TOK + tt * 128 + 128],
                                   start=True, stop=True,
                                   waits=w1 if odd == 0 else [])
                    m1 = p.t.mark()
                    ring_use[r] = m1
                    if ATTN_DBG < 2.3:
                        pb ^= 1
                        continue
                    ws = [m1]
                    if jj > 0:
                        ws.append(m_bias)
                    mpsb = psb_use.get((sl, pr))
                    if mpsb is not None:
                        ws.append(mpsb)
                    dst = c.a_P[sl]
                    for odd in range(2):
                        h = pr * 2 + odd
                        bias_ap = (c.diag_bias[:, h:h + 1] if jj == 0
                                   else c.a_bias[bb][:, (jj - 1) * 16 + h:(jj - 1) * 16 + h + 1])
                        p.s.activation(dst[:, pr * 256 + odd * 128: pr * 256 + odd * 128 + 128],
                                       c.pL[pb][:, odd * 128:(odd + 1) * 128],
                                       AF.Exp, bias=bias_ap,
                                       waits=ws if odd == 0 else [])
                    mp = p.s.mark()
                    pl_use[pb] = mp
                    if jj == 0 and ATTN_DBG >= 2.4:
                        p.v.tensor_mul(dst[:, pr * 256:(pr + 1) * 256],
                                       dst[:, pr * 256:(pr + 1) * 256],
                                       c.mask256[:], waits=[mp])
                    pb ^= 1
                psb_ready[jj % (2 * CH)] = (p.s.mark(), p.v.mark())

            if ATTN_DBG < 2.5:
                continue
            for pr in range(8):
                jlist = [jj for jj in range(jlo, jhi) if active(jj, pr)]
                if not jlist:
                    continue
                for odd in range(2):
                    pb2i = pb2[0]
                    first = True
                    for jj in jlist:
                        r = jj % NR
                        sl = jj % (2 * CH)
                        w2 = []
                        if first:
                            w2 += list(psb_ready[(jhi - 1) % (2 * CH)])
                            mflush = po_flush.get(pb2i)
                            if mflush is not None:
                                w2.append(mflush)
                        p.t.matmul(c.psPO[pb2i][:],
                                   c.a_vring[r][:, pr * 132 + odd * 66: pr * 132 + odd * 66 + 65],
                                   c.a_P[sl][:, pr * 256 + odd * 128: pr * 256 + odd * 128 + 128],
                                   start=first, stop=(jj == jlist[-1]),
                                   waits=w2)
                        first = False
                        psb_use[(sl, pr)] = p.t.mark()
                    m2 = p.t.mark()
                    for jj in jlist:
                        ring_use[jj % NR] = m2
                    aslice = c.a_accO[0:65, (pr * 2 + odd) * 128:(pr * 2 + odd) * 128 + 128]
                    wv = [m2]
                    if chi == 0 and accO_use is not None:
                        wv.append(accO_use)
                    if chi == 0:
                        p.v.tensor_copy(aslice, c.psPO[pb2i][:], waits=wv)
                    else:
                        p.v.tensor_add(aslice, aslice, c.psPO[pb2i][:], waits=wv)
                    po_flush[pb2i] = p.v.mark()
                    pb2[0] ^= 1
        m_attn = p.v.mark()
        if ATTN_DBG < 2.95:
            continue

        for pr in range(8):
            for odd in range(2):
                cslice = slice((pr * 2 + odd) * 128, (pr * 2 + odd) * 128 + 128)
                p.v.reciprocal(c.a_recip[:], c.a_accO[64:65, cslice], waits=[m_attn])
                mrc = p.v.mark()
                w = [mrc]
                if pl_use[0] is not None:
                    w.append(pl_use[0])
                p.t.matmul(c.pL[0][0:64, 0:128], c.ones_row[:, 0:64], c.a_recip[:],
                           start=True, stop=True, waits=w)
                mrb = p.t.mark()
                p.s.copy(c.a_rb[:], c.pL[0][0:64, 0:128], waits=[mrb])
                mrbe = p.s.mark()
                pl_use[0] = mrbe
                p.v.tensor_mul(c.a_aT[odd * 64:(odd + 1) * 64, pr * 128:(pr + 1) * 128],
                               c.a_accO[0:64, cslice], c.a_rb[:], waits=[mrbe])
        accO_use = p.v.mark()
        m_norm = p.v.mark()
        bias_use[bb] = p.s.mark()

        if ATTN_DBG < 2.98:
            continue
        xb = tt % 2
        w = [] if xst_use[xb] is None else [xst_use[xb]]
        p.sd.dma_start(out=c.a_xst[xb][:], in_=c.x_in[tt * 128:(tt + 1) * 128, :],
                       waits=w, chan=f"xst{xb}")
        mx = p.cmark(f"xst{xb}")
        for nh in range(2):
            first = True
            for pr in range(8):
                w = [m_norm, m_wout] if first else []
                if first and pout_use is not None:
                    w.append(pout_use)
                p.t.matmul(c.pOut[nh][:],
                           c.a_aT[:, pr * 128:(pr + 1) * 128],
                           c.a_wout[:, pr * 1024 + nh * 512: pr * 1024 + nh * 512 + 512],
                           start=first, stop=(pr == 7), waits=w)
                first = False
            mo = p.t.mark()
            dst = c.a_xst[xb][:, nh * 512:(nh + 1) * 512]
            p.v.tensor_add(dst, dst, c.pOut[nh][:], waits=[mo, mx])
            pout_use = p.v.mark()
        p.sd.dma_start(out=c.x1_dram[tt * 128:(tt + 1) * 128, :], in_=c.a_xst[xb][:],
                       waits=[pout_use], chan="x1st")
        xst_use[xb] = p.cmark("x1st")
        if tt == DBG_TT:
            p.sd.dma_start(out=c.dbg_aT, in_=c.a_aT[:], waits=[m_norm], chan="dbg")
            p.sd.dma_start(out=c.dbg_acc, in_=c.a_accO[:], waits=[m_attn], chan="dbg")
            p.sd.dma_start(out=c.dbg_kv[:, 0:1024], in_=c.cc_v_in[TOK - 256:TOK, :], waits=[m_store_kv], chan="dbg")
            p.sd.dma_start(out=c.dbg_vr, in_=c.a_vring[0][:], waits=[m_attn], chan="dbg")




def emit_attn_b(p, c):
    NCH = T // 128
    p.v.memset(c.b_S[0][:], 0.0)
    p.v.memset(c.b_S[1][:], 0.0)
    xnst = [None, None]
    for tt in range(NTILE):
        xb = tt % 2
        dep = [] if xnst[xb] is None else [xnst[xb]]
        mln = emit_ln_tile(p, c, c.x_res[:, tt * 1024:(tt + 1) * 1024], c.b_xn[xb][:],
                           extra_dep=dep)
        p.sd.dma_start(out=c.cc_xn_in[tt * 128:(tt + 1) * 128, :], in_=c.b_xn[xb][:],
                       waits=[mln], chan="bxnst")
        xnst[xb] = p.cmark("bxnst")
    m_xnst = p.cmark("bxnst")
    p.g.collective_compute("AllGather", ALU.bypass, RG8,
                           [c.cc_xn_in.opt()], [c.cc_xn_all.opt()],
                           waits=[m_xnst], inc=1, chan="cc")
    m_ag = p.cmark("cc")

    p.g.dma_start(out=c.b_wqkvr[:].rearrange("p (dc n) -> p dc n", dc=8),
                  in_=c.d_wbqkvr.rearrange("(dc p) n -> p dc n", p=128), chan="bw")
    p.g.dma_start(out=c.b_wa[:].rearrange("p (dc n) -> p dc n", dc=8),
                  in_=c.d_wba.rearrange("(dc p) n -> p dc n", p=128), chan="bw",
                  allow_slow_non_contiguous=True)
    p.g.dma_start(out=c.b_wo[:].rearrange("p (dc n) -> p dc n", dc=8),
                  in_=c.d_wbo.rearrange("(dc p) n -> p dc n", p=128), chan="bw")
    m_bw = p.cmark("bw")
    p.sd.dma_start(out=c.b_rows[0:1, 0:128], in_=c.d_qnw, chan="brow")
    p.sd.dma_start(out=c.b_rows[0:1, 128:256], in_=c.d_knw, chan="brow")
    p.sd.dma_start(out=c.b_rows[0:1, 256:384], in_=c.d_gnw, chan="brow")
    p.sd.dma_start(out=c.b_rows[0:1, 384:385], in_=c.d_alog, chan="brow")
    p.sd.dma_start(out=c.b_rows[0:1, 385:386], in_=c.d_dtb, chan="brow")
    p.sd.dma_start(out=c.b_ucum[:], in_=c.ucum_d.ap(), chan="brow")
    m_rows = p.cmark("brow")
    p.s.activation(c.b_rows[0:1, 384:385], c.b_rows[0:1, 384:385], AF.Exp,
                   waits=[m_rows])
    m_ea = p.s.mark()
    p.v.tensor_scalar(c.b_rows[0:1, 384:385], c.b_rows[0:1, 384:385], -1.0, None,
                      ALU.mult, waits=[m_ea])
    m_ea2 = p.v.mark()
    p.t.matmul(c.pQKVR[:], c.ones_row[:], c.b_rows[0:1, :], start=True, stop=True,
               waits=[m_ea2])
    m_rep = p.t.mark()
    p.v.tensor_copy(c.b_rep[:], c.pQKVR[:], waits=[m_rep])
    m_repv = p.v.mark()
    qnw_rep = c.b_rep[:, 0:128]
    knw_rep = c.b_rep[:, 128:256]
    gnw_rep = c.b_rep[:, 256:384]
    neg_ea = c.b_rep[:, 384:385]
    dtb_rep = c.b_rep[:, 385:386]

    RB, W = 3, 2
    NST = NCH * B
    st = {"pO": None, "pS": None, "pF": None, "pSm": None, "pQK": m_repv,
          "pA2": None, "pT0": None, "pT1": None}
    xr_use = [None] * RB
    st_done = {}
    scan_rd = {}
    pmk = {}

    def prep(i):
        n, b = i // B, i % B
        ri = i % RB
        war = scan_rd.get(i - RB, [])
        rb = (n % 8) * TOK + (b * NSLOT + n // 8) * 128
        w = [m_ag] + list(war)
        if xr_use[ri] is not None:
            w.append(xr_use[ri])
        for dc in range(8):
            p.sd.dma_start_transpose(
                c.b_xT[ri][:, dc * 128:(dc + 1) * 128],
                c.cc_xn_all[rb:rb + 128, dc * 128:(dc + 1) * 128],
                waits=w if dc == 0 else [], chan=f"bx{ri}")
        m_xT = p.cmark(f"bx{ri}")

        w = [m_xT, m_bw]
        if st["pQK"] is not None:
            w.append(st["pQK"])
        for dc in range(8):
            p.t.matmul(c.pQKVR[:], c.b_xT[ri][:, dc * 128:(dc + 1) * 128],
                       c.b_wqkvr[:, dc * 512:(dc + 1) * 512],
                       start=(dc == 0), stop=(dc == 7), waits=w if dc == 0 else [])
        m_proj = p.t.mark()
        w = [m_proj]
        if st["pA2"] is not None:
            w.append(st["pA2"])
        for dc in range(8):
            p.t.matmul(c.pA2[:], c.b_xT[ri][:, dc * 128:(dc + 1) * 128],
                       c.b_wa[:, dc:dc + 1],
                       start=(dc == 0), stop=(dc == 7), waits=w if dc == 0 else [])
        m_aproj = p.t.mark()
        xr_use[ri] = m_aproj

        p.s.activation(c.b_sqkv[ri][:], c.pQKVR[:, 0:384], AF.Silu,
                       waits=[m_proj, m_repv] + list(war))
        p.s.activation(c.b_scr[ri][:], c.pQKVR[:, 384:512], AF.Sigmoid)
        st["pQK"] = p.s.mark()
        pQK_use = st["pQK"]
        p.v.tensor_add(c.b_z[ri][:], c.pA2[:], dtb_rep, waits=[m_aproj])
        st["pA2"] = p.v.mark()
        p.s.activation(c.b_zab[ri][:], c.b_z[ri][:], AF.Abs, waits=[st["pA2"]])
        p.s.activation(c.b_zab[ri][:], c.b_zab[ri][:], AF.Exp, scale=-1.0)
        p.s.activation(c.b_zab[ri][:], c.b_zab[ri][:], AF.Ln, bias=c.cpool[:, 1:2])
        m_sp1 = p.s.mark()
        p.v.tensor_scalar(c.b_z[ri][:], c.b_z[ri][:], 0.0, None, ALU.max, waits=[m_sp1])
        p.v.tensor_add(c.b_z[ri][:], c.b_z[ri][:], c.b_zab[ri][:])
        p.v.tensor_mul(c.b_z[ri][:], c.b_z[ri][:], neg_ea)

        for qk in range(2):
            sl_ = slice(qk * 128, qk * 128 + 128)
            p.v.tensor_mul(c.b_dmp[ri][:], c.b_sqkv[ri][:, sl_], c.b_sqkv[ri][:, sl_],
                           waits=[pQK_use])
            p.v.reduce_sum(c.b_ss[ri][:], c.b_dmp[ri][:], AX.X)
            mss = p.v.mark()
            p.s.activation(c.b_ss[ri][:], c.b_ss[ri][:], AF.Ln, bias=c.cpool[:, 2:3],
                           scale=1.0 / 128, waits=[mss])
            p.s.activation(c.b_ss[ri][:], c.b_ss[ri][:], AF.Exp, scale=-0.5)
            mrs = p.s.mark()
            p.v.scalar_tensor_tensor(
                c.b_qn[ri][:, sl_], c.b_sqkv[ri][:, sl_], c.b_ss[ri][:, 0:1],
                qnw_rep if qk == 0 else knw_rep, ALU.mult, ALU.mult, waits=[mrs])
        m_qn = p.v.mark()

        p.v.tensor_copy(c.b_buf[ri][:], c.b_scr[ri][:], waits=[pQK_use])
        for r4 in range(4):
            p.v.max(c.b_v8[ri][:], c.b_buf[ri][:])
            p.v.match_replace(c.b_mh[ri][:], c.b_v8[ri][:], c.b_buf[ri][:], -1.0)
            p.v.tensor_copy(c.b_buf[ri][:], c.b_mh[ri][:])
        p.v.tensor_scalar(c.b_mh[ri][:], c.b_buf[ri][:], -1.0, None, ALU.is_equal)
        p.v.tensor_mul(c.b_mh[ri][:], c.b_scr[ri][:], c.b_mh[ri][:])
        p.v.reduce_sum(c.b_ss[ri][:], c.b_mh[ri][:], AX.X)
        p.v.tensor_scalar(c.b_ss[ri][:], c.b_ss[ri][:], 1e-9, None, ALU.add)
        p.v.reciprocal(c.b_ss[ri][:], c.b_ss[ri][:])
        p.v.tensor_scalar(c.b_mh[ri][:], c.b_mh[ri][:], c.b_ss[ri][:, 0:1], None, ALU.mult)
        p.v.tensor_scalar(c.b_f[ri][:], c.b_mh[ri][:], c.b_z[ri][:, 0:1], None, ALU.mult)
        m_f = p.v.mark()

        w = [m_f, m_rows]
        if st["pF"] is not None:
            w.append(st["pF"])
        p.t.matmul(c.pF[:], c.b_ucum[:], c.b_f[ri][:], start=True, stop=True, waits=w)
        m_F = p.t.mark()
        p.s.activation(c.b_E[ri][:], c.pF[:], AF.Exp, waits=[m_F])
        p.s.activation(c.b_Ei[ri][:], c.pF[:], AF.Exp, scale=-1.0)
        st["pF"] = p.s.mark()
        w = [m_f]
        if st["pSm"] is not None:
            w.append(st["pSm"])
        p.t.matmul(c.pSmall[0:1, 0:128], c.ones_col[:], c.b_f[ri][:],
                   start=True, stop=True, waits=w)
        m_fc = p.t.mark()
        p.v.tensor_copy(c.b_fc[ri][:], c.pSmall[0:1, 0:128], waits=[m_fc])
        m_fcv = p.v.mark()
        p.t.transpose(c.pSmall[:, 0:1], c.b_fc[ri][:], c.ident32[0:1, 0:1],
                      waits=[m_fcv])
        m_fct = p.t.mark()
        p.s.activation(c.b_efc[ri][:], c.pSmall[:, 0:1], AF.Exp,
                       waits=[m_fct] + list(war))
        st["pSm"] = p.s.mark()
        m_efc = st["pSm"]

        p.v.tensor_mul(c.b_qt[ri][:], c.b_qn[ri][:, 0:128], c.b_E[ri][:],
                       waits=[st["pF"], m_qn])
        p.s.activation(c.b_ef[ri][:], c.b_f[ri][:], AF.Exp, waits=[m_f])
        m_ef = p.s.mark()
        p.v.tensor_scalar(c.b_ef[ri][:], c.b_ef[ri][:], -1.0, c.cpool[0:128, 1:2],
                          ALU.mult, ALU.add, waits=[m_ef])
        p.v.tensor_mul(c.b_kt[ri][:], c.b_qn[ri][:, 128:256], c.b_ef[ri][:])
        p.v.tensor_mul(c.b_kt[ri][:], c.b_kt[ri][:], c.b_Ei[ri][:])
        m_ktil = p.v.mark()

        w = [m_ktil]
        if st["pT0"] is not None:
            w.append(st["pT0"])
        p.t.transpose(c.pT32b[0][:], c.b_qt[ri][:], c.ident32[:], waits=w)
        mtq = p.t.mark()
        p.v.tensor_copy(c.b_qtT[ri][:], c.pT32b[0][:], waits=[mtq] + list(war))
        st["pT0"] = p.v.mark()
        m_qtT = st["pT0"]
        w = [m_ktil]
        if st["pT1"] is not None:
            w.append(st["pT1"])
        p.t.transpose(c.pT32b[1][:], c.b_kt[ri][:], c.ident32[:], waits=w)
        mtk = p.t.mark()
        p.v.tensor_copy(c.b_ktT[ri][:], c.pT32b[1][:], waits=[mtk])
        st["pT1"] = p.v.mark()

        w = [st["pT0"], st["pT1"], st["pF"]]
        p.t.matmul(c.pF[:], c.b_ktT[ri][:], c.b_qtT[ri][:], start=True, stop=True,
                   waits=w)
        m_at = p.t.mark()
        p.v.tensor_mul(c.b_am[ri][:], c.pF[:], c.b_ucum[:], waits=[m_at] + list(war))
        st["pF"] = p.v.mark()
        pmk[i] = dict(qtT=m_qtT, am=st["pF"], ktil=m_ktil, efc=m_efc, sqkv=pQK_use)

    def scan(i):
        n, b = i // B, i % B
        ri = i % RB
        mk = pmk.pop(i)
        w = [mk["qtT"], mk["am"], mk["sqkv"]]
        if b in st_done:
            w.append(st_done[b])
        if st["pO"] is not None:
            w.append(st["pO"])
        p.t.matmul(c.pO[:], c.b_qtT[ri][:], c.b_S[b][:], start=True, stop=False,
                   waits=w)
        p.t.matmul(c.pO[:], c.b_am[ri][:], c.b_sqkv[ri][:, 256:384],
                   start=False, stop=True)
        m_O = p.t.mark()

        w = [mk["ktil"]]
        if st["pS"] is not None:
            w.append(st["pS"])
        p.t.matmul(c.pS[:], c.b_kt[ri][:], c.b_sqkv[ri][:, 256:384],
                   start=True, stop=True, waits=w)
        m_S = p.t.mark()
        p.v.tensor_add(c.b_S[b][:], c.b_S[b][:], c.pS[:], waits=[m_S, mk["efc"]])
        p.v.tensor_scalar(c.b_S[b][:], c.b_S[b][:], c.b_efc[ri][:, 0:1], None, ALU.mult)
        st_done[b] = p.v.mark()
        st["pS"] = st_done[b]

        p.v.tensor_copy(c.b_o32[:], c.pO[:], waits=[m_O])
        p.v.tensor_mul(c.b_dmp[ri][:], c.b_o32[:], c.b_o32[:])
        p.v.reduce_sum(c.b_oss[:, (b * NCH + n):(b * NCH + n) + 1], c.b_dmp[ri][:], AX.X)
        p.v.tensor_copy(c.b_obf[:, (b * NCH + n) * 128:(b * NCH + n) * 128 + 128],
                        c.b_o32[:])
        st["pO"] = p.v.mark()
        scan_rd[i] = [m_S, st_done[b]]

    for step in range(NST + W):
        if step < NST:
            prep(step)
        if step >= W:
            scan(step - W)

    m_chunks = p.v.mark()
    p.sd.dma_start(out=c.cc_oss_in, in_=c.b_oss[:], waits=[m_chunks], chan="bosst")
    m_oss = p.cmark("bosst")
    p.g.collective_compute("AllReduce", ALU.add, RG8,
                           [c.cc_oss_in.opt()], [c.cc_oss_out.opt()],
                           waits=[m_oss], inc=1, chan="cc")
    m_ar = p.cmark("cc")
    p.sd.dma_start(out=c.b_oss[:], in_=c.cc_oss_out, waits=[m_ar], chan="bossl")
    m_ossl = p.cmark("bossl")
    p.s.activation(c.b_oss[:], c.b_oss[:], AF.Ln, bias=c.cpool[:, 2:3],
                   scale=1.0 / 1024, waits=[m_ossl])
    p.s.activation(c.b_oss[:], c.b_oss[:], AF.Exp, scale=-0.5)
    m_rstd = p.s.mark()
    return m_rstd, gnw_rep, m_bw


def emit_attn_b_out(p, c, m_rstd, gnw_rep, m_bw):
    NCH = T // 128
    ot_use = None
    pOut_use = None
    for n in range(NCH):
        for b in range(B):
            col = b * NCH + n
            rb = (n % 8) * TOK + (b * NSLOT + n // 8) * 128
            p.v.scalar_tensor_tensor(
                c.b_on[:], c.b_obf[:, col * 128: col * 128 + 128],
                c.b_oss[:, col:col + 1], gnw_rep, ALU.mult, ALU.mult,
                waits=[m_rstd])
            m_on = p.v.mark()
            w = [m_on]
            if ot_use is not None:
                w.append(ot_use)
            p.t.transpose(c.pTbf[:], c.b_on[:], c.ident_bf[:], waits=w)
            m_ot = p.t.mark()
            p.v.tensor_copy(c.b_onT[:], c.pTbf[:], waits=[m_ot])
            m_otv = p.v.mark()
            ot_use = m_otv
            w = [m_otv, m_bw]
            if pOut_use is not None:
                w.append(pOut_use)
            for nh in range(2):
                p.t.matmul(c.pWo[nh][:], c.b_onT[:],
                           c.b_wo[:, nh * 512:(nh + 1) * 512],
                           start=True, stop=True, waits=w if nh == 0 else [])
            m_wo = p.t.mark()
            wv = [m_wo]
            if n + b > 0:
                wv.append(p.cmark("brsst"))
            p.v.tensor_copy(c.b_part[:, 0:512], c.pWo[0][:], waits=wv)
            p.v.tensor_copy(c.b_part[:, 512:1024], c.pWo[1][:])
            pOut_use = p.v.mark()
            p.sd.dma_start(out=c.cc_rs_in[rb:rb + 128, :], in_=c.b_part[:],
                           waits=[pOut_use], chan="brsst")
    m_rsst = p.cmark("brsst")
    p.g.collective_compute("ReduceScatter", ALU.add, RG8,
                           [c.cc_rs_in.opt()], [c.cc_rs_out.opt()],
                           waits=[m_rsst], inc=1, chan="cc")
    m_rs = p.cmark("cc")
    xadd_use = None
    for tt in range(NTILE):
        w = [m_rs]
        if xadd_use is not None:
            w.append(xadd_use)
        p.sd.dma_start(out=c.b_stage[:], in_=c.cc_rs_out[tt * 128:(tt + 1) * 128, :],
                       waits=w, chan="brsl")
        m_st = p.cmark("brsl")
        dst = c.x_res[:, tt * 1024:(tt + 1) * 1024]
        p.v.tensor_add(dst, dst, c.b_stage[:], waits=[m_st])
        xadd_use = p.v.mark()


def build_program(phases=ALL_PHASES):
    nc = bass.Bass("TRN2", target_bir_lowering=False)
    es = ExitStack()
    c = Ctx()
    c.nc = nc

    def sbt(name, shape, dtype=F32):
        return es.enter_context(nc.sbuf_tensor(name, shape, dtype))

    def pst(name, shape, dtype=F32):
        return es.enter_context(nc.psum_tensor(name, shape, dtype))

    x_in = nc.dram_tensor("x_c", [TOK, D], F32, kind="ExternalInput").ap()
    y_out = nc.dram_tensor("y", [TOK, D], F32, kind="ExternalOutput").ap()
    c.dbg_aT = nc.dram_tensor("dbg_aT", [128, 1024], BF16, kind="ExternalOutput").ap()
    c.dbg_acc = nc.dram_tensor("dbg_acc", [65, 2048], F32, kind="ExternalOutput").ap()
    c.dbg_kv = nc.dram_tensor("dbg_kv", [256, 2048], BF16, kind="ExternalOutput").ap()
    c.dbg_vr = nc.dram_tensor("dbg_vr", [128, 1056], BF16, kind="ExternalOutput").ap()
    need_ffn = ("ffn1" in phases) or ("ffn2" in phases)
    if "attn_b" in phases:
        c.d_wbqkvr = nc.dram_tensor("wbqkvr", [D, 512], F32, kind="ExternalInput").ap()
        c.d_wba = nc.dram_tensor("wba", [D, 1], F32, kind="ExternalInput").ap()
        c.d_wbo = nc.dram_tensor("wbo", [128, D], F32, kind="ExternalInput").ap()
        c.d_qnw = nc.dram_tensor("qnw", [1, 128], F32, kind="ExternalInput").ap()
        c.d_knw = nc.dram_tensor("knw", [1, 128], F32, kind="ExternalInput").ap()
        c.d_gnw = nc.dram_tensor("gnw", [1, 128], F32, kind="ExternalInput").ap()
        c.d_alog = nc.dram_tensor("alog", [1, 1], F32, kind="ExternalInput").ap()
        c.d_dtb = nc.dram_tensor("dtb", [1, 1], F32, kind="ExternalInput").ap()
        c.cc_xn_in = nc.dram_tensor("cc_xn_in", [TOK, D], BF16, kind="Internal").ap()
        c.cc_xn_all = nc.dram_tensor("cc_xn_all", [NCORE * TOK, D], BF16, kind="Internal").ap()
        c.cc_oss_in = nc.dram_tensor("cc_oss_in", [128, 128], F32, kind="Internal").ap()
        c.cc_oss_out = nc.dram_tensor("cc_oss_out", [128, 128], F32, kind="Internal").ap()
        c.cc_rs_in = nc.dram_tensor("cc_rs_in", [NCORE * TOK, D], BF16, kind="Internal").ap()
        c.cc_rs_out = nc.dram_tensor("cc_rs_out", [TOK, D], BF16, kind="Internal").ap()
        c.ucum_d = nc.inline_tensor(np.triu(np.ones((128, 128), np.float32)), "ucum32")
    if "attn_a" in phases:
        c.d_wqkv = nc.dram_tensor("wqkv", [D, 3 * D], F32, kind="ExternalInput").ap()
        c.d_wout = nc.dram_tensor("wout", [D, D], F32, kind="ExternalInput").ap()
        c.d_abias = nc.dram_tensor("abias", [128, ABW()], F32, kind="ExternalInput").ap()
        c.cc_v_in = nc.dram_tensor("cc_v_in", [TOK, 1024], BF16, kind="Internal").ap()
        c.cc_v_all = nc.dram_tensor("cc_v_all", [NCORE * TOK, 1024], BF16, kind="Internal").ap()
        c.cc_kt_in = nc.dram_tensor("cc_kt_in", [TOK * 8, 128], BF16, kind="Internal").ap()
        c.cc_kt_all = nc.dram_tensor("cc_kt_all", [NCORE * TOK * 8, 128], BF16, kind="Internal").ap()
        c.x1_dram = nc.dram_tensor("x1_dram", [TOK, D], F32, kind="Internal").ap()
    if need_ffn:
        c.d_rw = nc.dram_tensor("rw", [D, FFN_E], F32, kind="ExternalInput").ap()
        c.d_rb = nc.dram_tensor("rb", [1, FFN_E], F32, kind="ExternalInput").ap()
        c.d_w1 = nc.dram_tensor("w1", [FFN_E, D, 2 * FFN_H], F32, kind="ExternalInput").ap()
        c.d_w2 = nc.dram_tensor("w2", [FFN_E, FFN_H, D], F32, kind="ExternalInput").ap()
        c.d_w1sh = nc.dram_tensor("w1sh", [FFN_E, 8, 2, 128, 4096], BF16, kind="Internal").ap()
        c.d_w2sh = nc.dram_tensor("w2sh", [FFN_E, 2, 4, 128, 4096], BF16, kind="Internal").ap()

    id_bf_d = nc.inline_tensor(np.eye(128, dtype=np.float32).astype(ml_dtypes.bfloat16), "id_bf")
    id_32_d = nc.inline_tensor(np.eye(128, dtype=np.float32), "id_32")
    if "attn_a" in phases:
        _pk = np.arange(128, dtype=np.float32)
        _sl = _slopes(HEADS_A)
        diag_b = (_sl[None, :] * _pk[:, None] - SHIFT_A).astype(np.float32)
        diag_b_d = nc.inline_tensor(diag_b, "diag_b")
        um = (np.tril(np.ones((128, 128), np.float32)).T)
        mask256_np = np.concatenate([um, um], axis=1).astype(ml_dtypes.bfloat16)
        mask256_d = nc.inline_tensor(mask256_np, "mask256")
    ones_row_d = nc.inline_tensor(np.ones((1, 128), np.float32), "ones_row_c")
    ones_col_d = nc.inline_tensor(np.ones((128, 1), np.float32), "ones_col_c")

    c.x_res = None
    c.ident_bf = sbt("ident_bf", [128, 128], BF16)
    c.ident32 = sbt("ident32", [128, 128])
    c.ones_row = sbt("ones_row", [1, 128], F32)
    c.ones_col = sbt("ones_col", [128, 1], F32)
    c.cpool = sbt("cpool", [128, 4], F32)
    c.ln_nm = sbt("ln_nm", [128, 1])
    c.ln_ssq = sbt("ln_ssq", [128, 1])
    c.ln_m2 = sbt("ln_m2", [128, 1])
    c.ln_dump = sbt("ln_dump", [128, 1024])
    c.gates = sbt("gates", [128, NTILE * FFN_E])

    p = Prog()
    c.x_in = x_in

    def alloc_x_res(src_dram):
        c.x_res = es.enter_context(nc.sbuf_tensor("x_res", [128, NTILE * 1024], F32))
        for tt in range(NTILE):
            p.sd.dma_start(out=c.x_res[:, tt * 1024:(tt + 1) * 1024],
                           in_=src_dram[tt * 128:(tt + 1) * 128, :], chan="xload")
        p.barrier()

    p.sd.dma_start(out=c.ident_bf[:], in_=id_bf_d.ap())
    p.sd.dma_start(out=c.ident32[:], in_=id_32_d.ap())
    p.sd.dma_start(out=c.ones_col[:], in_=ones_col_d.ap())
    p.sd.dma_start(out=c.ones_row[:], in_=ones_row_d.ap())
    p.v.memset(c.cpool[:, 0:1], EPS_LN)
    p.v.memset(c.cpool[:, 1:2], 1.0)
    p.v.memset(c.cpool[:, 2:3], EPS_B)
    p.v.memset(c.cpool[:, 3:4], 0.0)
    p.barrier()


    phase_scopes = []
    for ph in phases:
        if c.x_res is None and ph != "attn_a":
            alloc_x_res(x_in)
        if ph == "attn_a":
            es2 = ExitStack()
            phase_scopes.append(es2)

            def sA(name, shape, dtype=F32, _es=es2):
                return _es.enter_context(nc.sbuf_tensor(name, shape, dtype))

            c.a_QTz = [sA(f"a_QTz{i}", [128, 8 * TOK], BF16) for i in range(2)]
            c.a_xst = [sA(f"a_xst{i}", [128, 1024]) for i in range(2)]
            c.diag_bias = sA("diag_bias", [128, HEADS_A])
            c.mask256 = sA("mask256s", [128, 256], BF16)
            p.sd.dma_start(out=c.diag_bias[:], in_=diag_b_d.ap())
            p.sd.dma_start(out=c.mask256[:], in_=mask256_d.ap())
            p.barrier()

            es3 = ExitStack()
            c.a_xn = es3.enter_context(nc.sbuf_tensor("a_xn", [128, 1024], BF16))
            c.a_xnT = es3.enter_context(nc.sbuf_tensor("a_xnT", [128, NTILE * 8 * 128], BF16))
            c.a_wbuf = [es3.enter_context(nc.sbuf_tensor(f"a_wbuf{i}", [128, 8 * 512], BF16)) for i in range(2)]
            c.a_ktmp = es3.enter_context(nc.sbuf_tensor("a_ktmp", [128, 512], BF16))
            c.a_ktT = es3.enter_context(nc.sbuf_tensor("a_ktT", [128, 512], BF16))
            c.pT = [es3.enter_context(nc.psum_tensor(f"a_pT{i}", [128, 128], BF16)) for i in range(2)]
            c.pQ = [es3.enter_context(nc.psum_tensor(f"a_pQ{i}", [64, min(512, TOK)])) for i in range(2)]
            c.pKV = es3.enter_context(nc.psum_tensor("a_pKV", [128, 512]))
            m_store = emit_attn_a_proj(p, c)
            es3.close()

            c.a_wout = sA("a_wout", [128, 8 * 1024], BF16)
            c.a_bias = [sA(f"a_bias{i}", [128, GMAXW()]) for i in range(2)]
            c.a_kring = [sA(f"a_kring{i}", [128, 1024], BF16) for i in range(8)]
            c.a_vring = [sA(f"a_vring{i}", [128, 1056], BF16) for i in range(8)]
            c.a_P = [sA(f"a_P{i}", [128, 2048], BF16) for i in range(8)]
            c.a_accO = sA("a_accO", [65, 2048])
            c.a_aT = sA("a_aT", [128, 1024], BF16)
            c.a_recip = sA("a_recip", [1, 128])
            c.a_rb = sA("a_rb", [64, 128])

            es4 = ExitStack()
            c.psPO = [es4.enter_context(nc.psum_tensor(f"a_psPO{i}", [65, 128])) for i in range(2)]
            c.pL = [es4.enter_context(nc.psum_tensor(f"a_pL{i}", [128, 256])) for i in range(2)]
            c.pOut = [es4.enter_context(nc.psum_tensor(f"a_pOut{i}", [128, 512])) for i in range(2)]
            emit_attn_a(p, c, m_store)
            es4.close()
            es2.close()
            p.barrier()
            alloc_x_res(c.x1_dram)
        if ph == "attn_b":
            es2 = ExitStack()
            phase_scopes.append(es2)

            def sB(name, shape, dtype=F32, _es=es2):
                return _es.enter_context(nc.sbuf_tensor(name, shape, dtype))

            RB = 3
            c.b_xn = [sB(f"b_xn{i}", [128, 1024], BF16) for i in range(2)]
            c.b_xT = [sB(f"b_xT{i}", [128, 1024], BF16) for i in range(RB)]
            c.b_wqkvr = sB("b_wqkvr", [128, 8 * 512], BF16)
            c.b_wa = sB("b_wa", [128, 8], BF16)
            c.b_wo = sB("b_wo", [128, 1024], BF16)
            c.b_rows = sB("b_rows", [1, 512])
            c.b_rep = sB("b_rep", [128, 512])
            c.b_ucum = sB("b_ucum", [128, 128])
            c.b_sqkv = [sB(f"b_sqkv{i}", [128, 384]) for i in range(RB)]
            c.b_scr = [sB(f"b_scr{i}", [128, 128]) for i in range(RB)]
            c.b_z = [sB(f"b_z{i}", [128, 1]) for i in range(RB)]
            c.b_zab = [sB(f"b_zab{i}", [128, 1]) for i in range(RB)]
            c.b_ss = [sB(f"b_ss{i}", [128, 1]) for i in range(RB)]
            c.b_qn = [sB(f"b_qn{i}", [128, 256]) for i in range(RB)]
            c.b_buf = [sB(f"b_buf{i}", [128, 128]) for i in range(RB)]
            c.b_v8 = [sB(f"b_v8{i}", [128, 8]) for i in range(RB)]
            c.b_mh = [sB(f"b_mh{i}", [128, 128]) for i in range(RB)]
            c.b_f = [sB(f"b_f{i}", [128, 128]) for i in range(RB)]
            c.b_E = [sB(f"b_E{i}", [128, 128]) for i in range(RB)]
            c.b_Ei = [sB(f"b_Ei{i}", [128, 128]) for i in range(RB)]
            c.b_fc = [sB(f"b_fc{i}", [1, 128]) for i in range(RB)]
            c.b_efc = [sB(f"b_efc{i}", [128, 1]) for i in range(RB)]
            c.b_ef = [sB(f"b_ef{i}", [128, 128]) for i in range(RB)]
            c.b_dmp = [sB(f"b_dmp{i}", [128, 128]) for i in range(RB)]
            c.b_qt = [sB(f"b_qt{i}", [128, 128]) for i in range(RB)]
            c.b_kt = [sB(f"b_kt{i}", [128, 128]) for i in range(RB)]
            c.b_qtT = [sB(f"b_qtT{i}", [128, 128]) for i in range(RB)]
            c.b_ktT = [sB(f"b_ktT{i}", [128, 128]) for i in range(RB)]
            c.b_am = [sB(f"b_am{i}", [128, 128]) for i in range(RB)]
            c.b_S = [sB(f"b_S{i}", [128, 128]) for i in range(2)]
            c.b_o32 = sB("b_o32", [128, 128])
            c.b_obf = sB("b_obf", [128, 128 * 128], BF16)
            c.b_oss = sB("b_oss", [128, 128])
            c.b_on = sB("b_on", [128, 128], BF16)
            c.b_onT = sB("b_onT", [128, 128], BF16)
            c.b_part = sB("b_part", [128, 1024], BF16)
            c.b_stage = sB("b_stage", [128, 1024], BF16)

            es5 = ExitStack()
            c.pQKVR = es5.enter_context(nc.psum_tensor("b_pQKVR", [128, 512]))
            c.pA2 = es5.enter_context(nc.psum_tensor("b_pA2", [128, 1]))
            c.pF = es5.enter_context(nc.psum_tensor("b_pF", [128, 128]))
            c.pSmall = es5.enter_context(nc.psum_tensor("b_pSmall", [128, 128]))
            c.pT32b = [es5.enter_context(nc.psum_tensor(f"b_pT32{i}", [128, 128])) for i in range(2)]
            c.pO = es5.enter_context(nc.psum_tensor("b_pO", [128, 128]))
            c.pS = es5.enter_context(nc.psum_tensor("b_pS", [128, 128]))
            m_rstd, gnw_rep, m_bw = emit_attn_b(p, c)
            es5.close()
            es6 = ExitStack()
            c.pTbf = es6.enter_context(nc.psum_tensor("b_pTbf", [128, 128], BF16))
            c.pWo = [es6.enter_context(nc.psum_tensor(f"b_pWo{i}", [128, 512])) for i in range(2)]
            emit_attn_b_out(p, c, m_rstd, gnw_rep, m_bw)
            es6.close()
            es2.close()
            p.barrier()
        if ph in ("ffn1", "ffn2"):
            es2 = ExitStack()
            phase_scopes.append(es2)

            def s2(name, shape, dtype=F32, _es=es2, _ph=ph):
                return _es.enter_context(nc.sbuf_tensor(name + "_" + _ph, shape, dtype))

            def p2(name, shape, dtype=F32, _es=es2, _ph=ph):
                return _es.enter_context(nc.psum_tensor(name + "_" + _ph, shape, dtype))

            c.f_xn = s2("f_xn", [128, 1024], BF16)
            c.f_xnT = s2("f_xnT", [128, 8 * NTILE * 128], BF16)
            c.f_actT = s2("f_actT", [128, 32 * 4 * 128], BF16)
            c.f_w1a = [s2(f"f_w1a{i}", [128, 8 * 512], BF16) for i in range(2)]
            c.f_w1g = [s2(f"f_w1g{i}", [128, 8 * 512], BF16) for i in range(2)]
            c.f_w2c = [s2(f"f_w2c{i}", [128, 8 * 512], BF16) for i in range(2)]
            c.f_rw = s2("f_rw", [128, 8 * FFN_E])
            c.f_xn32 = s2("f_xn32", [128, 1024])
            c.f_xnT32 = s2("f_xnT32", [128, 1024])
            c.f_rb_row = s2("f_rb_row", [1, FFN_E])
            c.f_rb_rep = s2("f_rb_rep", [128, FFN_E])
            c.f_rlog = s2("f_rlog", [128, FFN_E])
            c.f_rmax = s2("f_rmax", [128, 1])
            c.f_esum = s2("f_esum", [128, 1])
            c.f_silu = [s2(f"f_silu{i}", [128, 512]) for i in range(2)]
            c.f_mask = s2("f_mask", [128, FFN_E])
            es3 = ExitStack()
            c.pT = [es3.enter_context(nc.psum_tensor(f"pT{i}_" + ph, [128, 128], BF16)) for i in range(2)]
            c.pT32 = [es3.enter_context(nc.psum_tensor(f"pT32{i}_" + ph, [128, 128])) for i in range(2)]
            c.pR8 = es3.enter_context(nc.psum_tensor("pR8_" + ph, [128, FFN_E]))
            m_gates = emit_ffn_prologue(p, c)
            es3.close()
            c.pA = [p2(f"pA{i}", [128, 512]) for i in range(2)]
            c.pG = [p2(f"pG{i}", [128, 512]) for i in range(2)]
            c.pY = [p2(f"pY{i}", [128, 512]) for i in range(4)]
            emit_ffn_experts(p, c, m_gates, grp_tiles=4,
                             img_ready=(ph == "ffn2" and "ffn1" in phases))
            es2.close()
            p.barrier()

    for tt in range(NTILE):
        p.sd.dma_start(out=y_out[tt * 128:(tt + 1) * 128, :],
                       in_=c.x_res[:, tt * 1024:(tt + 1) * 1024], chan="yst")
    p.barrier()

    block = es.enter_context(nc.Block())
    with es:
        p.finalize(block, nc, es)
    return nc


def shard_tokens(x):
    xt = np.ascontiguousarray(x.reshape(B, T // 128, 128, D))
    out = []
    for cid in range(NCORE):
        sl = xt[:, cid::8]
        out.append(np.ascontiguousarray(sl.reshape(TOK, D)))
    return out

def unshard_tokens(parts):
    y = np.empty((B, T // 128, 128, D), np.float32)
    for cid in range(NCORE):
        y[:, cid::8] = parts[cid].reshape(B, NSLOT, 128, D)
    return y.reshape(B, T, D)


def make_abias(cid):
    sl = _slopes(HEADS_A)
    pk = np.arange(128, dtype=np.float32)
    blocks = []
    for tt in range(NTILE):
        i = tt % NSLOT
        g = cid + 8 * i
        gmax = 8 * i + 7
        blk = np.full((128, gmax * 16), NEG, np.float32)
        for j in range(gmax):
            if j < g:
                for h in range(HEADS_A):
                    blk[:, j * 16 + h] = sl[h] * (128.0 * (j - g) + pk) - SHIFT_A
        blocks.append(blk)
    return np.ascontiguousarray(np.concatenate(blocks, axis=1))


_NC_CACHE = {}


def kernel(L0_attn_a_x, L0_attn_a_Wqkv, L0_attn_a_Wout, L0_attn_b_Wq, L0_attn_b_Wk,
           L0_attn_b_Wv, L0_attn_b_Wr, L0_attn_b_Wa, L0_attn_b_A_log, L0_attn_b_dt_bias,
           L0_attn_b_q_norm_w, L0_attn_b_k_norm_w, L0_attn_b_g_norm_w, L0_attn_b_Wo,
           L0_ffn_router_w, L0_ffn_router_b, L0_ffn_w1, L0_ffn_w2):
    phases = ALL_PHASES
    key = phases
    if key not in _NC_CACHE:
        _NC_CACHE[key] = build_program(phases)
    nc = _NC_CACHE[key]
    xs = shard_tokens(np.asarray(L0_attn_a_x, np.float32))
    in_maps = []
    for cid in range(NCORE):
        m = dict(x_c=xs[cid],
                 rw=np.ascontiguousarray(L0_ffn_router_w),
                 rb=np.ascontiguousarray(L0_ffn_router_b).reshape(1, FFN_E),
                 w1=np.ascontiguousarray(L0_ffn_w1),
                 w2=np.ascontiguousarray(L0_ffn_w2),
                 wqkv=np.ascontiguousarray(L0_attn_a_Wqkv),
                 wout=np.ascontiguousarray(L0_attn_a_Wout),
                 abias=make_abias(cid),
                 wbqkvr=np.ascontiguousarray(np.concatenate(
                     [np.asarray(W)[:, cid * 128:(cid + 1) * 128] for W in
                      (L0_attn_b_Wq, L0_attn_b_Wk, L0_attn_b_Wv, L0_attn_b_Wr)],
                     axis=1)),
                 wba=np.ascontiguousarray(np.asarray(L0_attn_b_Wa)[:, cid:cid + 1]),
                 wbo=np.ascontiguousarray(np.asarray(L0_attn_b_Wo)[cid * 128:(cid + 1) * 128, :]),
                 qnw=np.ascontiguousarray(np.asarray(L0_attn_b_q_norm_w).reshape(1, 128)),
                 knw=np.ascontiguousarray(np.asarray(L0_attn_b_k_norm_w).reshape(1, 128)),
                 gnw=np.ascontiguousarray(np.asarray(L0_attn_b_g_norm_w)[cid * 128:(cid + 1) * 128].reshape(1, 128)),
                 alog=np.asarray(L0_attn_b_A_log)[cid].reshape(1, 1).astype(np.float32),
                 dtb=np.asarray(L0_attn_b_dt_bias)[cid].reshape(1, 1).astype(np.float32))
        in_maps.append(m)
    res = run_bass_kernel_spmd(nc, in_maps, core_ids=list(range(NCORE)))
    return unshard_tokens([r["y"] for r in res.results]).astype(np.float32)


# version 07
# speedup vs baseline: 1.1431x; 1.1431x over previous
import math
import os
import numpy as np
from contextlib import ExitStack

import ml_dtypes
import concourse.bass as bass
import concourse.mybir as mybir
from concourse.bass_utils import run_bass_kernel_spmd
from concourse.bass import IndirectOffsetOnAxis

F32 = mybir.dt.float32
BF16 = mybir.dt.bfloat16
I32 = mybir.dt.int32
AF = mybir.ActivationFunctionType
ALU = mybir.AluOpType
AX = mybir.AxisListType

B, T, D = 2, 8192, 1024
NCORE = 8
NSLOT = 8
NTILE = 2 * NSLOT
TOK = NTILE * 128
SHIFT_A = 20.0
HEADS_A = 16
NEG = -1e30
FFN_E, FFN_H = 8, 4096
EPS_LN = 1e-5
EPS_B = 1e-5
RG8 = [list(range(8))]

ATTN_DBG = 3
A_MAXJJ = 99
A_SLIM = int(os.environ.get("A_SLIM", "0"))
A_SKIP = int(os.environ.get("A_SKIP", "1"))
DBG_TT = 15
ALL_PHASES = ("attn_a", "ffn1", "attn_b", "ffn2")


def _pair_windows():
    sl = _slopes(HEADS_A)
    return [int((97.3 / sl[2 * pr + 1] + 127.0) // 128.0) for pr in range(8)]


def ABW():
    return 2 * sum(8 * i + 7 for i in range(NSLOT)) * 16


def GMAXW():
    return (8 * (NSLOT - 1) + 7) * 16


def _slopes(n):
    start = 2.0 ** (-(8.0 / n))
    return np.asarray([start * start ** i for i in range(n)], dtype=np.float32)


class _EngProxy:
    def __init__(self, prog, name):
        self._p = prog
        self._n = name

    def __getattr__(self, fn):
        def emit(*a, waits=(), inc=None, chan=None, **k):
            return self._p._add(self._n, fn, a, k, waits, inc, chan)
        return emit

    def mark(self):
        return (self._n, self._p.vals[self._n])


class Prog:
    SERIAL = {"v": True, "s": True, "t": False, "sd": False, "g": False}
    STEP = {"v": 1, "s": 1, "t": 1, "sd": 16, "g": 16}

    def __init__(self):
        self.ops = {n: [] for n in self.STEP}
        self.vals = dict.fromkeys(self.STEP, 0)
        self.chan_eng = {n: n for n in self.STEP}
        self.v = _EngProxy(self, "v")
        self.s = _EngProxy(self, "s")
        self.t = _EngProxy(self, "t")
        self.sd = _EngProxy(self, "sd")
        self.g = _EngProxy(self, "g")
        self._max_waited = {}

    def cmark(self, chan):
        return (chan, self.vals.get(chan, 0))

    def _add(self, eng, fn, a, k, waits, inc, chan):
        chan = chan or eng
        if chan not in self.vals:
            self.vals[chan] = 0
        step = self.STEP[eng] if inc is None else inc
        ws = []
        for (pc, pv) in waits:
            if pc == eng and self.SERIAL.get(eng):
                continue
            key = (eng, pc)
            if pv > 0 and pv > self._max_waited.get(key, 0):
                self._max_waited[key] = pv
                ws.append((pc, pv))
        self.ops[eng].append((fn, a, k, ws, step, chan, self.vals[eng] if chan == eng else -1,
                              self.vals[eng]))
        self.vals[chan] += step
        return (chan, self.vals[chan])

    def barrier(self, exclude=("wsh",)):
        snap = {cn: v for cn, v in self.vals.items() if v > 0 and cn not in exclude}
        for e in self.STEP:
            for pc, pv in snap.items():
                key = (e, pc)
                if pv > self._max_waited.get(key, 0):
                    self._max_waited[key] = pv
                    self.ops[e].append(("__wait__", (pc, pv), {}, [], 0, None, -1, 0))

    def finalize(self, block, nc, es):
        sems = {}
        for cn in self.vals:
            sems[cn] = es.enter_context(nc.semaphore("sem_" + cn))
        secmap = {"v": block.vector, "s": block.scalar, "t": block.tensor,
                  "sd": block.sync, "g": block.gpsimd}

        def runner(eng_name):
            def run(e):
                serial = self.SERIAL[eng_name]
                engsem = sems[eng_name]
                for (fn, a, k, ws, step, chan, selfwait_val, _ev) in self.ops[eng_name]:
                    if fn == "__wait__":
                        pc, pv = a
                        e.wait_ge(sems[pc], pv)
                        continue
                    for (pc, pv) in ws:
                        e.wait_ge(sems[pc], pv)
                    r = getattr(e, fn)(*a, **k)
                    if serial and chan == eng_name and selfwait_val > 0:
                        r._wait_ge(engsem, selfwait_val)
                    if step:
                        r.then_inc(sems[chan], step)
            return run
        for n, sec in secmap.items():
            sec(runner(n))


class Ctx:
    pass


def emit_ln_tile(p, c, src_ap, out_ap, extra_dep=()):
    v, s = p.v, p.s
    m = v.tensor_reduce(c.ln_nm[:], src_ap, AX.X, ALU.add, waits=extra_dep)
    v.tensor_scalar(c.ln_nm[:], c.ln_nm[:], -1.0 / D, None, ALU.mult)
    mnm = v.mark()
    s.activation(c.ln_dump[:], src_ap, AF.Square, accum_out=c.ln_ssq[:], waits=[mnm])
    mssq = s.mark()
    v.tensor_mul(c.ln_m2[:], c.ln_nm[:], c.ln_nm[:], waits=[mssq])
    v.tensor_scalar(c.ln_ssq[:], c.ln_ssq[:], 1.0 / D, None, ALU.mult)
    v.tensor_sub(c.ln_ssq[:], c.ln_ssq[:], c.ln_m2[:])
    mvar = v.mark()
    s.activation(c.ln_ssq[:], c.ln_ssq[:], AF.Ln, bias=c.cpool[:, 0:1], scale=1.0, waits=[mvar])
    s.activation(c.ln_ssq[:], c.ln_ssq[:], AF.Exp, scale=-0.5)
    mrstd = s.mark()
    v.tensor_scalar(out_ap, src_ap, c.ln_nm[:, 0:1], c.ln_ssq[:, 0:1],
                    ALU.add, ALU.mult, waits=[mrstd])
    return v.mark()


def emit_transpose8(p, c, src_tile_bf, dst_fn, dep):
    marks = []
    prev_evac = [None, None]
    for dc in range(8):
        bank = dc % 2
        w = [dep]
        if prev_evac[bank] is not None:
            w.append(prev_evac[bank])
        p.t.transpose(c.pT[bank][:], src_tile_bf[:, dc * 128:(dc + 1) * 128],
                      c.ident_bf[:], waits=w)
        mt = p.t.mark()
        p.v.tensor_copy(dst_fn(dc), c.pT[bank][:], waits=[mt])
        prev_evac[bank] = p.v.mark()
        marks.append(prev_evac[bank])
    return marks[-1]


def emit_ffn_prologue(p, c):
    p.sd.dma_start(out=c.f_rw[:].rearrange("p (dc e) -> p dc e", dc=8),
                   in_=c.d_rw.rearrange("(dc p) e -> p dc e", p=128), chan="fpro")
    p.sd.dma_start(out=c.f_rb_row[:], in_=c.d_rb, chan="fpro")
    mrw = p.cmark("fpro")
    mrb = mrw
    p.t.matmul(c.pR8[:], c.ones_row[:], c.f_rb_row[:], start=True, stop=True,
               waits=[mrb])
    mrbb = p.t.mark()
    p.v.tensor_copy(c.f_rb_rep[:], c.pR8[:], waits=[mrbb])
    mrbrep = p.v.mark()

    tp_done = []
    for tt in range(NTILE):
        mln = emit_ln_tile(p, c, c.x_res[:, tt * 1024:(tt + 1) * 1024], c.f_xn[:],
                           extra_dep=())
        p.v.tensor_scalar(c.f_xn32[:], c.x_res[:, tt * 1024:(tt + 1) * 1024],
                          c.ln_nm[:, 0:1], c.ln_ssq[:, 0:1], ALU.add, ALU.mult)
        mln32 = p.v.mark()
        mtp = emit_transpose8(
            p, c, c.f_xn[:],
            lambda dc, tt=tt: c.f_xnT[:, (dc * NTILE + tt) * 128:(dc * NTILE + tt) * 128 + 128],
            mln)
        tp_done.append(mtp)
        prev = [None, None]
        for dc in range(8):
            bk = dc % 2
            w = [mln32]
            if prev[bk] is not None:
                w.append(prev[bk])
            p.t.transpose(c.pT32[bk][:], c.f_xn32[:, dc * 128:(dc + 1) * 128],
                          c.ident32[:], waits=w)
            mtt = p.t.mark()
            p.v.tensor_copy(c.f_xnT32[:, dc * 128:(dc + 1) * 128], c.pT32[bk][:], waits=[mtt])
            prev[bk] = p.v.mark()
        mtp32 = p.v.mark()
        for dc in range(8):
            w = [mtp32, mrw] if dc == 0 else []
            if dc == 0 and tt == 0:
                w.append(mrbrep)
            p.t.matmul(c.pR8[:], c.f_xnT32[:, dc * 128:(dc + 1) * 128],
                       c.f_rw[:, dc * FFN_E:(dc + 1) * FFN_E],
                       start=(dc == 0), stop=(dc == 7), waits=w)
        mrl = p.t.mark()
        p.v.tensor_add(c.f_rlog[:], c.pR8[:], c.f_rb_rep[:], waits=[mrl])
        p.v.reduce_max(c.f_rmax[:], c.f_rlog[:], AX.X)
        p.v.tensor_scalar(c.f_mask[:], c.f_rlog[:], c.f_rmax[:, 0:1], None, ALU.is_equal)
        p.v.tensor_scalar(c.f_rmax[:], c.f_rmax[:], -1.0, None, ALU.mult)
        mrm = p.v.mark()
        p.s.activation(c.f_rlog[:], c.f_rlog[:], AF.Exp, bias=c.f_rmax[:, 0:1],
                       accum_out=c.f_esum[:], waits=[mrm])
        mre = p.s.mark()
        p.v.reciprocal(c.f_esum[:], c.f_esum[:], waits=[mre])
        p.v.tensor_scalar(c.gates[:, tt * FFN_E:(tt + 1) * FFN_E],
                          c.f_mask[:], c.f_esum[:, 0:1], None, ALU.mult)

    return p.v.mark()


def emit_ffn_experts(p, c, m_gates, grp_tiles=4, img_ready=False):
    NG = NTILE // grp_tiles
    w1_use = [[], []]
    w2_use = [[], []]
    pa_use = [None, None]
    pg_use = [None, None]
    y_use = None
    act_ready = {}
    wb1 = {}
    wb2 = {}

    for grp in range(NG):
        for e in range(FFN_E):
            for j in range(8):
                buf = (e * 8 + j) % 2
                w = list(w1_use[buf])
                if img_ready or (e, j) in wb1:
                    if (e, j) in wb1:
                        w.append(wb1[(e, j)])
                    p.g.dma_start(out=c.f_w1a[buf][:], in_=c.d_w1sh[e, j, 0],
                                  waits=w, chan=f"w1_{buf}")
                    p.g.dma_start(out=c.f_w1g[buf][:], in_=c.d_w1sh[e, j, 1],
                                  chan=f"w1_{buf}")
                    mw1 = p.cmark(f"w1_{buf}")
                    w1_use[buf] = []
                else:
                    p.g.dma_start(out=c.f_w1a[buf][:].rearrange("p (dc n) -> p dc n", dc=8),
                                  in_=c.d_w1[e, :, j * 512:(j + 1) * 512]
                                  .rearrange("(dc p) n -> p dc n", p=128), waits=w,
                                  chan=f"w1_{buf}")
                    p.g.dma_start(out=c.f_w1g[buf][:].rearrange("p (dc n) -> p dc n", dc=8),
                                  in_=c.d_w1[e, :, FFN_H + j * 512:FFN_H + (j + 1) * 512]
                                  .rearrange("(dc p) n -> p dc n", p=128), chan=f"w1_{buf}")
                    mw1 = p.cmark(f"w1_{buf}")
                    p.sd.dma_start(out=c.d_w1sh[e, j, 0], in_=c.f_w1a[buf][:],
                                   waits=[mw1], chan="w1wb")
                    p.sd.dma_start(out=c.d_w1sh[e, j, 1], in_=c.f_w1g[buf][:],
                                   chan="w1wb")
                    wb1[(e, j)] = p.cmark("w1wb")
                    w1_use[buf] = [wb1[(e, j)]]
                for hc in range(4):
                    ch = j * 4 + hc
                    pb = ch % 2
                    w = [mw1]
                    if ch < 2 and grp == 0 and e == 0:
                        w.append(m_gates)
                    if pa_use[pb] is not None:
                        w.append(pa_use[pb])
                    if pg_use[pb] is not None:
                        w.append(pg_use[pb])
                    xs = c.f_xnT
                    base = lambda dc: (dc * NTILE + grp * grp_tiles) * 128
                    for dc in range(8):
                        p.t.matmul(c.pA[pb][:],
                                   c.f_w1a[buf][:, dc * 512 + hc * 128: dc * 512 + hc * 128 + 128],
                                   xs[:, base(dc): base(dc) + grp_tiles * 128],
                                   start=(dc == 0), stop=(dc == 7),
                                   waits=w if dc == 0 else [])
                    for dc in range(8):
                        p.t.matmul(c.pG[pb][:],
                                   c.f_w1g[buf][:, dc * 512 + hc * 128: dc * 512 + hc * 128 + 128],
                                   xs[:, base(dc): base(dc) + grp_tiles * 128],
                                   start=(dc == 0), stop=(dc == 7))
                    mag = p.t.mark()
                    w1_use[buf] = ([wb1[(e, j)]] if (e, j) in wb1 else []) + [mag]
                    p.s.activation(c.f_silu[pb][:], c.pA[pb][:], AF.Silu, waits=[mag])
                    pa_use[pb] = p.s.mark()
                    p.v.tensor_mul(c.f_actT[:, ch * (grp_tiles * 128):(ch + 1) * (grp_tiles * 128)],
                                   c.pG[pb][:], c.f_silu[pb][:], waits=[pa_use[pb]])
                    pg_use[pb] = p.v.mark()
                    act_ready[ch] = pg_use[pb]

            GT = grp_tiles
            for nh in range(2):
                for cb in range(4):
                    buf = (nh * 4 + cb) % 2
                    w = list(w2_use[buf])
                    key = (e, nh, cb)
                    if img_ready or key in wb2:
                        if key in wb2:
                            w.append(wb2[key])
                        p.g.dma_start(out=c.f_w2c[buf][:], in_=c.d_w2sh[e, nh, cb],
                                      waits=w, chan=f"w2_{buf}")
                        mw2 = p.cmark(f"w2_{buf}")
                        wbdep2 = []
                    else:
                        p.g.dma_start(out=c.f_w2c[buf][:].rearrange("p (cc n) -> p cc n", cc=8),
                                      in_=c.d_w2[e, cb * 1024:(cb + 1) * 1024,
                                                 nh * 512:(nh + 1) * 512]
                                      .rearrange("(cc p) n -> p cc n", p=128),
                                      waits=w, chan=f"w2_{buf}")
                        mw2 = p.cmark(f"w2_{buf}")
                        p.sd.dma_start(out=c.d_w2sh[e, nh, cb], in_=c.f_w2c[buf][:],
                                       waits=[mw2], chan="w2wb")
                        wb2[key] = p.cmark("w2wb")
                        wbdep2 = [wb2[key]]
                    for ch8 in range(8):
                        ch = cb * 8 + ch8
                        w = [mw2, act_ready[ch]]
                        if ch == 0 and y_use is not None:
                            w.append(y_use)
                        for ti in range(4):
                            p.t.matmul(c.pY[ti][:],
                                       c.f_actT[:, ch * (GT * 128) + ti * 128: ch * (GT * 128) + ti * 128 + 128],
                                       c.f_w2c[buf][:, ch8 * 512:(ch8 + 1) * 512],
                                       start=(ch == 0), stop=(ch == 31),
                                       waits=w if ti == 0 else [])
                        w2_use[buf] = wbdep2 + [p.t.mark()]
                my = p.t.mark()
                for ti in range(4):
                    tt = grp * grp_tiles + ti
                    p.v.scalar_tensor_tensor(
                        c.x_res[:, tt * 1024 + nh * 512: tt * 1024 + nh * 512 + 512],
                        c.pY[ti][:],
                        c.gates[:, tt * FFN_E + e: tt * FFN_E + e + 1],
                        c.x_res[:, tt * 1024 + nh * 512: tt * 1024 + nh * 512 + 512],
                        ALU.mult, ALU.add, waits=[my])
                y_use = p.v.mark()




def emit_attn_a_proj(p, c):
    p.v.memset(c.a_QTz[0][64:128, :], 0.0)
    p.v.memset(c.a_QTz[1][0:64, :], 0.0)
    xst_use = [None, None]
    for tt in range(NTILE):
        xb = tt % 2
        w = [] if xst_use[xb] is None else [xst_use[xb]]
        p.sd.dma_start(out=c.a_xst[xb][:], in_=c.x_in[tt * 128:(tt + 1) * 128, :],
                       waits=w, chan=f"xst{xb}")
        mx = p.cmark(f"xst{xb}")
        mln = emit_ln_tile(p, c, c.a_xst[xb][:], c.a_xn[:], extra_dep=[mx])
        emit_transpose8(
            p, c, c.a_xn[:],
            lambda dc, tt=tt: c.a_xnT[:, (dc * NTILE + tt) * 128:(dc * NTILE + tt) * 128 + 128],
            mln)
        xst_use[xb] = p.s.mark()
    mtp_all = p.v.mark()

    wb_use = [None, None]
    pq_use = [None, None]
    pkv_use = None
    ktmp_use = None
    ktst_use = None
    for cg in range(6):
        buf = cg % 2
        w = []
        if wb_use[buf] is not None:
            w.append(wb_use[buf])
        p.g.dma_start(out=c.a_wbuf[buf][:].rearrange("p (dc n) -> p dc n", dc=8),
                      in_=c.d_wqkv[:, cg * 512:(cg + 1) * 512]
                      .rearrange("(dc p) n -> p dc n", p=128), waits=w, chan=f"wq{buf}")
        mw = p.cmark(f"wq{buf}")
        if cg < 2:
            QW = min(512, TOK)
            NQG = TOK // QW
            for hh in range(8):
                h = cg * 8 + hh
                pr, odd = h // 2, h % 2
                for tg in range(NQG):
                    pb = (hh * NQG + tg) % 2
                    w = [mw]
                    if cg == 0 and hh == 0 and tg == 0:
                        w.append(mtp_all)
                    if pq_use[pb] is not None:
                        w.append(pq_use[pb])
                    for dc in range(8):
                        p.t.matmul(c.pQ[pb][:],
                                   c.a_wbuf[buf][:, dc * 512 + hh * 64: dc * 512 + hh * 64 + 64],
                                   c.a_xnT[:, dc * NTILE * 128 + tg * QW: dc * NTILE * 128 + tg * QW + QW],
                                   start=(dc == 0), stop=(dc == 7),
                                   waits=w if dc == 0 else [])
                    mq = p.t.mark()
                    wb_use[buf] = mq
                    dst = c.a_QTz[odd][odd * 64:(odd + 1) * 64,
                                       pr * TOK + tg * QW: pr * TOK + tg * QW + QW]
                    wq2 = [mq]
                    p.v.tensor_scalar(dst, c.pQ[pb][:], 0.125, None, ALU.mult, waits=wq2)
                    pq_use[pb] = p.v.mark()
        else:
            for tt in range(NTILE):
                w = [mw]
                if cg == 2 and tt == 0:
                    w.append(mtp_all)
                if pkv_use is not None:
                    w.append(pkv_use)
                for dc in range(8):
                    p.t.matmul(c.pKV[:],
                               c.a_xnT[:, (dc * NTILE + tt) * 128:(dc * NTILE + tt) * 128 + 128],
                               c.a_wbuf[buf][:, dc * 512:(dc + 1) * 512],
                               start=(dc == 0), stop=(dc == 7),
                               waits=w if dc == 0 else [])
                mkv = p.t.mark()
                wb_use[buf] = mkv
                ww = [mkv]
                if ktmp_use is not None:
                    ww.append(ktmp_use)
                p.v.tensor_copy(c.a_ktmp[:], c.pKV[:], waits=ww)
                pkv_use = p.v.mark()
                if cg >= 4:
                    p.sd.dma_start(out=c.cc_v_in[tt * 128:(tt + 1) * 128,
                                                 (cg - 4) * 512:(cg - 3) * 512],
                                   in_=c.a_ktmp[:], waits=[pkv_use], chan="kvst")
                    ktmp_use = p.cmark("kvst")
                else:
                    prevq = [None, None]
                    for qq in range(4):
                        wq = [pkv_use]
                        if prevq[qq % 2] is not None:
                            wq.append(prevq[qq % 2])
                        p.t.transpose(c.pT[qq % 2][:],
                                      c.a_ktmp[:, qq * 128:(qq + 1) * 128],
                                      c.ident_bf[:], waits=wq)
                        mtt2 = p.t.mark()
                        p.v.tensor_copy(c.a_ktT[:, qq * 128:(qq + 1) * 128],
                                        c.pT[qq % 2][:], waits=[mtt2])
                        prevq[qq % 2] = p.v.mark()
                    mktT = p.v.mark()
                    ktmp_use = mktT
                    wst = [mktT]
                    if ktst_use is not None:
                        wst.append(ktst_use)
                    p.sd.dma_start(
                        out=c.cc_kt_in[tt * 1024 + (cg - 2) * 512:
                                       tt * 1024 + (cg - 2) * 512 + 512, :]
                        .rearrange("(pr d) kp -> d pr kp", d=128),
                        in_=c.a_ktT[:].rearrange("p (pr kp) -> p pr kp", pr=4),
                        waits=wst, chan="ktst")
                    ktst_use = p.cmark("ktst")
    c.m_proj_s = p.s.mark()
    return (p.cmark("kvst"), p.cmark("ktst"))


def emit_attn_a(p, c, m_store):
    NR = 8
    m_store_kv, m_store_kt = m_store
    xst_use = [c.m_proj_s, c.m_proj_s]
    p.g.dma_start(out=c.a_wout[:].rearrange("p (dc n) -> p dc n", dc=8),
                  in_=c.d_wout.rearrange("(dc p) n -> p dc n", p=128), chan="wout",
                  waits=[p.t.mark(), p.v.mark(), c.m_proj_s, m_store_kv, m_store_kt])
    m_wout = p.cmark("wout")
    p.g.collective_compute("AllGather", ALU.bypass, RG8,
                           [c.cc_v_in.opt()], [c.cc_v_all.opt()],
                           waits=[m_store_kv], inc=1, chan="cc")
    p.g.collective_compute("AllGather", ALU.bypass, RG8,
                           [c.cc_kt_in.opt()], [c.cc_kt_all.opt()],
                           waits=[m_store_kt], inc=1, chan="cc")
    m_ag = p.cmark("cc")

    for r in range(NR):
        p.v.memset(c.a_vring[r].rearrange("p (k s) -> p k s", s=66)[:, :, 64:65], 1.0,
                   waits=[m_store_kv, m_store_kt] if r == 0 else [])
    m_ones = p.v.mark()

    m_proj_t = p.t.mark()
    ring_use = [m_proj_t] * NR
    bias_use = [[m_store_kv, m_store_kt], [m_store_kv, m_store_kt]]
    pl_use = [None, None]
    psb_use = {}
    psb_ready = {}
    po_flush = {}
    pb2 = [0]
    pout_use = None
    abias_off = 0
    pb = 0

    if ATTN_DBG < 1:
        return
    CH = 4
    acc_first = {}
    accO_use = None
    A_WIN = _pair_windows()
    for tt in range(NTILE):
        bnum, i = tt // NSLOT, tt % NSLOT
        gmax = 8 * i + 7

        def active(jj, pr):
            if not A_SKIP:
                return True
            return jj == 0 or (8 * i - (jj - 1)) <= A_WIN[pr]

        def prmin_of(jj):
            for pr in range(8):
                if active(jj, pr):
                    return pr
            return 8
        bb = tt % 2
        bu = bias_use[bb]
        w = [] if bu is None else (list(bu) if isinstance(bu, list) else [bu])
        p.sd.dma_start(out=c.a_bias[bb][:, 0:gmax * 16],
                       in_=c.d_abias[:, abias_off:abias_off + gmax * 16], waits=w,
                       chan=f"ab{bb}")
        m_bias = p.cmark(f"ab{bb}")
        abias_off += gmax * 16

        njj = min(gmax + 1, A_MAXJJ)
        nch = (njj + CH - 1) // CH
        for chi in range(nch):
            jlo, jhi = chi * CH, min(njj, chi * CH + CH)
            for jj in range(jlo, jhi):
                r = jj % NR
                sl = jj % (2 * CH)
                if jj == 0:
                    vbuf, vrow = c.cc_v_in, tt * 128
                    ktbuf, ktrow = c.cc_kt_in, tt * 1024
                else:
                    j = jj - 1
                    vbuf = c.cc_v_all
                    vrow = (j % 8) * TOK + (bnum * NSLOT + j // 8) * 128
                    ktbuf = c.cc_kt_all
                    ktrow = (j % 8) * TOK * 8 + (bnum * NSLOT + j // 8) * 1024
                w = [] if ring_use[r] is None else [ring_use[r]]
                if jj == 1:
                    w.append(m_ag)
                if jj == 0:
                    w.append(m_store_kv)
                    w.append(m_store_kt)
                pm = prmin_of(jj)
                pml = pm if A_SLIM else 0
                p.sd.dma_start(
                    out=c.a_kring[r][:, pml * 128:].rearrange("p (pr kp) -> p pr kp", pr=8 - pml),
                    in_=ktbuf[ktrow + pml * 128:ktrow + 1024, :]
                    .rearrange("(pr d) kp -> d pr kp", d=128),
                    waits=w, chan=f"kv{r}")
                for hh in range(2):
                    p.g.dma_start(
                        out=c.a_vring[r].rearrange("p (k m) -> p k m", m=132)[:, pml:8, hh * 66:hh * 66 + 64],
                        in_=vbuf[vrow:vrow + 128, :]
                        .rearrange("p (k n) -> p k n", n=128)[:, pml:8, hh * 64:hh * 64 + 64],
                        waits=w if hh == 0 else [], chan=f"kv{r}")
                m_ring = p.cmark(f"kv{r}")
                m_ring_w = [m_ring, m_ones] if jj == 0 else [m_ring]

                if ATTN_DBG < 2.2:
                    ring_use[r] = m_ring
                    continue
                for pri, pr in enumerate(range(pm, 8)):
                    w1 = list(m_ring_w) if pri == 0 else []
                    if pl_use[pb] is not None:
                        w1.append(pl_use[pb])
                    for odd in range(2):
                        p.t.matmul(c.pL[pb][:, odd * 128:(odd + 1) * 128],
                                   c.a_kring[r][:, pr * 128:(pr + 1) * 128],
                                   c.a_QTz[odd][:, pr * TOK + tt * 128: pr * TOK + tt * 128 + 128],
                                   start=True, stop=True,
                                   waits=w1 if odd == 0 else [])
                    m1 = p.t.mark()
                    ring_use[r] = m1
                    if ATTN_DBG < 2.3:
                        pb ^= 1
                        continue
                    ws = [m1]
                    if jj > 0:
                        ws.append(m_bias)
                    mpsb = psb_use.get((sl, pr))
                    if mpsb is not None:
                        ws.append(mpsb)
                    dst = c.a_P[sl]
                    for odd in range(2):
                        h = pr * 2 + odd
                        bias_ap = (c.diag_bias[:, h:h + 1] if jj == 0
                                   else c.a_bias[bb][:, (jj - 1) * 16 + h:(jj - 1) * 16 + h + 1])
                        p.s.activation(dst[:, pr * 256 + odd * 128: pr * 256 + odd * 128 + 128],
                                       c.pL[pb][:, odd * 128:(odd + 1) * 128],
                                       AF.Exp, bias=bias_ap,
                                       waits=ws if odd == 0 else [])
                    mp = p.s.mark()
                    pl_use[pb] = mp
                    if jj == 0 and ATTN_DBG >= 2.4:
                        p.v.tensor_mul(dst[:, pr * 256:(pr + 1) * 256],
                                       dst[:, pr * 256:(pr + 1) * 256],
                                       c.mask256[:], waits=[mp])
                    pb ^= 1
                psb_ready[jj % (2 * CH)] = (p.s.mark(), p.v.mark())

            if ATTN_DBG < 2.5:
                continue
            for pr in range(8):
                jlist = [jj for jj in range(jlo, jhi) if active(jj, pr)]
                if not jlist:
                    continue
                for odd in range(2):
                    pb2i = pb2[0]
                    first = True
                    for jj in jlist:
                        r = jj % NR
                        sl = jj % (2 * CH)
                        w2 = []
                        if first:
                            w2 += list(psb_ready[(jhi - 1) % (2 * CH)])
                            mflush = po_flush.get(pb2i)
                            if mflush is not None:
                                w2.append(mflush)
                        p.t.matmul(c.psPO[pb2i][:],
                                   c.a_vring[r][:, pr * 132 + odd * 66: pr * 132 + odd * 66 + 65],
                                   c.a_P[sl][:, pr * 256 + odd * 128: pr * 256 + odd * 128 + 128],
                                   start=first, stop=(jj == jlist[-1]),
                                   waits=w2)
                        first = False
                        psb_use[(sl, pr)] = p.t.mark()
                    m2 = p.t.mark()
                    for jj in jlist:
                        ring_use[jj % NR] = m2
                    aslice = c.a_accO[0:65, (pr * 2 + odd) * 128:(pr * 2 + odd) * 128 + 128]
                    wv = [m2]
                    if chi == 0 and accO_use is not None:
                        wv.append(accO_use)
                    if chi == 0:
                        p.v.tensor_copy(aslice, c.psPO[pb2i][:], waits=wv)
                    else:
                        p.v.tensor_add(aslice, aslice, c.psPO[pb2i][:], waits=wv)
                    po_flush[pb2i] = p.v.mark()
                    pb2[0] ^= 1
        m_attn = p.v.mark()
        if ATTN_DBG < 2.95:
            continue

        for pr in range(8):
            for odd in range(2):
                cslice = slice((pr * 2 + odd) * 128, (pr * 2 + odd) * 128 + 128)
                p.v.reciprocal(c.a_recip[:], c.a_accO[64:65, cslice], waits=[m_attn])
                mrc = p.v.mark()
                w = [mrc]
                if pl_use[0] is not None:
                    w.append(pl_use[0])
                p.t.matmul(c.pL[0][0:64, 0:128], c.ones_row[:, 0:64], c.a_recip[:],
                           start=True, stop=True, waits=w)
                mrb = p.t.mark()
                p.s.copy(c.a_rb[:], c.pL[0][0:64, 0:128], waits=[mrb])
                mrbe = p.s.mark()
                pl_use[0] = mrbe
                p.v.tensor_mul(c.a_aT[odd * 64:(odd + 1) * 64, pr * 128:(pr + 1) * 128],
                               c.a_accO[0:64, cslice], c.a_rb[:], waits=[mrbe])
        accO_use = p.v.mark()
        m_norm = p.v.mark()
        bias_use[bb] = p.s.mark()

        if ATTN_DBG < 2.98:
            continue
        xb = tt % 2
        w = [] if xst_use[xb] is None else [xst_use[xb]]
        p.sd.dma_start(out=c.a_xst[xb][:], in_=c.x_in[tt * 128:(tt + 1) * 128, :],
                       waits=w, chan=f"xst{xb}")
        mx = p.cmark(f"xst{xb}")
        for nh in range(2):
            first = True
            for pr in range(8):
                w = [m_norm, m_wout] if first else []
                if first and pout_use is not None:
                    w.append(pout_use)
                p.t.matmul(c.pOut[nh][:],
                           c.a_aT[:, pr * 128:(pr + 1) * 128],
                           c.a_wout[:, pr * 1024 + nh * 512: pr * 1024 + nh * 512 + 512],
                           start=first, stop=(pr == 7), waits=w)
                first = False
            mo = p.t.mark()
            dst = c.a_xst[xb][:, nh * 512:(nh + 1) * 512]
            p.v.tensor_add(dst, dst, c.pOut[nh][:], waits=[mo, mx])
            pout_use = p.v.mark()
        p.sd.dma_start(out=c.x1_dram[tt * 128:(tt + 1) * 128, :], in_=c.a_xst[xb][:],
                       waits=[pout_use], chan="x1st")
        xst_use[xb] = p.cmark("x1st")
        if tt == DBG_TT:
            p.sd.dma_start(out=c.dbg_aT, in_=c.a_aT[:], waits=[m_norm], chan="dbg")
            p.sd.dma_start(out=c.dbg_acc, in_=c.a_accO[:], waits=[m_attn], chan="dbg")
            p.sd.dma_start(out=c.dbg_kv[:, 0:1024], in_=c.cc_v_in[TOK - 256:TOK, :], waits=[m_store_kv], chan="dbg")
            p.sd.dma_start(out=c.dbg_vr, in_=c.a_vring[0][:], waits=[m_attn], chan="dbg")




def emit_attn_b(p, c):
    NCH = T // 128
    p.v.memset(c.b_S[0][:], 0.0)
    p.v.memset(c.b_S[1][:], 0.0)
    xnst = [None, None]
    for tt in range(NTILE):
        xb = tt % 2
        dep = [] if xnst[xb] is None else [xnst[xb]]
        mln = emit_ln_tile(p, c, c.x_res[:, tt * 1024:(tt + 1) * 1024], c.b_xn[xb][:],
                           extra_dep=dep)
        p.sd.dma_start(out=c.cc_xn_in[tt * 128:(tt + 1) * 128, :], in_=c.b_xn[xb][:],
                       waits=[mln], chan="bxnst")
        xnst[xb] = p.cmark("bxnst")
    m_xnst = p.cmark("bxnst")
    p.g.collective_compute("AllGather", ALU.bypass, RG8,
                           [c.cc_xn_in.opt()], [c.cc_xn_all.opt()],
                           waits=[m_xnst], inc=1, chan="cc")
    m_ag = p.cmark("cc")

    p.g.dma_start(out=c.b_wqkvr[:].rearrange("p (dc n) -> p dc n", dc=8),
                  in_=c.d_wbqkvr.rearrange("(dc p) n -> p dc n", p=128), chan="bw")
    p.g.dma_start(out=c.b_wa[:].rearrange("p (dc n) -> p dc n", dc=8),
                  in_=c.d_wba.rearrange("(dc p) n -> p dc n", p=128), chan="bw",
                  allow_slow_non_contiguous=True)
    p.g.dma_start(out=c.b_wo[:].rearrange("p (dc n) -> p dc n", dc=8),
                  in_=c.d_wbo.rearrange("(dc p) n -> p dc n", p=128), chan="bw")
    m_bw = p.cmark("bw")
    p.sd.dma_start(out=c.b_rows[0:1, 0:128], in_=c.d_qnw, chan="brow")
    p.sd.dma_start(out=c.b_rows[0:1, 128:256], in_=c.d_knw, chan="brow")
    p.sd.dma_start(out=c.b_rows[0:1, 256:384], in_=c.d_gnw, chan="brow")
    p.sd.dma_start(out=c.b_rows[0:1, 384:385], in_=c.d_alog, chan="brow")
    p.sd.dma_start(out=c.b_rows[0:1, 385:386], in_=c.d_dtb, chan="brow")
    p.sd.dma_start(out=c.b_ucum[:], in_=c.ucum_d.ap(), chan="brow")
    m_rows = p.cmark("brow")
    p.s.activation(c.b_rows[0:1, 384:385], c.b_rows[0:1, 384:385], AF.Exp,
                   waits=[m_rows])
    m_ea = p.s.mark()
    p.v.tensor_scalar(c.b_rows[0:1, 384:385], c.b_rows[0:1, 384:385], -1.0, None,
                      ALU.mult, waits=[m_ea])
    m_ea2 = p.v.mark()
    p.t.matmul(c.pQKVR[:], c.ones_row[:], c.b_rows[0:1, :], start=True, stop=True,
               waits=[m_ea2])
    m_rep = p.t.mark()
    p.v.tensor_copy(c.b_rep[:], c.pQKVR[:], waits=[m_rep])
    m_repv = p.v.mark()
    qnw_rep = c.b_rep[:, 0:128]
    knw_rep = c.b_rep[:, 128:256]
    gnw_rep = c.b_rep[:, 256:384]
    neg_ea = c.b_rep[:, 384:385]
    dtb_rep = c.b_rep[:, 385:386]

    RB, W = 6, 2
    NST = NCH * B
    st = {"pO": None, "pS": None, "pF": None, "pSm": None, "pQK": m_repv,
          "pA2": None, "pT0": None, "pT1": None}
    xr_use = [None] * RB
    st_done = {}
    scan_rd = {}
    pmk = {}

    def prep(i):
        n, b = i // B, i % B
        ri = i % RB
        war = scan_rd.get(i - RB, [])
        rb = (n % 8) * TOK + (b * NSLOT + n // 8) * 128
        w = [m_ag] + list(war)
        if xr_use[ri] is not None:
            w.append(xr_use[ri])
        for dc in range(8):
            p.sd.dma_start_transpose(
                c.b_xT[ri][:, dc * 128:(dc + 1) * 128],
                c.cc_xn_all[rb:rb + 128, dc * 128:(dc + 1) * 128],
                waits=w if dc == 0 else [], chan=f"bx{ri}")
        m_xT = p.cmark(f"bx{ri}")

        w = [m_xT, m_bw]
        if st["pQK"] is not None:
            w.append(st["pQK"])
        for dc in range(8):
            p.t.matmul(c.pQKVR[:], c.b_xT[ri][:, dc * 128:(dc + 1) * 128],
                       c.b_wqkvr[:, dc * 512:(dc + 1) * 512],
                       start=(dc == 0), stop=(dc == 7), waits=w if dc == 0 else [])
        m_proj = p.t.mark()
        w = [m_proj]
        if st["pA2"] is not None:
            w.append(st["pA2"])
        for dc in range(8):
            p.t.matmul(c.pA2[:], c.b_xT[ri][:, dc * 128:(dc + 1) * 128],
                       c.b_wa[:, dc:dc + 1],
                       start=(dc == 0), stop=(dc == 7), waits=w if dc == 0 else [])
        m_aproj = p.t.mark()
        xr_use[ri] = m_aproj

        p.s.activation(c.b_sqkv[ri][:], c.pQKVR[:, 0:384], AF.Silu,
                       waits=[m_proj, m_repv] + list(war))
        p.s.activation(c.b_scr[ri][:], c.pQKVR[:, 384:512], AF.Sigmoid)
        st["pQK"] = p.s.mark()
        pQK_use = st["pQK"]
        p.v.tensor_add(c.b_z[ri][:], c.pA2[:], dtb_rep, waits=[m_aproj])
        st["pA2"] = p.v.mark()
        p.s.activation(c.b_zab[ri][:], c.b_z[ri][:], AF.Abs, waits=[st["pA2"]])
        p.s.activation(c.b_zab[ri][:], c.b_zab[ri][:], AF.Exp, scale=-1.0)
        p.s.activation(c.b_zab[ri][:], c.b_zab[ri][:], AF.Ln, bias=c.cpool[:, 1:2])
        m_sp1 = p.s.mark()
        p.v.tensor_scalar(c.b_z[ri][:], c.b_z[ri][:], 0.0, None, ALU.max, waits=[m_sp1])
        p.v.tensor_add(c.b_z[ri][:], c.b_z[ri][:], c.b_zab[ri][:])
        p.v.tensor_mul(c.b_z[ri][:], c.b_z[ri][:], neg_ea)

        for qk in range(2):
            sl_ = slice(qk * 128, qk * 128 + 128)
            p.v.tensor_mul(c.b_dmp[ri][:], c.b_sqkv[ri][:, sl_], c.b_sqkv[ri][:, sl_],
                           waits=[pQK_use])
            p.v.reduce_sum(c.b_ss[ri][:], c.b_dmp[ri][:], AX.X)
            mss = p.v.mark()
            p.s.activation(c.b_ss[ri][:], c.b_ss[ri][:], AF.Ln, bias=c.cpool[:, 2:3],
                           scale=1.0 / 128, waits=[mss])
            p.s.activation(c.b_ss[ri][:], c.b_ss[ri][:], AF.Exp, scale=-0.5)
            mrs = p.s.mark()
            p.v.scalar_tensor_tensor(
                c.b_qn[ri][:, sl_], c.b_sqkv[ri][:, sl_], c.b_ss[ri][:, 0:1],
                qnw_rep if qk == 0 else knw_rep, ALU.mult, ALU.mult, waits=[mrs])
        m_qn = p.v.mark()

        p.v.tensor_copy(c.b_buf[ri][:], c.b_scr[ri][:], waits=[pQK_use])
        for r4 in range(4):
            p.v.max(c.b_v8[ri][:], c.b_buf[ri][:])
            p.v.match_replace(c.b_mh[ri][:], c.b_v8[ri][:], c.b_buf[ri][:], -1.0)
            p.v.tensor_copy(c.b_buf[ri][:], c.b_mh[ri][:])
        p.v.tensor_scalar(c.b_mh[ri][:], c.b_buf[ri][:], -1.0, None, ALU.is_equal)
        p.v.tensor_mul(c.b_mh[ri][:], c.b_scr[ri][:], c.b_mh[ri][:])
        p.v.reduce_sum(c.b_ss[ri][:], c.b_mh[ri][:], AX.X)
        p.v.tensor_scalar(c.b_ss[ri][:], c.b_ss[ri][:], 1e-9, None, ALU.add)
        p.v.reciprocal(c.b_ss[ri][:], c.b_ss[ri][:])
        p.v.tensor_scalar(c.b_mh[ri][:], c.b_mh[ri][:], c.b_ss[ri][:, 0:1], None, ALU.mult)
        p.v.tensor_scalar(c.b_f[ri][:], c.b_mh[ri][:], c.b_z[ri][:, 0:1], None, ALU.mult)
        m_f = p.v.mark()

        w = [m_f, m_rows]
        if st["pF"] is not None:
            w.append(st["pF"])
        p.t.matmul(c.pF[:], c.b_ucum[:], c.b_f[ri][:], start=True, stop=True, waits=w)
        m_F = p.t.mark()
        p.s.activation(c.b_E[ri][:], c.pF[:], AF.Exp, waits=[m_F])
        p.s.activation(c.b_Ei[ri][:], c.pF[:], AF.Exp, scale=-1.0)
        st["pF"] = p.s.mark()
        w = [m_f]
        if st["pSm"] is not None:
            w.append(st["pSm"])
        p.t.matmul(c.pSmall[0:1, 0:128], c.ones_col[:], c.b_f[ri][:],
                   start=True, stop=True, waits=w)
        m_fc = p.t.mark()
        p.v.tensor_copy(c.b_fc[ri][:], c.pSmall[0:1, 0:128], waits=[m_fc])
        m_fcv = p.v.mark()
        p.t.transpose(c.pSmall[:, 0:1], c.b_fc[ri][:], c.ident32[0:1, 0:1],
                      waits=[m_fcv])
        m_fct = p.t.mark()
        p.s.activation(c.b_efc[ri][:], c.pSmall[:, 0:1], AF.Exp,
                       waits=[m_fct] + list(war))
        st["pSm"] = p.s.mark()
        m_efc = st["pSm"]

        p.v.tensor_mul(c.b_qt[ri][:], c.b_qn[ri][:, 0:128], c.b_E[ri][:],
                       waits=[st["pF"], m_qn])
        p.s.activation(c.b_ef[ri][:], c.b_f[ri][:], AF.Exp, waits=[m_f])
        m_ef = p.s.mark()
        p.v.tensor_scalar(c.b_ef[ri][:], c.b_ef[ri][:], -1.0, c.cpool[0:128, 1:2],
                          ALU.mult, ALU.add, waits=[m_ef])
        p.v.tensor_mul(c.b_kt[ri][:], c.b_qn[ri][:, 128:256], c.b_ef[ri][:])
        p.v.tensor_mul(c.b_kt[ri][:], c.b_kt[ri][:], c.b_Ei[ri][:])
        m_ktil = p.v.mark()

        w = [m_ktil]
        if st["pT0"] is not None:
            w.append(st["pT0"])
        p.t.transpose(c.pT32b[0][:], c.b_qt[ri][:], c.ident32[:], waits=w)
        mtq = p.t.mark()
        p.v.tensor_copy(c.b_qtT[ri][:], c.pT32b[0][:], waits=[mtq] + list(war))
        st["pT0"] = p.v.mark()
        m_qtT = st["pT0"]
        w = [m_ktil]
        if st["pT1"] is not None:
            w.append(st["pT1"])
        p.t.transpose(c.pT32b[1][:], c.b_kt[ri][:], c.ident32[:], waits=w)
        mtk = p.t.mark()
        p.v.tensor_copy(c.b_ktT[ri][:], c.pT32b[1][:], waits=[mtk])
        st["pT1"] = p.v.mark()

        w = [st["pT0"], st["pT1"], st["pF"]]
        p.t.matmul(c.pF[:], c.b_ktT[ri][:], c.b_qtT[ri][:], start=True, stop=True,
                   waits=w)
        m_at = p.t.mark()
        p.v.tensor_mul(c.b_am[ri][:], c.pF[:], c.b_ucum[:], waits=[m_at] + list(war))
        st["pF"] = p.v.mark()
        pmk[i] = dict(qtT=m_qtT, am=st["pF"], ktil=m_ktil, efc=m_efc, sqkv=pQK_use)

    def scan(i):
        n, b = i // B, i % B
        ri = i % RB
        mk = pmk.pop(i)
        w = [mk["qtT"], mk["am"], mk["sqkv"]]
        if b in st_done:
            w.append(st_done[b])
        if st["pO"] is not None:
            w.append(st["pO"])
        p.t.matmul(c.pO[:], c.b_qtT[ri][:], c.b_S[b][:], start=True, stop=False,
                   waits=w)
        p.t.matmul(c.pO[:], c.b_am[ri][:], c.b_sqkv[ri][:, 256:384],
                   start=False, stop=True)
        m_O = p.t.mark()

        w = [mk["ktil"]]
        if st["pS"] is not None:
            w.append(st["pS"])
        p.t.matmul(c.pS[:], c.b_kt[ri][:], c.b_sqkv[ri][:, 256:384],
                   start=True, stop=True, waits=w)
        m_S = p.t.mark()
        p.v.tensor_add(c.b_S[b][:], c.b_S[b][:], c.pS[:], waits=[m_S, mk["efc"]])
        p.v.tensor_scalar(c.b_S[b][:], c.b_S[b][:], c.b_efc[ri][:, 0:1], None, ALU.mult)
        st_done[b] = p.v.mark()
        st["pS"] = st_done[b]

        p.v.tensor_copy(c.b_o32[:], c.pO[:], waits=[m_O])
        p.v.tensor_mul(c.b_dmp[ri][:], c.b_o32[:], c.b_o32[:])
        p.v.reduce_sum(c.b_oss[:, (b * NCH + n):(b * NCH + n) + 1], c.b_dmp[ri][:], AX.X)
        p.v.tensor_copy(c.b_obf[:, (b * NCH + n) * 128:(b * NCH + n) * 128 + 128],
                        c.b_o32[:])
        st["pO"] = p.v.mark()
        scan_rd[i] = [m_S, st_done[b]]

    for step in range(NST + W):
        if step < NST:
            prep(step)
        if step >= W:
            scan(step - W)

    m_chunks = p.v.mark()
    p.sd.dma_start(out=c.cc_oss_in, in_=c.b_oss[:], waits=[m_chunks], chan="bosst")
    m_oss = p.cmark("bosst")
    p.g.collective_compute("AllReduce", ALU.add, RG8,
                           [c.cc_oss_in.opt()], [c.cc_oss_out.opt()],
                           waits=[m_oss], inc=1, chan="cc")
    m_ar = p.cmark("cc")
    p.sd.dma_start(out=c.b_oss[:], in_=c.cc_oss_out, waits=[m_ar], chan="bossl")
    m_ossl = p.cmark("bossl")
    p.s.activation(c.b_oss[:], c.b_oss[:], AF.Ln, bias=c.cpool[:, 2:3],
                   scale=1.0 / 1024, waits=[m_ossl])
    p.s.activation(c.b_oss[:], c.b_oss[:], AF.Exp, scale=-0.5)
    m_rstd = p.s.mark()
    return m_rstd, gnw_rep, m_bw


def emit_attn_b_out(p, c, m_rstd, gnw_rep, m_bw):
    NCH = T // 128
    ot_use = None
    pOut_use = None
    for n in range(NCH):
        for b in range(B):
            col = b * NCH + n
            rb = (n % 8) * TOK + (b * NSLOT + n // 8) * 128
            p.v.scalar_tensor_tensor(
                c.b_on[:], c.b_obf[:, col * 128: col * 128 + 128],
                c.b_oss[:, col:col + 1], gnw_rep, ALU.mult, ALU.mult,
                waits=[m_rstd])
            m_on = p.v.mark()
            w = [m_on]
            if ot_use is not None:
                w.append(ot_use)
            p.t.transpose(c.pTbf[:], c.b_on[:], c.ident_bf[:], waits=w)
            m_ot = p.t.mark()
            p.v.tensor_copy(c.b_onT[:], c.pTbf[:], waits=[m_ot])
            m_otv = p.v.mark()
            ot_use = m_otv
            w = [m_otv, m_bw]
            if pOut_use is not None:
                w.append(pOut_use)
            for nh in range(2):
                p.t.matmul(c.pWo[nh][:], c.b_onT[:],
                           c.b_wo[:, nh * 512:(nh + 1) * 512],
                           start=True, stop=True, waits=w if nh == 0 else [])
            m_wo = p.t.mark()
            wv = [m_wo]
            if n + b > 0:
                wv.append(p.cmark("brsst"))
            p.v.tensor_copy(c.b_part[:, 0:512], c.pWo[0][:], waits=wv)
            p.v.tensor_copy(c.b_part[:, 512:1024], c.pWo[1][:])
            pOut_use = p.v.mark()
            p.sd.dma_start(out=c.cc_rs_in[rb:rb + 128, :], in_=c.b_part[:],
                           waits=[pOut_use], chan="brsst")
    m_rsst = p.cmark("brsst")
    p.g.collective_compute("ReduceScatter", ALU.add, RG8,
                           [c.cc_rs_in.opt()], [c.cc_rs_out.opt()],
                           waits=[m_rsst], inc=1, chan="cc")
    m_rs = p.cmark("cc")
    xadd_use = None
    for tt in range(NTILE):
        w = [m_rs]
        if xadd_use is not None:
            w.append(xadd_use)
        p.sd.dma_start(out=c.b_stage[:], in_=c.cc_rs_out[tt * 128:(tt + 1) * 128, :],
                       waits=w, chan="brsl")
        m_st = p.cmark("brsl")
        dst = c.x_res[:, tt * 1024:(tt + 1) * 1024]
        p.v.tensor_add(dst, dst, c.b_stage[:], waits=[m_st])
        xadd_use = p.v.mark()


def build_program(phases=ALL_PHASES):
    nc = bass.Bass("TRN2", target_bir_lowering=False)
    es = ExitStack()
    c = Ctx()
    c.nc = nc

    def sbt(name, shape, dtype=F32):
        return es.enter_context(nc.sbuf_tensor(name, shape, dtype))

    def pst(name, shape, dtype=F32):
        return es.enter_context(nc.psum_tensor(name, shape, dtype))

    x_in = nc.dram_tensor("x_c", [TOK, D], F32, kind="ExternalInput").ap()
    y_out = nc.dram_tensor("y", [TOK, D], F32, kind="ExternalOutput").ap()
    c.dbg_aT = nc.dram_tensor("dbg_aT", [128, 1024], BF16, kind="ExternalOutput").ap()
    c.dbg_acc = nc.dram_tensor("dbg_acc", [65, 2048], F32, kind="ExternalOutput").ap()
    c.dbg_kv = nc.dram_tensor("dbg_kv", [256, 2048], BF16, kind="ExternalOutput").ap()
    c.dbg_vr = nc.dram_tensor("dbg_vr", [128, 1056], BF16, kind="ExternalOutput").ap()
    need_ffn = ("ffn1" in phases) or ("ffn2" in phases)
    if "attn_b" in phases:
        c.d_wbqkvr = nc.dram_tensor("wbqkvr", [D, 512], F32, kind="ExternalInput").ap()
        c.d_wba = nc.dram_tensor("wba", [D, 1], F32, kind="ExternalInput").ap()
        c.d_wbo = nc.dram_tensor("wbo", [128, D], F32, kind="ExternalInput").ap()
        c.d_qnw = nc.dram_tensor("qnw", [1, 128], F32, kind="ExternalInput").ap()
        c.d_knw = nc.dram_tensor("knw", [1, 128], F32, kind="ExternalInput").ap()
        c.d_gnw = nc.dram_tensor("gnw", [1, 128], F32, kind="ExternalInput").ap()
        c.d_alog = nc.dram_tensor("alog", [1, 1], F32, kind="ExternalInput").ap()
        c.d_dtb = nc.dram_tensor("dtb", [1, 1], F32, kind="ExternalInput").ap()
        c.cc_xn_in = nc.dram_tensor("cc_xn_in", [TOK, D], BF16, kind="Internal").ap()
        c.cc_xn_all = nc.dram_tensor("cc_xn_all", [NCORE * TOK, D], BF16, kind="Internal").ap()
        c.cc_oss_in = nc.dram_tensor("cc_oss_in", [128, 128], F32, kind="Internal").ap()
        c.cc_oss_out = nc.dram_tensor("cc_oss_out", [128, 128], F32, kind="Internal").ap()
        c.cc_rs_in = nc.dram_tensor("cc_rs_in", [NCORE * TOK, D], BF16, kind="Internal").ap()
        c.cc_rs_out = nc.dram_tensor("cc_rs_out", [TOK, D], BF16, kind="Internal").ap()
        c.ucum_d = nc.inline_tensor(np.triu(np.ones((128, 128), np.float32)), "ucum32")
    if "attn_a" in phases:
        c.d_wqkv = nc.dram_tensor("wqkv", [D, 3 * D], F32, kind="ExternalInput").ap()
        c.d_wout = nc.dram_tensor("wout", [D, D], F32, kind="ExternalInput").ap()
        c.d_abias = nc.dram_tensor("abias", [128, ABW()], F32, kind="ExternalInput").ap()
        c.cc_v_in = nc.dram_tensor("cc_v_in", [TOK, 1024], BF16, kind="Internal").ap()
        c.cc_v_all = nc.dram_tensor("cc_v_all", [NCORE * TOK, 1024], BF16, kind="Internal").ap()
        c.cc_kt_in = nc.dram_tensor("cc_kt_in", [TOK * 8, 128], BF16, kind="Internal").ap()
        c.cc_kt_all = nc.dram_tensor("cc_kt_all", [NCORE * TOK * 8, 128], BF16, kind="Internal").ap()
        c.x1_dram = nc.dram_tensor("x1_dram", [TOK, D], F32, kind="Internal").ap()
    if need_ffn:
        c.d_rw = nc.dram_tensor("rw", [D, FFN_E], F32, kind="ExternalInput").ap()
        c.d_rb = nc.dram_tensor("rb", [1, FFN_E], F32, kind="ExternalInput").ap()
        c.d_w1 = nc.dram_tensor("w1", [FFN_E, D, 2 * FFN_H], F32, kind="ExternalInput").ap()
        c.d_w2 = nc.dram_tensor("w2", [FFN_E, FFN_H, D], F32, kind="ExternalInput").ap()
        c.d_w1sh = nc.dram_tensor("w1sh", [FFN_E, 8, 2, 128, 4096], BF16, kind="Internal").ap()
        c.d_w2sh = nc.dram_tensor("w2sh", [FFN_E, 2, 4, 128, 4096], BF16, kind="Internal").ap()

    id_bf_d = nc.inline_tensor(np.eye(128, dtype=np.float32).astype(ml_dtypes.bfloat16), "id_bf")
    id_32_d = nc.inline_tensor(np.eye(128, dtype=np.float32), "id_32")
    if "attn_a" in phases:
        _pk = np.arange(128, dtype=np.float32)
        _sl = _slopes(HEADS_A)
        diag_b = (_sl[None, :] * _pk[:, None] - SHIFT_A).astype(np.float32)
        diag_b_d = nc.inline_tensor(diag_b, "diag_b")
        um = (np.tril(np.ones((128, 128), np.float32)).T)
        mask256_np = np.concatenate([um, um], axis=1).astype(ml_dtypes.bfloat16)
        mask256_d = nc.inline_tensor(mask256_np, "mask256")
    ones_row_d = nc.inline_tensor(np.ones((1, 128), np.float32), "ones_row_c")
    ones_col_d = nc.inline_tensor(np.ones((128, 1), np.float32), "ones_col_c")

    c.x_res = None
    c.ident_bf = sbt("ident_bf", [128, 128], BF16)
    c.ident32 = sbt("ident32", [128, 128])
    c.ones_row = sbt("ones_row", [1, 128], F32)
    c.ones_col = sbt("ones_col", [128, 1], F32)
    c.cpool = sbt("cpool", [128, 4], F32)
    c.ln_nm = sbt("ln_nm", [128, 1])
    c.ln_ssq = sbt("ln_ssq", [128, 1])
    c.ln_m2 = sbt("ln_m2", [128, 1])
    c.ln_dump = sbt("ln_dump", [128, 1024])
    c.gates = sbt("gates", [128, NTILE * FFN_E])

    p = Prog()
    c.x_in = x_in

    def alloc_x_res(src_dram):
        c.x_res = es.enter_context(nc.sbuf_tensor("x_res", [128, NTILE * 1024], F32))
        for tt in range(NTILE):
            p.sd.dma_start(out=c.x_res[:, tt * 1024:(tt + 1) * 1024],
                           in_=src_dram[tt * 128:(tt + 1) * 128, :], chan="xload")
        p.barrier()

    p.sd.dma_start(out=c.ident_bf[:], in_=id_bf_d.ap())
    p.sd.dma_start(out=c.ident32[:], in_=id_32_d.ap())
    p.sd.dma_start(out=c.ones_col[:], in_=ones_col_d.ap())
    p.sd.dma_start(out=c.ones_row[:], in_=ones_row_d.ap())
    p.v.memset(c.cpool[:, 0:1], EPS_LN)
    p.v.memset(c.cpool[:, 1:2], 1.0)
    p.v.memset(c.cpool[:, 2:3], EPS_B)
    p.v.memset(c.cpool[:, 3:4], 0.0)
    p.barrier()


    phase_scopes = []
    for ph in phases:
        if c.x_res is None and ph != "attn_a":
            alloc_x_res(x_in)
        if ph == "attn_a":
            es2 = ExitStack()
            phase_scopes.append(es2)

            def sA(name, shape, dtype=F32, _es=es2):
                return _es.enter_context(nc.sbuf_tensor(name, shape, dtype))

            c.a_QTz = [sA(f"a_QTz{i}", [128, 8 * TOK], BF16) for i in range(2)]
            c.a_xst = [sA(f"a_xst{i}", [128, 1024]) for i in range(2)]
            c.diag_bias = sA("diag_bias", [128, HEADS_A])
            c.mask256 = sA("mask256s", [128, 256], BF16)
            p.sd.dma_start(out=c.diag_bias[:], in_=diag_b_d.ap())
            p.sd.dma_start(out=c.mask256[:], in_=mask256_d.ap())
            p.barrier()

            es3 = ExitStack()
            c.a_xn = es3.enter_context(nc.sbuf_tensor("a_xn", [128, 1024], BF16))
            c.a_xnT = es3.enter_context(nc.sbuf_tensor("a_xnT", [128, NTILE * 8 * 128], BF16))
            c.a_wbuf = [es3.enter_context(nc.sbuf_tensor(f"a_wbuf{i}", [128, 8 * 512], BF16)) for i in range(2)]
            c.a_ktmp = es3.enter_context(nc.sbuf_tensor("a_ktmp", [128, 512], BF16))
            c.a_ktT = es3.enter_context(nc.sbuf_tensor("a_ktT", [128, 512], BF16))
            c.pT = [es3.enter_context(nc.psum_tensor(f"a_pT{i}", [128, 128], BF16)) for i in range(2)]
            c.pQ = [es3.enter_context(nc.psum_tensor(f"a_pQ{i}", [64, min(512, TOK)])) for i in range(2)]
            c.pKV = es3.enter_context(nc.psum_tensor("a_pKV", [128, 512]))
            m_store = emit_attn_a_proj(p, c)
            es3.close()

            c.a_wout = sA("a_wout", [128, 8 * 1024], BF16)
            c.a_bias = [sA(f"a_bias{i}", [128, GMAXW()]) for i in range(2)]
            c.a_kring = [sA(f"a_kring{i}", [128, 1024], BF16) for i in range(8)]
            c.a_vring = [sA(f"a_vring{i}", [128, 1056], BF16) for i in range(8)]
            c.a_P = [sA(f"a_P{i}", [128, 2048], BF16) for i in range(8)]
            c.a_accO = sA("a_accO", [65, 2048])
            c.a_aT = sA("a_aT", [128, 1024], BF16)
            c.a_recip = sA("a_recip", [1, 128])
            c.a_rb = sA("a_rb", [64, 128])

            es4 = ExitStack()
            c.psPO = [es4.enter_context(nc.psum_tensor(f"a_psPO{i}", [65, 128])) for i in range(2)]
            c.pL = [es4.enter_context(nc.psum_tensor(f"a_pL{i}", [128, 256])) for i in range(2)]
            c.pOut = [es4.enter_context(nc.psum_tensor(f"a_pOut{i}", [128, 512])) for i in range(2)]
            emit_attn_a(p, c, m_store)
            es4.close()
            es2.close()
            p.barrier()
            alloc_x_res(c.x1_dram)
        if ph == "attn_b":
            es2 = ExitStack()
            phase_scopes.append(es2)

            def sB(name, shape, dtype=F32, _es=es2):
                return _es.enter_context(nc.sbuf_tensor(name, shape, dtype))

            RB = 6
            c.b_xn = [sB(f"b_xn{i}", [128, 1024], BF16) for i in range(2)]
            c.b_xT = [sB(f"b_xT{i}", [128, 1024], BF16) for i in range(RB)]
            c.b_wqkvr = sB("b_wqkvr", [128, 8 * 512], BF16)
            c.b_wa = sB("b_wa", [128, 8], BF16)
            c.b_wo = sB("b_wo", [128, 1024], BF16)
            c.b_rows = sB("b_rows", [1, 512])
            c.b_rep = sB("b_rep", [128, 512])
            c.b_ucum = sB("b_ucum", [128, 128])
            c.b_sqkv = [sB(f"b_sqkv{i}", [128, 384]) for i in range(RB)]
            c.b_scr = [sB(f"b_scr{i}", [128, 128]) for i in range(RB)]
            c.b_z = [sB(f"b_z{i}", [128, 1]) for i in range(RB)]
            c.b_zab = [sB(f"b_zab{i}", [128, 1]) for i in range(RB)]
            c.b_ss = [sB(f"b_ss{i}", [128, 1]) for i in range(RB)]
            c.b_qn = [sB(f"b_qn{i}", [128, 256]) for i in range(RB)]
            c.b_buf = [sB(f"b_buf{i}", [128, 128]) for i in range(RB)]
            c.b_v8 = [sB(f"b_v8{i}", [128, 8]) for i in range(RB)]
            c.b_mh = [sB(f"b_mh{i}", [128, 128]) for i in range(RB)]
            c.b_f = [sB(f"b_f{i}", [128, 128]) for i in range(RB)]
            c.b_E = [sB(f"b_E{i}", [128, 128]) for i in range(RB)]
            c.b_Ei = [sB(f"b_Ei{i}", [128, 128]) for i in range(RB)]
            c.b_fc = [sB(f"b_fc{i}", [1, 128]) for i in range(RB)]
            c.b_efc = [sB(f"b_efc{i}", [128, 1]) for i in range(RB)]
            c.b_ef = [sB(f"b_ef{i}", [128, 128]) for i in range(RB)]
            c.b_dmp = [sB(f"b_dmp{i}", [128, 128]) for i in range(RB)]
            c.b_qt = [sB(f"b_qt{i}", [128, 128]) for i in range(RB)]
            c.b_kt = [sB(f"b_kt{i}", [128, 128]) for i in range(RB)]
            c.b_qtT = [sB(f"b_qtT{i}", [128, 128]) for i in range(RB)]
            c.b_ktT = [sB(f"b_ktT{i}", [128, 128]) for i in range(RB)]
            c.b_am = [sB(f"b_am{i}", [128, 128]) for i in range(RB)]
            c.b_S = [sB(f"b_S{i}", [128, 128]) for i in range(2)]
            c.b_o32 = sB("b_o32", [128, 128])
            c.b_obf = sB("b_obf", [128, 128 * 128], BF16)
            c.b_oss = sB("b_oss", [128, 128])
            c.b_on = sB("b_on", [128, 128], BF16)
            c.b_onT = sB("b_onT", [128, 128], BF16)
            c.b_part = sB("b_part", [128, 1024], BF16)
            c.b_stage = sB("b_stage", [128, 1024], BF16)

            es5 = ExitStack()
            c.pQKVR = es5.enter_context(nc.psum_tensor("b_pQKVR", [128, 512]))
            c.pA2 = es5.enter_context(nc.psum_tensor("b_pA2", [128, 1]))
            c.pF = es5.enter_context(nc.psum_tensor("b_pF", [128, 128]))
            c.pSmall = es5.enter_context(nc.psum_tensor("b_pSmall", [128, 128]))
            c.pT32b = [es5.enter_context(nc.psum_tensor(f"b_pT32{i}", [128, 128])) for i in range(2)]
            c.pO = es5.enter_context(nc.psum_tensor("b_pO", [128, 128]))
            c.pS = es5.enter_context(nc.psum_tensor("b_pS", [128, 128]))
            m_rstd, gnw_rep, m_bw = emit_attn_b(p, c)
            es5.close()
            es6 = ExitStack()
            c.pTbf = es6.enter_context(nc.psum_tensor("b_pTbf", [128, 128], BF16))
            c.pWo = [es6.enter_context(nc.psum_tensor(f"b_pWo{i}", [128, 512])) for i in range(2)]
            emit_attn_b_out(p, c, m_rstd, gnw_rep, m_bw)
            es6.close()
            es2.close()
            p.barrier()
        if ph in ("ffn1", "ffn2"):
            es2 = ExitStack()
            phase_scopes.append(es2)

            def s2(name, shape, dtype=F32, _es=es2, _ph=ph):
                return _es.enter_context(nc.sbuf_tensor(name + "_" + _ph, shape, dtype))

            def p2(name, shape, dtype=F32, _es=es2, _ph=ph):
                return _es.enter_context(nc.psum_tensor(name + "_" + _ph, shape, dtype))

            c.f_xn = s2("f_xn", [128, 1024], BF16)
            c.f_xnT = s2("f_xnT", [128, 8 * NTILE * 128], BF16)
            c.f_actT = s2("f_actT", [128, 32 * 4 * 128], BF16)
            c.f_w1a = [s2(f"f_w1a{i}", [128, 8 * 512], BF16) for i in range(2)]
            c.f_w1g = [s2(f"f_w1g{i}", [128, 8 * 512], BF16) for i in range(2)]
            c.f_w2c = [s2(f"f_w2c{i}", [128, 8 * 512], BF16) for i in range(2)]
            c.f_rw = s2("f_rw", [128, 8 * FFN_E])
            c.f_xn32 = s2("f_xn32", [128, 1024])
            c.f_xnT32 = s2("f_xnT32", [128, 1024])
            c.f_rb_row = s2("f_rb_row", [1, FFN_E])
            c.f_rb_rep = s2("f_rb_rep", [128, FFN_E])
            c.f_rlog = s2("f_rlog", [128, FFN_E])
            c.f_rmax = s2("f_rmax", [128, 1])
            c.f_esum = s2("f_esum", [128, 1])
            c.f_silu = [s2(f"f_silu{i}", [128, 512]) for i in range(2)]
            c.f_mask = s2("f_mask", [128, FFN_E])
            es3 = ExitStack()
            c.pT = [es3.enter_context(nc.psum_tensor(f"pT{i}_" + ph, [128, 128], BF16)) for i in range(2)]
            c.pT32 = [es3.enter_context(nc.psum_tensor(f"pT32{i}_" + ph, [128, 128])) for i in range(2)]
            c.pR8 = es3.enter_context(nc.psum_tensor("pR8_" + ph, [128, FFN_E]))
            m_gates = emit_ffn_prologue(p, c)
            es3.close()
            c.pA = [p2(f"pA{i}", [128, 512]) for i in range(2)]
            c.pG = [p2(f"pG{i}", [128, 512]) for i in range(2)]
            c.pY = [p2(f"pY{i}", [128, 512]) for i in range(4)]
            emit_ffn_experts(p, c, m_gates, grp_tiles=4,
                             img_ready=(ph == "ffn2" and "ffn1" in phases))
            es2.close()
            p.barrier()

    for tt in range(NTILE):
        p.sd.dma_start(out=y_out[tt * 128:(tt + 1) * 128, :],
                       in_=c.x_res[:, tt * 1024:(tt + 1) * 1024], chan="yst")
    p.barrier()

    block = es.enter_context(nc.Block())
    with es:
        p.finalize(block, nc, es)
    return nc


def shard_tokens(x):
    xt = np.ascontiguousarray(x.reshape(B, T // 128, 128, D))
    out = []
    for cid in range(NCORE):
        sl = xt[:, cid::8]
        out.append(np.ascontiguousarray(sl.reshape(TOK, D)))
    return out

def unshard_tokens(parts):
    y = np.empty((B, T // 128, 128, D), np.float32)
    for cid in range(NCORE):
        y[:, cid::8] = parts[cid].reshape(B, NSLOT, 128, D)
    return y.reshape(B, T, D)


def make_abias(cid):
    sl = _slopes(HEADS_A)
    pk = np.arange(128, dtype=np.float32)
    blocks = []
    for tt in range(NTILE):
        i = tt % NSLOT
        g = cid + 8 * i
        gmax = 8 * i + 7
        blk = np.full((128, gmax * 16), NEG, np.float32)
        for j in range(gmax):
            if j < g:
                for h in range(HEADS_A):
                    blk[:, j * 16 + h] = sl[h] * (128.0 * (j - g) + pk) - SHIFT_A
        blocks.append(blk)
    return np.ascontiguousarray(np.concatenate(blocks, axis=1))


_NC_CACHE = {}


def kernel(L0_attn_a_x, L0_attn_a_Wqkv, L0_attn_a_Wout, L0_attn_b_Wq, L0_attn_b_Wk,
           L0_attn_b_Wv, L0_attn_b_Wr, L0_attn_b_Wa, L0_attn_b_A_log, L0_attn_b_dt_bias,
           L0_attn_b_q_norm_w, L0_attn_b_k_norm_w, L0_attn_b_g_norm_w, L0_attn_b_Wo,
           L0_ffn_router_w, L0_ffn_router_b, L0_ffn_w1, L0_ffn_w2):
    phases = ALL_PHASES
    key = phases
    if key not in _NC_CACHE:
        _NC_CACHE[key] = build_program(phases)
    nc = _NC_CACHE[key]
    xs = shard_tokens(np.asarray(L0_attn_a_x, np.float32))
    in_maps = []
    for cid in range(NCORE):
        m = dict(x_c=xs[cid],
                 rw=np.ascontiguousarray(L0_ffn_router_w),
                 rb=np.ascontiguousarray(L0_ffn_router_b).reshape(1, FFN_E),
                 w1=np.ascontiguousarray(L0_ffn_w1),
                 w2=np.ascontiguousarray(L0_ffn_w2),
                 wqkv=np.ascontiguousarray(L0_attn_a_Wqkv),
                 wout=np.ascontiguousarray(L0_attn_a_Wout),
                 abias=make_abias(cid),
                 wbqkvr=np.ascontiguousarray(np.concatenate(
                     [np.asarray(W)[:, cid * 128:(cid + 1) * 128] for W in
                      (L0_attn_b_Wq, L0_attn_b_Wk, L0_attn_b_Wv, L0_attn_b_Wr)],
                     axis=1)),
                 wba=np.ascontiguousarray(np.asarray(L0_attn_b_Wa)[:, cid:cid + 1]),
                 wbo=np.ascontiguousarray(np.asarray(L0_attn_b_Wo)[cid * 128:(cid + 1) * 128, :]),
                 qnw=np.ascontiguousarray(np.asarray(L0_attn_b_q_norm_w).reshape(1, 128)),
                 knw=np.ascontiguousarray(np.asarray(L0_attn_b_k_norm_w).reshape(1, 128)),
                 gnw=np.ascontiguousarray(np.asarray(L0_attn_b_g_norm_w)[cid * 128:(cid + 1) * 128].reshape(1, 128)),
                 alog=np.asarray(L0_attn_b_A_log)[cid].reshape(1, 1).astype(np.float32),
                 dtb=np.asarray(L0_attn_b_dt_bias)[cid].reshape(1, 1).astype(np.float32))
        in_maps.append(m)
    res = run_bass_kernel_spmd(nc, in_maps, core_ids=list(range(NCORE)))
    return unshard_tokens([r["y"] for r in res.results]).astype(np.float32)


# version 08
# speedup vs baseline: 1.1515x; 1.0073x over previous
import math
import os
import numpy as np
from contextlib import ExitStack

import ml_dtypes
import concourse.bass as bass
import concourse.mybir as mybir
from concourse.bass_utils import run_bass_kernel_spmd
from concourse.bass import IndirectOffsetOnAxis

F32 = mybir.dt.float32
BF16 = mybir.dt.bfloat16
I32 = mybir.dt.int32
AF = mybir.ActivationFunctionType
ALU = mybir.AluOpType
AX = mybir.AxisListType

B, T, D = 2, 8192, 1024
NCORE = 8
NSLOT = 8
NTILE = 2 * NSLOT
TOK = NTILE * 128
SHIFT_A = 20.0
HEADS_A = 16
NEG = -1e30
FFN_E, FFN_H = 8, 4096
EPS_LN = 1e-5
EPS_B = 1e-5
RG8 = [list(range(8))]

ATTN_DBG = 3
A_MAXJJ = 99
A_SLIM = int(os.environ.get("A_SLIM", "0"))
A_SKIP = int(os.environ.get("A_SKIP", "1"))
DBG_TT = 15
ALL_PHASES = ("attn_a", "ffn1", "attn_b", "ffn2")


def _pair_windows():
    sl = _slopes(HEADS_A)
    return [int((97.3 / sl[2 * pr + 1] + 127.0) // 128.0) for pr in range(8)]


def ABW():
    return 2 * sum(8 * i + 7 for i in range(NSLOT)) * 16


def GMAXW():
    return (8 * (NSLOT - 1) + 7) * 16


def _slopes(n):
    start = 2.0 ** (-(8.0 / n))
    return np.asarray([start * start ** i for i in range(n)], dtype=np.float32)


class _EngProxy:
    def __init__(self, prog, name):
        self._p = prog
        self._n = name

    def __getattr__(self, fn):
        def emit(*a, waits=(), inc=None, chan=None, **k):
            return self._p._add(self._n, fn, a, k, waits, inc, chan)
        return emit

    def mark(self):
        return (self._n, self._p.vals[self._n])


class Prog:
    SERIAL = {"v": True, "s": True, "t": False, "sd": False, "g": False}
    STEP = {"v": 1, "s": 1, "t": 1, "sd": 16, "g": 16}

    def __init__(self):
        self.ops = {n: [] for n in self.STEP}
        self.vals = dict.fromkeys(self.STEP, 0)
        self.chan_eng = {n: n for n in self.STEP}
        self.v = _EngProxy(self, "v")
        self.s = _EngProxy(self, "s")
        self.t = _EngProxy(self, "t")
        self.sd = _EngProxy(self, "sd")
        self.g = _EngProxy(self, "g")
        self._max_waited = {}

    def cmark(self, chan):
        return (chan, self.vals.get(chan, 0))

    def _add(self, eng, fn, a, k, waits, inc, chan):
        chan = chan or eng
        if chan not in self.vals:
            self.vals[chan] = 0
        step = self.STEP[eng] if inc is None else inc
        ws = []
        for (pc, pv) in waits:
            if pc == eng and self.SERIAL.get(eng):
                continue
            key = (eng, pc)
            if pv > 0 and pv > self._max_waited.get(key, 0):
                self._max_waited[key] = pv
                ws.append((pc, pv))
        self.ops[eng].append((fn, a, k, ws, step, chan, self.vals[eng] if chan == eng else -1,
                              self.vals[eng]))
        self.vals[chan] += step
        return (chan, self.vals[chan])

    def barrier(self, exclude=("wsh",)):
        snap = {cn: v for cn, v in self.vals.items() if v > 0 and cn not in exclude}
        for e in self.STEP:
            for pc, pv in snap.items():
                key = (e, pc)
                if pv > self._max_waited.get(key, 0):
                    self._max_waited[key] = pv
                    self.ops[e].append(("__wait__", (pc, pv), {}, [], 0, None, -1, 0))

    def finalize(self, block, nc, es):
        sems = {}
        for cn in self.vals:
            sems[cn] = es.enter_context(nc.semaphore("sem_" + cn))
        secmap = {"v": block.vector, "s": block.scalar, "t": block.tensor,
                  "sd": block.sync, "g": block.gpsimd}

        def runner(eng_name):
            def run(e):
                serial = self.SERIAL[eng_name]
                engsem = sems[eng_name]
                for (fn, a, k, ws, step, chan, selfwait_val, _ev) in self.ops[eng_name]:
                    if fn == "__wait__":
                        pc, pv = a
                        e.wait_ge(sems[pc], pv)
                        continue
                    for (pc, pv) in ws:
                        e.wait_ge(sems[pc], pv)
                    r = getattr(e, fn)(*a, **k)
                    if serial and chan == eng_name and selfwait_val > 0:
                        r._wait_ge(engsem, selfwait_val)
                    if step:
                        r.then_inc(sems[chan], step)
            return run
        for n, sec in secmap.items():
            sec(runner(n))


class Ctx:
    pass


def emit_ln_tile(p, c, src_ap, out_ap, extra_dep=()):
    v, s = p.v, p.s
    m = v.tensor_reduce(c.ln_nm[:], src_ap, AX.X, ALU.add, waits=extra_dep)
    v.tensor_scalar(c.ln_nm[:], c.ln_nm[:], -1.0 / D, None, ALU.mult)
    mnm = v.mark()
    s.activation(c.ln_dump[:], src_ap, AF.Square, accum_out=c.ln_ssq[:], waits=[mnm])
    mssq = s.mark()
    v.tensor_mul(c.ln_m2[:], c.ln_nm[:], c.ln_nm[:], waits=[mssq])
    v.tensor_scalar(c.ln_ssq[:], c.ln_ssq[:], 1.0 / D, None, ALU.mult)
    v.tensor_sub(c.ln_ssq[:], c.ln_ssq[:], c.ln_m2[:])
    mvar = v.mark()
    s.activation(c.ln_ssq[:], c.ln_ssq[:], AF.Ln, bias=c.cpool[:, 0:1], scale=1.0, waits=[mvar])
    s.activation(c.ln_ssq[:], c.ln_ssq[:], AF.Exp, scale=-0.5)
    mrstd = s.mark()
    v.tensor_scalar(out_ap, src_ap, c.ln_nm[:, 0:1], c.ln_ssq[:, 0:1],
                    ALU.add, ALU.mult, waits=[mrstd])
    return v.mark()


def emit_transpose8(p, c, src_tile_bf, dst_fn, dep):
    marks = []
    prev_evac = [None, None]
    for dc in range(8):
        bank = dc % 2
        w = [dep]
        if prev_evac[bank] is not None:
            w.append(prev_evac[bank])
        p.t.transpose(c.pT[bank][:], src_tile_bf[:, dc * 128:(dc + 1) * 128],
                      c.ident_bf[:], waits=w)
        mt = p.t.mark()
        p.v.tensor_copy(dst_fn(dc), c.pT[bank][:], waits=[mt])
        prev_evac[bank] = p.v.mark()
        marks.append(prev_evac[bank])
    return marks[-1]


def emit_ffn_prologue(p, c):
    p.sd.dma_start(out=c.f_rw[:].rearrange("p (dc e) -> p dc e", dc=8),
                   in_=c.d_rw.rearrange("(dc p) e -> p dc e", p=128), chan="fpro")
    p.sd.dma_start(out=c.f_rb_row[:], in_=c.d_rb, chan="fpro")
    mrw = p.cmark("fpro")
    mrb = mrw
    p.t.matmul(c.pR8[:], c.ones_row[:], c.f_rb_row[:], start=True, stop=True,
               waits=[mrb])
    mrbb = p.t.mark()
    p.v.tensor_copy(c.f_rb_rep[:], c.pR8[:], waits=[mrbb])
    mrbrep = p.v.mark()

    tp_done = []
    for tt in range(NTILE):
        mln = emit_ln_tile(p, c, c.x_res[:, tt * 1024:(tt + 1) * 1024], c.f_xn[:],
                           extra_dep=())
        p.v.tensor_scalar(c.f_xn32[:], c.x_res[:, tt * 1024:(tt + 1) * 1024],
                          c.ln_nm[:, 0:1], c.ln_ssq[:, 0:1], ALU.add, ALU.mult)
        mln32 = p.v.mark()
        mtp = emit_transpose8(
            p, c, c.f_xn[:],
            lambda dc, tt=tt: c.f_xnT[:, (dc * NTILE + tt) * 128:(dc * NTILE + tt) * 128 + 128],
            mln)
        tp_done.append(mtp)
        prev = [None, None]
        for dc in range(8):
            bk = dc % 2
            w = [mln32]
            if prev[bk] is not None:
                w.append(prev[bk])
            p.t.transpose(c.pT32[bk][:], c.f_xn32[:, dc * 128:(dc + 1) * 128],
                          c.ident32[:], waits=w)
            mtt = p.t.mark()
            p.v.tensor_copy(c.f_xnT32[:, dc * 128:(dc + 1) * 128], c.pT32[bk][:], waits=[mtt])
            prev[bk] = p.v.mark()
        mtp32 = p.v.mark()
        for dc in range(8):
            w = [mtp32, mrw] if dc == 0 else []
            if dc == 0 and tt == 0:
                w.append(mrbrep)
            p.t.matmul(c.pR8[:], c.f_xnT32[:, dc * 128:(dc + 1) * 128],
                       c.f_rw[:, dc * FFN_E:(dc + 1) * FFN_E],
                       start=(dc == 0), stop=(dc == 7), waits=w)
        mrl = p.t.mark()
        p.v.tensor_add(c.f_rlog[:], c.pR8[:], c.f_rb_rep[:], waits=[mrl])
        p.v.reduce_max(c.f_rmax[:], c.f_rlog[:], AX.X)
        p.v.tensor_scalar(c.f_mask[:], c.f_rlog[:], c.f_rmax[:, 0:1], None, ALU.is_equal)
        p.v.tensor_scalar(c.f_rmax[:], c.f_rmax[:], -1.0, None, ALU.mult)
        mrm = p.v.mark()
        p.s.activation(c.f_rlog[:], c.f_rlog[:], AF.Exp, bias=c.f_rmax[:, 0:1],
                       accum_out=c.f_esum[:], waits=[mrm])
        mre = p.s.mark()
        p.v.reciprocal(c.f_esum[:], c.f_esum[:], waits=[mre])
        p.v.tensor_scalar(c.gates[:, tt * FFN_E:(tt + 1) * FFN_E],
                          c.f_mask[:], c.f_esum[:, 0:1], None, ALU.mult)

    return p.v.mark()


def emit_ffn_experts(p, c, m_gates, grp_tiles=4, img_ready=False):
    NG = NTILE // grp_tiles
    w1_use = [[], []]
    w2_use = [[], []]
    pa_use = [None, None]
    pg_use = [None, None]
    y_use = None
    act_ready = {}
    wb1 = {}
    wb2 = {}

    for grp in range(NG):
        for e in range(FFN_E):
            for j in range(8):
                buf = (e * 8 + j) % 2
                w = list(w1_use[buf])
                if img_ready or (e, j) in wb1:
                    if (e, j) in wb1:
                        w.append(wb1[(e, j)])
                    p.g.dma_start(out=c.f_w1a[buf][:], in_=c.d_w1sh[e, j, 0],
                                  waits=w, chan=f"w1_{buf}")
                    p.g.dma_start(out=c.f_w1g[buf][:], in_=c.d_w1sh[e, j, 1],
                                  chan=f"w1_{buf}")
                    mw1 = p.cmark(f"w1_{buf}")
                    w1_use[buf] = []
                else:
                    p.g.dma_start(out=c.f_w1a[buf][:].rearrange("p (dc n) -> p dc n", dc=8),
                                  in_=c.d_w1[e, :, j * 512:(j + 1) * 512]
                                  .rearrange("(dc p) n -> p dc n", p=128), waits=w,
                                  chan=f"w1_{buf}")
                    p.g.dma_start(out=c.f_w1g[buf][:].rearrange("p (dc n) -> p dc n", dc=8),
                                  in_=c.d_w1[e, :, FFN_H + j * 512:FFN_H + (j + 1) * 512]
                                  .rearrange("(dc p) n -> p dc n", p=128), chan=f"w1_{buf}")
                    mw1 = p.cmark(f"w1_{buf}")
                    p.sd.dma_start(out=c.d_w1sh[e, j, 0], in_=c.f_w1a[buf][:],
                                   waits=[mw1], chan="w1wb")
                    p.sd.dma_start(out=c.d_w1sh[e, j, 1], in_=c.f_w1g[buf][:],
                                   chan="w1wb")
                    wb1[(e, j)] = p.cmark("w1wb")
                    w1_use[buf] = [wb1[(e, j)]]
                for hc in range(4):
                    ch = j * 4 + hc
                    pb = ch % 2
                    w = [mw1]
                    if ch < 2 and grp == 0 and e == 0:
                        w.append(m_gates)
                    if pa_use[pb] is not None:
                        w.append(pa_use[pb])
                    if pg_use[pb] is not None:
                        w.append(pg_use[pb])
                    xs = c.f_xnT
                    base = lambda dc: (dc * NTILE + grp * grp_tiles) * 128
                    for dc in range(8):
                        p.t.matmul(c.pA[pb][:],
                                   c.f_w1a[buf][:, dc * 512 + hc * 128: dc * 512 + hc * 128 + 128],
                                   xs[:, base(dc): base(dc) + grp_tiles * 128],
                                   start=(dc == 0), stop=(dc == 7),
                                   waits=w if dc == 0 else [])
                    for dc in range(8):
                        p.t.matmul(c.pG[pb][:],
                                   c.f_w1g[buf][:, dc * 512 + hc * 128: dc * 512 + hc * 128 + 128],
                                   xs[:, base(dc): base(dc) + grp_tiles * 128],
                                   start=(dc == 0), stop=(dc == 7))
                    mag = p.t.mark()
                    w1_use[buf] = ([wb1[(e, j)]] if (e, j) in wb1 else []) + [mag]
                    p.s.activation(c.f_silu[pb][:], c.pA[pb][:], AF.Silu, waits=[mag])
                    pa_use[pb] = p.s.mark()
                    p.v.tensor_mul(c.f_actT[:, ch * (grp_tiles * 128):(ch + 1) * (grp_tiles * 128)],
                                   c.pG[pb][:], c.f_silu[pb][:], waits=[pa_use[pb]])
                    pg_use[pb] = p.v.mark()
                    act_ready[ch] = pg_use[pb]

            GT = grp_tiles
            for nh in range(2):
                for cb in range(4):
                    buf = (nh * 4 + cb) % 2
                    w = list(w2_use[buf])
                    key = (e, nh, cb)
                    if img_ready or key in wb2:
                        if key in wb2:
                            w.append(wb2[key])
                        p.g.dma_start(out=c.f_w2c[buf][:], in_=c.d_w2sh[e, nh, cb],
                                      waits=w, chan=f"w2_{buf}")
                        mw2 = p.cmark(f"w2_{buf}")
                        wbdep2 = []
                    else:
                        p.g.dma_start(out=c.f_w2c[buf][:].rearrange("p (cc n) -> p cc n", cc=8),
                                      in_=c.d_w2[e, cb * 1024:(cb + 1) * 1024,
                                                 nh * 512:(nh + 1) * 512]
                                      .rearrange("(cc p) n -> p cc n", p=128),
                                      waits=w, chan=f"w2_{buf}")
                        mw2 = p.cmark(f"w2_{buf}")
                        p.sd.dma_start(out=c.d_w2sh[e, nh, cb], in_=c.f_w2c[buf][:],
                                       waits=[mw2], chan="w2wb")
                        wb2[key] = p.cmark("w2wb")
                        wbdep2 = [wb2[key]]
                    for ch8 in range(8):
                        ch = cb * 8 + ch8
                        w = [mw2, act_ready[ch]]
                        if ch == 0 and y_use is not None:
                            w.append(y_use)
                        for ti in range(4):
                            p.t.matmul(c.pY[ti][:],
                                       c.f_actT[:, ch * (GT * 128) + ti * 128: ch * (GT * 128) + ti * 128 + 128],
                                       c.f_w2c[buf][:, ch8 * 512:(ch8 + 1) * 512],
                                       start=(ch == 0), stop=(ch == 31),
                                       waits=w if ti == 0 else [])
                        w2_use[buf] = wbdep2 + [p.t.mark()]
                my = p.t.mark()
                for ti in range(4):
                    tt = grp * grp_tiles + ti
                    p.v.scalar_tensor_tensor(
                        c.x_res[:, tt * 1024 + nh * 512: tt * 1024 + nh * 512 + 512],
                        c.pY[ti][:],
                        c.gates[:, tt * FFN_E + e: tt * FFN_E + e + 1],
                        c.x_res[:, tt * 1024 + nh * 512: tt * 1024 + nh * 512 + 512],
                        ALU.mult, ALU.add, waits=[my])
                y_use = p.v.mark()




def emit_attn_a_proj(p, c):
    p.v.memset(c.a_QTz[0][64:128, :], 0.0)
    p.v.memset(c.a_QTz[1][0:64, :], 0.0)
    xst_use = [None, None]
    for tt in range(NTILE):
        xb = tt % 2
        w = [] if xst_use[xb] is None else [xst_use[xb]]
        p.sd.dma_start(out=c.a_xst[xb][:], in_=c.x_in[tt * 128:(tt + 1) * 128, :],
                       waits=w, chan=f"xst{xb}")
        mx = p.cmark(f"xst{xb}")
        mln = emit_ln_tile(p, c, c.a_xst[xb][:], c.a_xn[:], extra_dep=[mx])
        emit_transpose8(
            p, c, c.a_xn[:],
            lambda dc, tt=tt: c.a_xnT[:, (dc * NTILE + tt) * 128:(dc * NTILE + tt) * 128 + 128],
            mln)
        xst_use[xb] = p.s.mark()
    mtp_all = p.v.mark()

    wb_use = [None, None]
    pq_use = [None, None]
    pkv_use = None
    ktmp_use = None
    ktst_use = None
    for cg in range(6):
        buf = cg % 2
        w = []
        if wb_use[buf] is not None:
            w.append(wb_use[buf])
        p.g.dma_start(out=c.a_wbuf[buf][:].rearrange("p (dc n) -> p dc n", dc=8),
                      in_=c.d_wqkv[:, cg * 512:(cg + 1) * 512]
                      .rearrange("(dc p) n -> p dc n", p=128), waits=w, chan=f"wq{buf}")
        mw = p.cmark(f"wq{buf}")
        if cg < 2:
            QW = min(512, TOK)
            NQG = TOK // QW
            for hh in range(8):
                h = cg * 8 + hh
                pr, odd = h // 2, h % 2
                for tg in range(NQG):
                    pb = (hh * NQG + tg) % 2
                    w = [mw]
                    if cg == 0 and hh == 0 and tg == 0:
                        w.append(mtp_all)
                    if pq_use[pb] is not None:
                        w.append(pq_use[pb])
                    for dc in range(8):
                        p.t.matmul(c.pQ[pb][:],
                                   c.a_wbuf[buf][:, dc * 512 + hh * 64: dc * 512 + hh * 64 + 64],
                                   c.a_xnT[:, dc * NTILE * 128 + tg * QW: dc * NTILE * 128 + tg * QW + QW],
                                   start=(dc == 0), stop=(dc == 7),
                                   waits=w if dc == 0 else [])
                    mq = p.t.mark()
                    wb_use[buf] = mq
                    dst = c.a_QTz[odd][odd * 64:(odd + 1) * 64,
                                       pr * TOK + tg * QW: pr * TOK + tg * QW + QW]
                    wq2 = [mq]
                    p.v.tensor_scalar(dst, c.pQ[pb][:], 0.125, None, ALU.mult, waits=wq2)
                    pq_use[pb] = p.v.mark()
        else:
            for tt in range(NTILE):
                w = [mw]
                if cg == 2 and tt == 0:
                    w.append(mtp_all)
                if pkv_use is not None:
                    w.append(pkv_use)
                for dc in range(8):
                    p.t.matmul(c.pKV[:],
                               c.a_xnT[:, (dc * NTILE + tt) * 128:(dc * NTILE + tt) * 128 + 128],
                               c.a_wbuf[buf][:, dc * 512:(dc + 1) * 512],
                               start=(dc == 0), stop=(dc == 7),
                               waits=w if dc == 0 else [])
                mkv = p.t.mark()
                wb_use[buf] = mkv
                ww = [mkv]
                if ktmp_use is not None:
                    ww.append(ktmp_use)
                p.v.tensor_copy(c.a_ktmp[:], c.pKV[:], waits=ww)
                pkv_use = p.v.mark()
                if cg >= 4:
                    p.sd.dma_start(out=c.cc_v_in[tt * 128:(tt + 1) * 128,
                                                 (cg - 4) * 512:(cg - 3) * 512],
                                   in_=c.a_ktmp[:], waits=[pkv_use], chan="kvst")
                    ktmp_use = p.cmark("kvst")
                else:
                    prevq = [None, None]
                    for qq in range(4):
                        wq = [pkv_use]
                        if prevq[qq % 2] is not None:
                            wq.append(prevq[qq % 2])
                        p.t.transpose(c.pT[qq % 2][:],
                                      c.a_ktmp[:, qq * 128:(qq + 1) * 128],
                                      c.ident_bf[:], waits=wq)
                        mtt2 = p.t.mark()
                        p.v.tensor_copy(c.a_ktT[:, qq * 128:(qq + 1) * 128],
                                        c.pT[qq % 2][:], waits=[mtt2])
                        prevq[qq % 2] = p.v.mark()
                    mktT = p.v.mark()
                    ktmp_use = mktT
                    wst = [mktT]
                    if ktst_use is not None:
                        wst.append(ktst_use)
                    p.sd.dma_start(
                        out=c.cc_kt_in[tt * 1024 + (cg - 2) * 512:
                                       tt * 1024 + (cg - 2) * 512 + 512, :]
                        .rearrange("(pr d) kp -> d pr kp", d=128),
                        in_=c.a_ktT[:].rearrange("p (pr kp) -> p pr kp", pr=4),
                        waits=wst, chan="ktst")
                    ktst_use = p.cmark("ktst")
    c.m_proj_s = p.s.mark()
    return (p.cmark("kvst"), p.cmark("ktst"))


def emit_attn_a(p, c, m_store):
    NR = 8
    m_store_kv, m_store_kt = m_store
    xst_use = [c.m_proj_s, c.m_proj_s]
    p.g.dma_start(out=c.a_wout[:].rearrange("p (dc n) -> p dc n", dc=8),
                  in_=c.d_wout.rearrange("(dc p) n -> p dc n", p=128), chan="wout",
                  waits=[p.t.mark(), p.v.mark(), c.m_proj_s, m_store_kv, m_store_kt])
    m_wout = p.cmark("wout")
    p.g.collective_compute("AllGather", ALU.bypass, RG8,
                           [c.cc_v_in.opt()], [c.cc_v_all.opt()],
                           waits=[m_store_kv], inc=1, chan="cc")
    p.g.collective_compute("AllGather", ALU.bypass, RG8,
                           [c.cc_kt_in.opt()], [c.cc_kt_all.opt()],
                           waits=[m_store_kt], inc=1, chan="cc")
    m_ag = p.cmark("cc")

    for r in range(NR):
        p.v.memset(c.a_vring[r].rearrange("p (k s) -> p k s", s=66)[:, :, 64:65], 1.0,
                   waits=[m_store_kv, m_store_kt] if r == 0 else [])
    m_ones = p.v.mark()

    m_proj_t = p.t.mark()
    ring_use = [m_proj_t] * NR
    bias_use = [[m_store_kv, m_store_kt], [m_store_kv, m_store_kt]]
    pl_use = [None, None]
    psb_use = {}
    psb_ready = {}
    po_flush = {}
    pb2 = [0]
    pout_use = None
    abias_off = 0
    pb = 0

    if ATTN_DBG < 1:
        return
    CH = 4
    acc_first = {}
    accO_use = None
    A_WIN = _pair_windows()
    for tt in range(NTILE):
        bnum, i = tt // NSLOT, tt % NSLOT
        gmax = 8 * i + 7

        def active(jj, pr):
            if not A_SKIP:
                return True
            return jj == 0 or (8 * i - (jj - 1)) <= A_WIN[pr]

        def prmin_of(jj):
            for pr in range(8):
                if active(jj, pr):
                    return pr
            return 8
        bb = tt % 2
        bu = bias_use[bb]
        w = [] if bu is None else (list(bu) if isinstance(bu, list) else [bu])
        p.sd.dma_start(out=c.a_bias[bb][:, 0:gmax * 16],
                       in_=c.d_abias[:, abias_off:abias_off + gmax * 16], waits=w,
                       chan=f"ab{bb}")
        m_bias = p.cmark(f"ab{bb}")
        abias_off += gmax * 16

        njj = min(gmax + 1, A_MAXJJ)
        nch = (njj + CH - 1) // CH
        for chi in range(nch):
            jlo, jhi = chi * CH, min(njj, chi * CH + CH)
            for jj in range(jlo, jhi):
                r = jj % NR
                sl = jj % (2 * CH)
                if jj == 0:
                    vbuf, vrow = c.cc_v_in, tt * 128
                    ktbuf, ktrow = c.cc_kt_in, tt * 1024
                else:
                    j = jj - 1
                    vbuf = c.cc_v_all
                    vrow = (j % 8) * TOK + (bnum * NSLOT + j // 8) * 128
                    ktbuf = c.cc_kt_all
                    ktrow = (j % 8) * TOK * 8 + (bnum * NSLOT + j // 8) * 1024
                w = [] if ring_use[r] is None else [ring_use[r]]
                if jj == 1:
                    w.append(m_ag)
                if jj == 0:
                    w.append(m_store_kv)
                    w.append(m_store_kt)
                pm = prmin_of(jj)
                pml = pm if A_SLIM else 0
                p.sd.dma_start(
                    out=c.a_kring[r][:, pml * 128:].rearrange("p (pr kp) -> p pr kp", pr=8 - pml),
                    in_=ktbuf[ktrow + pml * 128:ktrow + 1024, :]
                    .rearrange("(pr d) kp -> d pr kp", d=128),
                    waits=w, chan=f"kv{r}")
                for hh in range(2):
                    p.g.dma_start(
                        out=c.a_vring[r].rearrange("p (k m) -> p k m", m=132)[:, pml:8, hh * 66:hh * 66 + 64],
                        in_=vbuf[vrow:vrow + 128, :]
                        .rearrange("p (k n) -> p k n", n=128)[:, pml:8, hh * 64:hh * 64 + 64],
                        waits=w if hh == 0 else [], chan=f"kv{r}")
                m_ring = p.cmark(f"kv{r}")
                m_ring_w = [m_ring, m_ones] if jj == 0 else [m_ring]

                if ATTN_DBG < 2.2:
                    ring_use[r] = m_ring
                    continue
                for pri, pr in enumerate(range(pm, 8)):
                    w1 = list(m_ring_w) if pri == 0 else []
                    if pl_use[pb] is not None:
                        w1.append(pl_use[pb])
                    for odd in range(2):
                        p.t.matmul(c.pL[pb][:, odd * 128:(odd + 1) * 128],
                                   c.a_kring[r][:, pr * 128:(pr + 1) * 128],
                                   c.a_QTz[odd][:, pr * TOK + tt * 128: pr * TOK + tt * 128 + 128],
                                   start=True, stop=True,
                                   waits=w1 if odd == 0 else [])
                    m1 = p.t.mark()
                    ring_use[r] = m1
                    if ATTN_DBG < 2.3:
                        pb ^= 1
                        continue
                    ws = [m1]
                    if jj > 0:
                        ws.append(m_bias)
                    mpsb = psb_use.get((sl, pr))
                    if mpsb is not None:
                        ws.append(mpsb)
                    dst = c.a_P[sl]
                    for odd in range(2):
                        h = pr * 2 + odd
                        bias_ap = (c.diag_bias[:, h:h + 1] if jj == 0
                                   else c.a_bias[bb][:, (jj - 1) * 16 + h:(jj - 1) * 16 + h + 1])
                        p.s.activation(dst[:, pr * 256 + odd * 128: pr * 256 + odd * 128 + 128],
                                       c.pL[pb][:, odd * 128:(odd + 1) * 128],
                                       AF.Exp, bias=bias_ap,
                                       waits=ws if odd == 0 else [])
                    mp = p.s.mark()
                    pl_use[pb] = mp
                    if jj == 0 and ATTN_DBG >= 2.4:
                        p.v.tensor_mul(dst[:, pr * 256:(pr + 1) * 256],
                                       dst[:, pr * 256:(pr + 1) * 256],
                                       c.mask256[:], waits=[mp])
                    pb ^= 1
                psb_ready[jj % (2 * CH)] = (p.s.mark(), p.v.mark())

            if ATTN_DBG < 2.5:
                continue
            for pr in range(8):
                jlist = [jj for jj in range(jlo, jhi) if active(jj, pr)]
                if not jlist:
                    continue
                for odd in range(2):
                    pb2i = pb2[0]
                    first = True
                    for jj in jlist:
                        r = jj % NR
                        sl = jj % (2 * CH)
                        w2 = []
                        if first:
                            w2 += list(psb_ready[(jhi - 1) % (2 * CH)])
                            mflush = po_flush.get(pb2i)
                            if mflush is not None:
                                w2.append(mflush)
                        p.t.matmul(c.psPO[pb2i][:],
                                   c.a_vring[r][:, pr * 132 + odd * 66: pr * 132 + odd * 66 + 65],
                                   c.a_P[sl][:, pr * 256 + odd * 128: pr * 256 + odd * 128 + 128],
                                   start=first, stop=(jj == jlist[-1]),
                                   waits=w2)
                        first = False
                        psb_use[(sl, pr)] = p.t.mark()
                    m2 = p.t.mark()
                    for jj in jlist:
                        ring_use[jj % NR] = m2
                    aslice = c.a_accO[0:65, (pr * 2 + odd) * 128:(pr * 2 + odd) * 128 + 128]
                    wv = [m2]
                    if chi == 0 and accO_use is not None:
                        wv.append(accO_use)
                    if chi == 0:
                        p.v.tensor_copy(aslice, c.psPO[pb2i][:], waits=wv)
                    else:
                        p.v.tensor_add(aslice, aslice, c.psPO[pb2i][:], waits=wv)
                    po_flush[pb2i] = p.v.mark()
                    pb2[0] ^= 1
        m_attn = p.v.mark()
        if ATTN_DBG < 2.95:
            continue

        for pr in range(8):
            for odd in range(2):
                cslice = slice((pr * 2 + odd) * 128, (pr * 2 + odd) * 128 + 128)
                p.v.reciprocal(c.a_recip[:], c.a_accO[64:65, cslice], waits=[m_attn])
                mrc = p.v.mark()
                w = [mrc]
                if pl_use[0] is not None:
                    w.append(pl_use[0])
                p.t.matmul(c.pL[0][0:64, 0:128], c.ones_row[:, 0:64], c.a_recip[:],
                           start=True, stop=True, waits=w)
                mrb = p.t.mark()
                p.s.copy(c.a_rb[:], c.pL[0][0:64, 0:128], waits=[mrb])
                mrbe = p.s.mark()
                pl_use[0] = mrbe
                p.v.tensor_mul(c.a_aT[odd * 64:(odd + 1) * 64, pr * 128:(pr + 1) * 128],
                               c.a_accO[0:64, cslice], c.a_rb[:], waits=[mrbe])
        accO_use = p.v.mark()
        m_norm = p.v.mark()
        bias_use[bb] = p.s.mark()

        if ATTN_DBG < 2.98:
            continue
        xb = tt % 2
        w = [] if xst_use[xb] is None else [xst_use[xb]]
        p.sd.dma_start(out=c.a_xst[xb][:], in_=c.x_in[tt * 128:(tt + 1) * 128, :],
                       waits=w, chan=f"xst{xb}")
        mx = p.cmark(f"xst{xb}")
        for nh in range(2):
            first = True
            for pr in range(8):
                w = [m_norm, m_wout] if first else []
                if first and pout_use is not None:
                    w.append(pout_use)
                p.t.matmul(c.pOut[nh][:],
                           c.a_aT[:, pr * 128:(pr + 1) * 128],
                           c.a_wout[:, pr * 1024 + nh * 512: pr * 1024 + nh * 512 + 512],
                           start=first, stop=(pr == 7), waits=w)
                first = False
            mo = p.t.mark()
            dst = c.a_xst[xb][:, nh * 512:(nh + 1) * 512]
            p.v.tensor_add(dst, dst, c.pOut[nh][:], waits=[mo, mx])
            pout_use = p.v.mark()
        p.sd.dma_start(out=c.x1_dram[tt * 128:(tt + 1) * 128, :], in_=c.a_xst[xb][:],
                       waits=[pout_use], chan="x1st")
        xst_use[xb] = p.cmark("x1st")
        if tt == DBG_TT:
            p.sd.dma_start(out=c.dbg_aT, in_=c.a_aT[:], waits=[m_norm], chan="dbg")
            p.sd.dma_start(out=c.dbg_acc, in_=c.a_accO[:], waits=[m_attn], chan="dbg")
            p.sd.dma_start(out=c.dbg_kv[:, 0:1024], in_=c.cc_v_in[TOK - 256:TOK, :], waits=[m_store_kv], chan="dbg")
            p.sd.dma_start(out=c.dbg_vr, in_=c.a_vring[0][:], waits=[m_attn], chan="dbg")




def emit_attn_b(p, c):
    NCH = T // 128
    p.v.memset(c.b_S[0][:], 0.0)
    p.v.memset(c.b_S[1][:], 0.0)
    xnst = [None, None]
    for tt in range(NTILE):
        xb = tt % 2
        dep = [] if xnst[xb] is None else [xnst[xb]]
        mln = emit_ln_tile(p, c, c.x_res[:, tt * 1024:(tt + 1) * 1024], c.b_xn[xb][:],
                           extra_dep=dep)
        p.sd.dma_start(out=c.cc_xn_in[tt * 128:(tt + 1) * 128, :], in_=c.b_xn[xb][:],
                       waits=[mln], chan="bxnst")
        xnst[xb] = p.cmark("bxnst")
    m_xnst = p.cmark("bxnst")
    p.g.collective_compute("AllGather", ALU.bypass, RG8,
                           [c.cc_xn_in.opt()], [c.cc_xn_all.opt()],
                           waits=[m_xnst], inc=1, chan="cc")
    m_ag = p.cmark("cc")

    p.g.dma_start(out=c.b_wqkvr[:].rearrange("p (dc n) -> p dc n", dc=8),
                  in_=c.d_wbqkvr.rearrange("(dc p) n -> p dc n", p=128), chan="bw")
    p.g.dma_start(out=c.b_wa[:].rearrange("p (dc n) -> p dc n", dc=8),
                  in_=c.d_wba.rearrange("(dc p) n -> p dc n", p=128), chan="bw",
                  allow_slow_non_contiguous=True)
    p.g.dma_start(out=c.b_wo[:].rearrange("p (dc n) -> p dc n", dc=8),
                  in_=c.d_wbo.rearrange("(dc p) n -> p dc n", p=128), chan="bw")
    m_bw = p.cmark("bw")
    p.sd.dma_start(out=c.b_rows[0:1, 0:128], in_=c.d_qnw, chan="brow")
    p.sd.dma_start(out=c.b_rows[0:1, 128:256], in_=c.d_knw, chan="brow")
    p.sd.dma_start(out=c.b_rows[0:1, 256:384], in_=c.d_gnw, chan="brow")
    p.sd.dma_start(out=c.b_rows[0:1, 384:385], in_=c.d_alog, chan="brow")
    p.sd.dma_start(out=c.b_rows[0:1, 385:386], in_=c.d_dtb, chan="brow")
    p.sd.dma_start(out=c.b_ucum[:], in_=c.ucum_d.ap(), chan="brow")
    m_rows = p.cmark("brow")
    p.s.activation(c.b_rows[0:1, 384:385], c.b_rows[0:1, 384:385], AF.Exp,
                   waits=[m_rows])
    m_ea = p.s.mark()
    p.v.tensor_scalar(c.b_rows[0:1, 384:385], c.b_rows[0:1, 384:385], -1.0, None,
                      ALU.mult, waits=[m_ea])
    m_ea2 = p.v.mark()
    p.t.matmul(c.pQKVR[:], c.ones_row[:], c.b_rows[0:1, :], start=True, stop=True,
               waits=[m_ea2])
    m_rep = p.t.mark()
    p.v.tensor_copy(c.b_rep[:], c.pQKVR[:], waits=[m_rep])
    m_repv = p.v.mark()
    qnw_rep = c.b_rep[:, 0:128]
    knw_rep = c.b_rep[:, 128:256]
    gnw_rep = c.b_rep[:, 256:384]
    neg_ea = c.b_rep[:, 384:385]
    dtb_rep = c.b_rep[:, 385:386]

    RB, W = 7, 4
    NST = NCH * B
    st = {"pO": None, "pS": None, "pF": None, "pSm": None, "pQK": m_repv,
          "pA2": None, "pT0": None, "pT1": None}
    xr_use = [None] * RB
    st_done = {}
    scan_rd = {}
    pmk = {}

    def prep(i):
        n, b = i // B, i % B
        ri = i % RB
        war = scan_rd.get(i - RB, [])
        rb = (n % 8) * TOK + (b * NSLOT + n // 8) * 128
        w = [m_ag] + list(war)
        if xr_use[ri] is not None:
            w.append(xr_use[ri])
        for dc in range(8):
            p.sd.dma_start_transpose(
                c.b_xT[ri][:, dc * 128:(dc + 1) * 128],
                c.cc_xn_all[rb:rb + 128, dc * 128:(dc + 1) * 128],
                waits=w if dc == 0 else [], chan=f"bx{ri}")
        m_xT = p.cmark(f"bx{ri}")

        w = [m_xT, m_bw]
        if st["pQK"] is not None:
            w.append(st["pQK"])
        for dc in range(8):
            p.t.matmul(c.pQKVR[:], c.b_xT[ri][:, dc * 128:(dc + 1) * 128],
                       c.b_wqkvr[:, dc * 512:(dc + 1) * 512],
                       start=(dc == 0), stop=(dc == 7), waits=w if dc == 0 else [])
        m_proj = p.t.mark()
        w = [m_proj]
        if st["pA2"] is not None:
            w.append(st["pA2"])
        for dc in range(8):
            p.t.matmul(c.pA2[:], c.b_xT[ri][:, dc * 128:(dc + 1) * 128],
                       c.b_wa[:, dc:dc + 1],
                       start=(dc == 0), stop=(dc == 7), waits=w if dc == 0 else [])
        m_aproj = p.t.mark()
        xr_use[ri] = m_aproj

        p.s.activation(c.b_sqkv[ri][:], c.pQKVR[:, 0:384], AF.Silu,
                       waits=[m_proj, m_repv] + list(war))
        p.s.activation(c.b_scr[ri][:], c.pQKVR[:, 384:512], AF.Sigmoid)
        st["pQK"] = p.s.mark()
        pQK_use = st["pQK"]
        p.v.tensor_add(c.b_z[ri][:], c.pA2[:], dtb_rep, waits=[m_aproj])
        st["pA2"] = p.v.mark()
        p.s.activation(c.b_zab[ri][:], c.b_z[ri][:], AF.Abs, waits=[st["pA2"]])
        p.s.activation(c.b_zab[ri][:], c.b_zab[ri][:], AF.Exp, scale=-1.0)
        p.s.activation(c.b_zab[ri][:], c.b_zab[ri][:], AF.Ln, bias=c.cpool[:, 1:2])
        m_sp1 = p.s.mark()
        p.v.tensor_scalar(c.b_z[ri][:], c.b_z[ri][:], 0.0, None, ALU.max, waits=[m_sp1])
        p.v.tensor_add(c.b_z[ri][:], c.b_z[ri][:], c.b_zab[ri][:])
        p.v.tensor_mul(c.b_z[ri][:], c.b_z[ri][:], neg_ea)

        for qk in range(2):
            sl_ = slice(qk * 128, qk * 128 + 128)
            p.v.tensor_mul(c.b_dmp[ri][:], c.b_sqkv[ri][:, sl_], c.b_sqkv[ri][:, sl_],
                           waits=[pQK_use])
            p.v.reduce_sum(c.b_ss[ri][:], c.b_dmp[ri][:], AX.X)
            mss = p.v.mark()
            p.s.activation(c.b_ss[ri][:], c.b_ss[ri][:], AF.Ln, bias=c.cpool[:, 2:3],
                           scale=1.0 / 128, waits=[mss])
            p.s.activation(c.b_ss[ri][:], c.b_ss[ri][:], AF.Exp, scale=-0.5)
            mrs = p.s.mark()
            p.v.scalar_tensor_tensor(
                c.b_qn[ri][:, sl_], c.b_sqkv[ri][:, sl_], c.b_ss[ri][:, 0:1],
                qnw_rep if qk == 0 else knw_rep, ALU.mult, ALU.mult, waits=[mrs])
        m_qn = p.v.mark()

        seq = [c.b_scr[ri], c.b_buf[ri], c.b_mh[ri], c.b_buf[ri], c.b_mh[ri]]
        for r4 in range(4):
            p.v.max(c.b_v8[ri][:], seq[r4][:],
                    waits=[pQK_use] if r4 == 0 else [])
            p.v.match_replace(seq[r4 + 1][:], c.b_v8[ri][:], seq[r4][:], -1.0)
        p.v.tensor_scalar(c.b_mh[ri][:], c.b_mh[ri][:], -1.0, None, ALU.is_equal)
        p.v.tensor_mul(c.b_mh[ri][:], c.b_scr[ri][:], c.b_mh[ri][:])
        p.v.reduce_sum(c.b_ss[ri][:], c.b_mh[ri][:], AX.X)
        p.v.tensor_scalar(c.b_ss[ri][:], c.b_ss[ri][:], 1e-9, None, ALU.add)
        p.v.reciprocal(c.b_ss[ri][:], c.b_ss[ri][:])
        p.v.tensor_scalar(c.b_mh[ri][:], c.b_mh[ri][:], c.b_ss[ri][:, 0:1], None, ALU.mult)
        p.v.tensor_scalar(c.b_f[ri][:], c.b_mh[ri][:], c.b_z[ri][:, 0:1], None, ALU.mult)
        m_f = p.v.mark()

        w = [m_f, m_rows]
        if st["pF"] is not None:
            w.append(st["pF"])
        p.t.matmul(c.pF[:], c.b_ucum[:], c.b_f[ri][:], start=True, stop=True, waits=w)
        m_F = p.t.mark()
        p.s.activation(c.b_E[ri][:], c.pF[:], AF.Exp, waits=[m_F])
        p.s.activation(c.b_Ei[ri][:], c.pF[:], AF.Exp, scale=-1.0)
        st["pF"] = p.s.mark()
        w = [m_f]
        if st["pSm"] is not None:
            w.append(st["pSm"])
        p.t.matmul(c.pSmall[0:1, 0:128], c.ones_col[:], c.b_f[ri][:],
                   start=True, stop=True, waits=w)
        m_fc = p.t.mark()
        p.v.tensor_copy(c.b_fc[ri][:], c.pSmall[0:1, 0:128], waits=[m_fc])
        m_fcv = p.v.mark()
        p.t.transpose(c.pSmall[:, 0:1], c.b_fc[ri][:], c.ident32[0:1, 0:1],
                      waits=[m_fcv])
        m_fct = p.t.mark()
        p.s.activation(c.b_efc[ri][:], c.pSmall[:, 0:1], AF.Exp,
                       waits=[m_fct] + list(war))
        st["pSm"] = p.s.mark()
        m_efc = st["pSm"]

        p.v.tensor_mul(c.b_qt[ri][:], c.b_qn[ri][:, 0:128], c.b_E[ri][:],
                       waits=[st["pF"], m_qn])
        p.s.activation(c.b_ef[ri][:], c.b_f[ri][:], AF.Exp, waits=[m_f])
        m_ef = p.s.mark()
        p.v.tensor_scalar(c.b_ef[ri][:], c.b_ef[ri][:], -1.0, c.cpool[0:128, 1:2],
                          ALU.mult, ALU.add, waits=[m_ef])
        p.v.tensor_mul(c.b_kt[ri][:], c.b_qn[ri][:, 128:256], c.b_ef[ri][:])
        p.v.tensor_mul(c.b_kt[ri][:], c.b_kt[ri][:], c.b_Ei[ri][:])
        m_ktil = p.v.mark()

        w = [m_ktil]
        if st["pT0"] is not None:
            w.append(st["pT0"])
        p.t.transpose(c.pT32b[0][:], c.b_qt[ri][:], c.ident32[:], waits=w)
        mtq = p.t.mark()
        p.v.tensor_copy(c.b_qtT[ri][:], c.pT32b[0][:], waits=[mtq] + list(war))
        st["pT0"] = p.v.mark()
        m_qtT = st["pT0"]
        w = [m_ktil]
        if st["pT1"] is not None:
            w.append(st["pT1"])
        p.t.transpose(c.pT32b[1][:], c.b_kt[ri][:], c.ident32[:], waits=w)
        mtk = p.t.mark()
        p.v.tensor_copy(c.b_ktT[ri][:], c.pT32b[1][:], waits=[mtk])
        st["pT1"] = p.v.mark()

        w = [st["pT0"], st["pT1"], st["pF"]]
        p.t.matmul(c.pF[:], c.b_ktT[ri][:], c.b_qtT[ri][:], start=True, stop=True,
                   waits=w)
        m_at = p.t.mark()
        p.v.tensor_mul(c.b_am[ri][:], c.pF[:], c.b_ucum[:], waits=[m_at] + list(war))
        st["pF"] = p.v.mark()
        pmk[i] = dict(qtT=m_qtT, am=st["pF"], ktil=m_ktil, efc=m_efc, sqkv=pQK_use)

    def scan(i):
        n, b = i // B, i % B
        ri = i % RB
        mk = pmk.pop(i)
        w = [mk["qtT"], mk["am"], mk["sqkv"]]
        if b in st_done:
            w.append(st_done[b])
        if st["pO"] is not None:
            w.append(st["pO"])
        p.t.matmul(c.pO[:], c.b_qtT[ri][:], c.b_S[b][:], start=True, stop=False,
                   waits=w)
        p.t.matmul(c.pO[:], c.b_am[ri][:], c.b_sqkv[ri][:, 256:384],
                   start=False, stop=True)
        m_O = p.t.mark()

        w = [mk["ktil"]]
        if st["pS"] is not None:
            w.append(st["pS"])
        p.t.matmul(c.pS[:], c.b_kt[ri][:], c.b_sqkv[ri][:, 256:384],
                   start=True, stop=True, waits=w)
        m_S = p.t.mark()
        p.v.tensor_add(c.b_S[b][:], c.b_S[b][:], c.pS[:], waits=[m_S, mk["efc"]])
        p.v.tensor_scalar(c.b_S[b][:], c.b_S[b][:], c.b_efc[ri][:, 0:1], None, ALU.mult)
        st_done[b] = p.v.mark()
        st["pS"] = st_done[b]

        p.v.tensor_copy(c.b_o32[:], c.pO[:], waits=[m_O])
        p.v.tensor_mul(c.b_dmp[ri][:], c.b_o32[:], c.b_o32[:])
        p.v.reduce_sum(c.b_oss[:, (b * NCH + n):(b * NCH + n) + 1], c.b_dmp[ri][:], AX.X)
        p.v.tensor_copy(c.b_obf[:, (b * NCH + n) * 128:(b * NCH + n) * 128 + 128],
                        c.b_o32[:])
        st["pO"] = p.v.mark()
        scan_rd[i] = [m_S, st_done[b]]

    for step in range(NST + W):
        if step < NST:
            prep(step)
        if step >= W:
            scan(step - W)

    m_chunks = p.v.mark()
    p.sd.dma_start(out=c.cc_oss_in, in_=c.b_oss[:], waits=[m_chunks], chan="bosst")
    m_oss = p.cmark("bosst")
    p.g.collective_compute("AllReduce", ALU.add, RG8,
                           [c.cc_oss_in.opt()], [c.cc_oss_out.opt()],
                           waits=[m_oss], inc=1, chan="cc")
    m_ar = p.cmark("cc")
    p.sd.dma_start(out=c.b_oss[:], in_=c.cc_oss_out, waits=[m_ar], chan="bossl")
    m_ossl = p.cmark("bossl")
    p.s.activation(c.b_oss[:], c.b_oss[:], AF.Ln, bias=c.cpool[:, 2:3],
                   scale=1.0 / 1024, waits=[m_ossl])
    p.s.activation(c.b_oss[:], c.b_oss[:], AF.Exp, scale=-0.5)
    m_rstd = p.s.mark()
    return m_rstd, gnw_rep, m_bw


def emit_attn_b_out(p, c, m_rstd, gnw_rep, m_bw):
    NCH = T // 128
    ot_use = None
    pOut_use = None
    for n in range(NCH):
        for b in range(B):
            col = b * NCH + n
            rb = (n % 8) * TOK + (b * NSLOT + n // 8) * 128
            p.v.scalar_tensor_tensor(
                c.b_on[:], c.b_obf[:, col * 128: col * 128 + 128],
                c.b_oss[:, col:col + 1], gnw_rep, ALU.mult, ALU.mult,
                waits=[m_rstd])
            m_on = p.v.mark()
            w = [m_on]
            if ot_use is not None:
                w.append(ot_use)
            p.t.transpose(c.pTbf[:], c.b_on[:], c.ident_bf[:], waits=w)
            m_ot = p.t.mark()
            p.v.tensor_copy(c.b_onT[:], c.pTbf[:], waits=[m_ot])
            m_otv = p.v.mark()
            ot_use = m_otv
            w = [m_otv, m_bw]
            if pOut_use is not None:
                w.append(pOut_use)
            for nh in range(2):
                p.t.matmul(c.pWo[nh][:], c.b_onT[:],
                           c.b_wo[:, nh * 512:(nh + 1) * 512],
                           start=True, stop=True, waits=w if nh == 0 else [])
            m_wo = p.t.mark()
            wv = [m_wo]
            if n + b > 0:
                wv.append(p.cmark("brsst"))
            p.v.tensor_copy(c.b_part[:, 0:512], c.pWo[0][:], waits=wv)
            p.v.tensor_copy(c.b_part[:, 512:1024], c.pWo[1][:])
            pOut_use = p.v.mark()
            p.sd.dma_start(out=c.cc_rs_in[rb:rb + 128, :], in_=c.b_part[:],
                           waits=[pOut_use], chan="brsst")
    m_rsst = p.cmark("brsst")
    p.g.collective_compute("ReduceScatter", ALU.add, RG8,
                           [c.cc_rs_in.opt()], [c.cc_rs_out.opt()],
                           waits=[m_rsst], inc=1, chan="cc")
    m_rs = p.cmark("cc")
    xadd_use = None
    for tt in range(NTILE):
        w = [m_rs]
        if xadd_use is not None:
            w.append(xadd_use)
        p.sd.dma_start(out=c.b_stage[:], in_=c.cc_rs_out[tt * 128:(tt + 1) * 128, :],
                       waits=w, chan="brsl")
        m_st = p.cmark("brsl")
        dst = c.x_res[:, tt * 1024:(tt + 1) * 1024]
        p.v.tensor_add(dst, dst, c.b_stage[:], waits=[m_st])
        xadd_use = p.v.mark()


def build_program(phases=ALL_PHASES):
    nc = bass.Bass("TRN2", target_bir_lowering=False)
    es = ExitStack()
    c = Ctx()
    c.nc = nc

    def sbt(name, shape, dtype=F32):
        return es.enter_context(nc.sbuf_tensor(name, shape, dtype))

    def pst(name, shape, dtype=F32):
        return es.enter_context(nc.psum_tensor(name, shape, dtype))

    x_in = nc.dram_tensor("x_c", [TOK, D], F32, kind="ExternalInput").ap()
    y_out = nc.dram_tensor("y", [TOK, D], F32, kind="ExternalOutput").ap()
    c.dbg_aT = nc.dram_tensor("dbg_aT", [128, 1024], BF16, kind="ExternalOutput").ap()
    c.dbg_acc = nc.dram_tensor("dbg_acc", [65, 2048], F32, kind="ExternalOutput").ap()
    c.dbg_kv = nc.dram_tensor("dbg_kv", [256, 2048], BF16, kind="ExternalOutput").ap()
    c.dbg_vr = nc.dram_tensor("dbg_vr", [128, 1056], BF16, kind="ExternalOutput").ap()
    need_ffn = ("ffn1" in phases) or ("ffn2" in phases)
    if "attn_b" in phases:
        c.d_wbqkvr = nc.dram_tensor("wbqkvr", [D, 512], F32, kind="ExternalInput").ap()
        c.d_wba = nc.dram_tensor("wba", [D, 1], F32, kind="ExternalInput").ap()
        c.d_wbo = nc.dram_tensor("wbo", [128, D], F32, kind="ExternalInput").ap()
        c.d_qnw = nc.dram_tensor("qnw", [1, 128], F32, kind="ExternalInput").ap()
        c.d_knw = nc.dram_tensor("knw", [1, 128], F32, kind="ExternalInput").ap()
        c.d_gnw = nc.dram_tensor("gnw", [1, 128], F32, kind="ExternalInput").ap()
        c.d_alog = nc.dram_tensor("alog", [1, 1], F32, kind="ExternalInput").ap()
        c.d_dtb = nc.dram_tensor("dtb", [1, 1], F32, kind="ExternalInput").ap()
        c.cc_xn_in = nc.dram_tensor("cc_xn_in", [TOK, D], BF16, kind="Internal").ap()
        c.cc_xn_all = nc.dram_tensor("cc_xn_all", [NCORE * TOK, D], BF16, kind="Internal").ap()
        c.cc_oss_in = nc.dram_tensor("cc_oss_in", [128, 128], F32, kind="Internal").ap()
        c.cc_oss_out = nc.dram_tensor("cc_oss_out", [128, 128], F32, kind="Internal").ap()
        c.cc_rs_in = nc.dram_tensor("cc_rs_in", [NCORE * TOK, D], BF16, kind="Internal").ap()
        c.cc_rs_out = nc.dram_tensor("cc_rs_out", [TOK, D], BF16, kind="Internal").ap()
        c.ucum_d = nc.inline_tensor(np.triu(np.ones((128, 128), np.float32)), "ucum32")
    if "attn_a" in phases:
        c.d_wqkv = nc.dram_tensor("wqkv", [D, 3 * D], F32, kind="ExternalInput").ap()
        c.d_wout = nc.dram_tensor("wout", [D, D], F32, kind="ExternalInput").ap()
        c.d_abias = nc.dram_tensor("abias", [128, ABW()], F32, kind="ExternalInput").ap()
        c.cc_v_in = nc.dram_tensor("cc_v_in", [TOK, 1024], BF16, kind="Internal").ap()
        c.cc_v_all = nc.dram_tensor("cc_v_all", [NCORE * TOK, 1024], BF16, kind="Internal").ap()
        c.cc_kt_in = nc.dram_tensor("cc_kt_in", [TOK * 8, 128], BF16, kind="Internal").ap()
        c.cc_kt_all = nc.dram_tensor("cc_kt_all", [NCORE * TOK * 8, 128], BF16, kind="Internal").ap()
        c.x1_dram = nc.dram_tensor("x1_dram", [TOK, D], F32, kind="Internal").ap()
    if need_ffn:
        c.d_rw = nc.dram_tensor("rw", [D, FFN_E], F32, kind="ExternalInput").ap()
        c.d_rb = nc.dram_tensor("rb", [1, FFN_E], F32, kind="ExternalInput").ap()
        c.d_w1 = nc.dram_tensor("w1", [FFN_E, D, 2 * FFN_H], F32, kind="ExternalInput").ap()
        c.d_w2 = nc.dram_tensor("w2", [FFN_E, FFN_H, D], F32, kind="ExternalInput").ap()
        c.d_w1sh = nc.dram_tensor("w1sh", [FFN_E, 8, 2, 128, 4096], BF16, kind="Internal").ap()
        c.d_w2sh = nc.dram_tensor("w2sh", [FFN_E, 2, 4, 128, 4096], BF16, kind="Internal").ap()

    id_bf_d = nc.inline_tensor(np.eye(128, dtype=np.float32).astype(ml_dtypes.bfloat16), "id_bf")
    id_32_d = nc.inline_tensor(np.eye(128, dtype=np.float32), "id_32")
    if "attn_a" in phases:
        _pk = np.arange(128, dtype=np.float32)
        _sl = _slopes(HEADS_A)
        diag_b = (_sl[None, :] * _pk[:, None] - SHIFT_A).astype(np.float32)
        diag_b_d = nc.inline_tensor(diag_b, "diag_b")
        um = (np.tril(np.ones((128, 128), np.float32)).T)
        mask256_np = np.concatenate([um, um], axis=1).astype(ml_dtypes.bfloat16)
        mask256_d = nc.inline_tensor(mask256_np, "mask256")
    ones_row_d = nc.inline_tensor(np.ones((1, 128), np.float32), "ones_row_c")
    ones_col_d = nc.inline_tensor(np.ones((128, 1), np.float32), "ones_col_c")

    c.x_res = None
    c.ident_bf = sbt("ident_bf", [128, 128], BF16)
    c.ident32 = sbt("ident32", [128, 128])
    c.ones_row = sbt("ones_row", [1, 128], F32)
    c.ones_col = sbt("ones_col", [128, 1], F32)
    c.cpool = sbt("cpool", [128, 4], F32)
    c.ln_nm = sbt("ln_nm", [128, 1])
    c.ln_ssq = sbt("ln_ssq", [128, 1])
    c.ln_m2 = sbt("ln_m2", [128, 1])
    c.ln_dump = sbt("ln_dump", [128, 1024])
    c.gates = sbt("gates", [128, NTILE * FFN_E])

    p = Prog()
    c.x_in = x_in

    def alloc_x_res(src_dram):
        c.x_res = es.enter_context(nc.sbuf_tensor("x_res", [128, NTILE * 1024], F32))
        for tt in range(NTILE):
            p.sd.dma_start(out=c.x_res[:, tt * 1024:(tt + 1) * 1024],
                           in_=src_dram[tt * 128:(tt + 1) * 128, :], chan="xload")
        p.barrier()

    p.sd.dma_start(out=c.ident_bf[:], in_=id_bf_d.ap())
    p.sd.dma_start(out=c.ident32[:], in_=id_32_d.ap())
    p.sd.dma_start(out=c.ones_col[:], in_=ones_col_d.ap())
    p.sd.dma_start(out=c.ones_row[:], in_=ones_row_d.ap())
    p.v.memset(c.cpool[:, 0:1], EPS_LN)
    p.v.memset(c.cpool[:, 1:2], 1.0)
    p.v.memset(c.cpool[:, 2:3], EPS_B)
    p.v.memset(c.cpool[:, 3:4], 0.0)
    p.barrier()


    phase_scopes = []
    for ph in phases:
        if c.x_res is None and ph != "attn_a":
            alloc_x_res(x_in)
        if ph == "attn_a":
            es2 = ExitStack()
            phase_scopes.append(es2)

            def sA(name, shape, dtype=F32, _es=es2):
                return _es.enter_context(nc.sbuf_tensor(name, shape, dtype))

            c.a_QTz = [sA(f"a_QTz{i}", [128, 8 * TOK], BF16) for i in range(2)]
            c.a_xst = [sA(f"a_xst{i}", [128, 1024]) for i in range(2)]
            c.diag_bias = sA("diag_bias", [128, HEADS_A])
            c.mask256 = sA("mask256s", [128, 256], BF16)
            p.sd.dma_start(out=c.diag_bias[:], in_=diag_b_d.ap())
            p.sd.dma_start(out=c.mask256[:], in_=mask256_d.ap())
            p.barrier()

            es3 = ExitStack()
            c.a_xn = es3.enter_context(nc.sbuf_tensor("a_xn", [128, 1024], BF16))
            c.a_xnT = es3.enter_context(nc.sbuf_tensor("a_xnT", [128, NTILE * 8 * 128], BF16))
            c.a_wbuf = [es3.enter_context(nc.sbuf_tensor(f"a_wbuf{i}", [128, 8 * 512], BF16)) for i in range(2)]
            c.a_ktmp = es3.enter_context(nc.sbuf_tensor("a_ktmp", [128, 512], BF16))
            c.a_ktT = es3.enter_context(nc.sbuf_tensor("a_ktT", [128, 512], BF16))
            c.pT = [es3.enter_context(nc.psum_tensor(f"a_pT{i}", [128, 128], BF16)) for i in range(2)]
            c.pQ = [es3.enter_context(nc.psum_tensor(f"a_pQ{i}", [64, min(512, TOK)])) for i in range(2)]
            c.pKV = es3.enter_context(nc.psum_tensor("a_pKV", [128, 512]))
            m_store = emit_attn_a_proj(p, c)
            es3.close()

            c.a_wout = sA("a_wout", [128, 8 * 1024], BF16)
            c.a_bias = [sA(f"a_bias{i}", [128, GMAXW()]) for i in range(2)]
            c.a_kring = [sA(f"a_kring{i}", [128, 1024], BF16) for i in range(8)]
            c.a_vring = [sA(f"a_vring{i}", [128, 1056], BF16) for i in range(8)]
            c.a_P = [sA(f"a_P{i}", [128, 2048], BF16) for i in range(8)]
            c.a_accO = sA("a_accO", [65, 2048])
            c.a_aT = sA("a_aT", [128, 1024], BF16)
            c.a_recip = sA("a_recip", [1, 128])
            c.a_rb = sA("a_rb", [64, 128])

            es4 = ExitStack()
            c.psPO = [es4.enter_context(nc.psum_tensor(f"a_psPO{i}", [65, 128])) for i in range(2)]
            c.pL = [es4.enter_context(nc.psum_tensor(f"a_pL{i}", [128, 256])) for i in range(2)]
            c.pOut = [es4.enter_context(nc.psum_tensor(f"a_pOut{i}", [128, 512])) for i in range(2)]
            emit_attn_a(p, c, m_store)
            es4.close()
            es2.close()
            p.barrier()
            alloc_x_res(c.x1_dram)
        if ph == "attn_b":
            es2 = ExitStack()
            phase_scopes.append(es2)

            def sB(name, shape, dtype=F32, _es=es2):
                return _es.enter_context(nc.sbuf_tensor(name, shape, dtype))

            RB = 7
            c.b_xn = [sB(f"b_xn{i}", [128, 1024], BF16) for i in range(2)]
            c.b_xT = [sB(f"b_xT{i}", [128, 1024], BF16) for i in range(RB)]
            c.b_wqkvr = sB("b_wqkvr", [128, 8 * 512], BF16)
            c.b_wa = sB("b_wa", [128, 8], BF16)
            c.b_wo = sB("b_wo", [128, 1024], BF16)
            c.b_rows = sB("b_rows", [1, 512])
            c.b_rep = sB("b_rep", [128, 512])
            c.b_ucum = sB("b_ucum", [128, 128])
            c.b_sqkv = [sB(f"b_sqkv{i}", [128, 384]) for i in range(RB)]
            c.b_scr = [sB(f"b_scr{i}", [128, 128]) for i in range(RB)]
            c.b_z = [sB(f"b_z{i}", [128, 1]) for i in range(RB)]
            c.b_zab = [sB(f"b_zab{i}", [128, 1]) for i in range(RB)]
            c.b_ss = [sB(f"b_ss{i}", [128, 1]) for i in range(RB)]
            c.b_qn = [sB(f"b_qn{i}", [128, 256]) for i in range(RB)]
            c.b_buf = [sB("b_buf0", [128, 128])] * RB
            c.b_v8 = [sB("b_v80", [128, 8])] * RB
            c.b_mh = [sB("b_mh0", [128, 128])] * RB
            c.b_f = [sB(f"b_f{i}", [128, 128]) for i in range(RB)]
            c.b_E = [sB(f"b_E{i}", [128, 128]) for i in range(RB)]
            c.b_Ei = [sB(f"b_Ei{i}", [128, 128]) for i in range(RB)]
            c.b_fc = [sB(f"b_fc{i}", [1, 128]) for i in range(RB)]
            c.b_efc = [sB(f"b_efc{i}", [128, 1]) for i in range(RB)]
            c.b_ef = [sB(f"b_ef{i}", [128, 128]) for i in range(RB)]
            c.b_dmp = [sB("b_dmp0", [128, 128])] * RB
            c.b_qt = [sB(f"b_qt{i}", [128, 128]) for i in range(RB)]
            c.b_kt = [sB(f"b_kt{i}", [128, 128]) for i in range(RB)]
            c.b_qtT = [sB(f"b_qtT{i}", [128, 128]) for i in range(RB)]
            c.b_ktT = [sB(f"b_ktT{i}", [128, 128]) for i in range(RB)]
            c.b_am = [sB(f"b_am{i}", [128, 128]) for i in range(RB)]
            c.b_S = [sB(f"b_S{i}", [128, 128]) for i in range(2)]
            c.b_o32 = sB("b_o32", [128, 128])
            c.b_obf = sB("b_obf", [128, 128 * 128], BF16)
            c.b_oss = sB("b_oss", [128, 128])
            c.b_on = sB("b_on", [128, 128], BF16)
            c.b_onT = sB("b_onT", [128, 128], BF16)
            c.b_part = sB("b_part", [128, 1024], BF16)
            c.b_stage = sB("b_stage", [128, 1024], BF16)

            es5 = ExitStack()
            c.pQKVR = es5.enter_context(nc.psum_tensor("b_pQKVR", [128, 512]))
            c.pA2 = es5.enter_context(nc.psum_tensor("b_pA2", [128, 1]))
            c.pF = es5.enter_context(nc.psum_tensor("b_pF", [128, 128]))
            c.pSmall = es5.enter_context(nc.psum_tensor("b_pSmall", [128, 128]))
            c.pT32b = [es5.enter_context(nc.psum_tensor(f"b_pT32{i}", [128, 128])) for i in range(2)]
            c.pO = es5.enter_context(nc.psum_tensor("b_pO", [128, 128]))
            c.pS = es5.enter_context(nc.psum_tensor("b_pS", [128, 128]))
            m_rstd, gnw_rep, m_bw = emit_attn_b(p, c)
            es5.close()
            es6 = ExitStack()
            c.pTbf = es6.enter_context(nc.psum_tensor("b_pTbf", [128, 128], BF16))
            c.pWo = [es6.enter_context(nc.psum_tensor(f"b_pWo{i}", [128, 512])) for i in range(2)]
            emit_attn_b_out(p, c, m_rstd, gnw_rep, m_bw)
            es6.close()
            es2.close()
            p.barrier()
        if ph in ("ffn1", "ffn2"):
            es2 = ExitStack()
            phase_scopes.append(es2)

            def s2(name, shape, dtype=F32, _es=es2, _ph=ph):
                return _es.enter_context(nc.sbuf_tensor(name + "_" + _ph, shape, dtype))

            def p2(name, shape, dtype=F32, _es=es2, _ph=ph):
                return _es.enter_context(nc.psum_tensor(name + "_" + _ph, shape, dtype))

            c.f_xn = s2("f_xn", [128, 1024], BF16)
            c.f_xnT = s2("f_xnT", [128, 8 * NTILE * 128], BF16)
            c.f_actT = s2("f_actT", [128, 32 * 4 * 128], BF16)
            c.f_w1a = [s2(f"f_w1a{i}", [128, 8 * 512], BF16) for i in range(2)]
            c.f_w1g = [s2(f"f_w1g{i}", [128, 8 * 512], BF16) for i in range(2)]
            c.f_w2c = [s2(f"f_w2c{i}", [128, 8 * 512], BF16) for i in range(2)]
            c.f_rw = s2("f_rw", [128, 8 * FFN_E])
            c.f_xn32 = s2("f_xn32", [128, 1024])
            c.f_xnT32 = s2("f_xnT32", [128, 1024])
            c.f_rb_row = s2("f_rb_row", [1, FFN_E])
            c.f_rb_rep = s2("f_rb_rep", [128, FFN_E])
            c.f_rlog = s2("f_rlog", [128, FFN_E])
            c.f_rmax = s2("f_rmax", [128, 1])
            c.f_esum = s2("f_esum", [128, 1])
            c.f_silu = [s2(f"f_silu{i}", [128, 512]) for i in range(2)]
            c.f_mask = s2("f_mask", [128, FFN_E])
            es3 = ExitStack()
            c.pT = [es3.enter_context(nc.psum_tensor(f"pT{i}_" + ph, [128, 128], BF16)) for i in range(2)]
            c.pT32 = [es3.enter_context(nc.psum_tensor(f"pT32{i}_" + ph, [128, 128])) for i in range(2)]
            c.pR8 = es3.enter_context(nc.psum_tensor("pR8_" + ph, [128, FFN_E]))
            m_gates = emit_ffn_prologue(p, c)
            es3.close()
            c.pA = [p2(f"pA{i}", [128, 512]) for i in range(2)]
            c.pG = [p2(f"pG{i}", [128, 512]) for i in range(2)]
            c.pY = [p2(f"pY{i}", [128, 512]) for i in range(4)]
            emit_ffn_experts(p, c, m_gates, grp_tiles=4,
                             img_ready=(ph == "ffn2" and "ffn1" in phases))
            es2.close()
            p.barrier()

    for tt in range(NTILE):
        p.sd.dma_start(out=y_out[tt * 128:(tt + 1) * 128, :],
                       in_=c.x_res[:, tt * 1024:(tt + 1) * 1024], chan="yst")
    p.barrier()

    block = es.enter_context(nc.Block())
    with es:
        p.finalize(block, nc, es)
    return nc


def shard_tokens(x):
    xt = np.ascontiguousarray(x.reshape(B, T // 128, 128, D))
    out = []
    for cid in range(NCORE):
        sl = xt[:, cid::8]
        out.append(np.ascontiguousarray(sl.reshape(TOK, D)))
    return out

def unshard_tokens(parts):
    y = np.empty((B, T // 128, 128, D), np.float32)
    for cid in range(NCORE):
        y[:, cid::8] = parts[cid].reshape(B, NSLOT, 128, D)
    return y.reshape(B, T, D)


def make_abias(cid):
    sl = _slopes(HEADS_A)
    pk = np.arange(128, dtype=np.float32)
    blocks = []
    for tt in range(NTILE):
        i = tt % NSLOT
        g = cid + 8 * i
        gmax = 8 * i + 7
        blk = np.full((128, gmax * 16), NEG, np.float32)
        for j in range(gmax):
            if j < g:
                for h in range(HEADS_A):
                    blk[:, j * 16 + h] = sl[h] * (128.0 * (j - g) + pk) - SHIFT_A
        blocks.append(blk)
    return np.ascontiguousarray(np.concatenate(blocks, axis=1))


_NC_CACHE = {}


def kernel(L0_attn_a_x, L0_attn_a_Wqkv, L0_attn_a_Wout, L0_attn_b_Wq, L0_attn_b_Wk,
           L0_attn_b_Wv, L0_attn_b_Wr, L0_attn_b_Wa, L0_attn_b_A_log, L0_attn_b_dt_bias,
           L0_attn_b_q_norm_w, L0_attn_b_k_norm_w, L0_attn_b_g_norm_w, L0_attn_b_Wo,
           L0_ffn_router_w, L0_ffn_router_b, L0_ffn_w1, L0_ffn_w2):
    phases = ALL_PHASES
    key = phases
    if key not in _NC_CACHE:
        _NC_CACHE[key] = build_program(phases)
    nc = _NC_CACHE[key]
    xs = shard_tokens(np.asarray(L0_attn_a_x, np.float32))
    in_maps = []
    for cid in range(NCORE):
        m = dict(x_c=xs[cid],
                 rw=np.ascontiguousarray(L0_ffn_router_w),
                 rb=np.ascontiguousarray(L0_ffn_router_b).reshape(1, FFN_E),
                 w1=np.ascontiguousarray(L0_ffn_w1),
                 w2=np.ascontiguousarray(L0_ffn_w2),
                 wqkv=np.ascontiguousarray(L0_attn_a_Wqkv),
                 wout=np.ascontiguousarray(L0_attn_a_Wout),
                 abias=make_abias(cid),
                 wbqkvr=np.ascontiguousarray(np.concatenate(
                     [np.asarray(W)[:, cid * 128:(cid + 1) * 128] for W in
                      (L0_attn_b_Wq, L0_attn_b_Wk, L0_attn_b_Wv, L0_attn_b_Wr)],
                     axis=1)),
                 wba=np.ascontiguousarray(np.asarray(L0_attn_b_Wa)[:, cid:cid + 1]),
                 wbo=np.ascontiguousarray(np.asarray(L0_attn_b_Wo)[cid * 128:(cid + 1) * 128, :]),
                 qnw=np.ascontiguousarray(np.asarray(L0_attn_b_q_norm_w).reshape(1, 128)),
                 knw=np.ascontiguousarray(np.asarray(L0_attn_b_k_norm_w).reshape(1, 128)),
                 gnw=np.ascontiguousarray(np.asarray(L0_attn_b_g_norm_w)[cid * 128:(cid + 1) * 128].reshape(1, 128)),
                 alog=np.asarray(L0_attn_b_A_log)[cid].reshape(1, 1).astype(np.float32),
                 dtb=np.asarray(L0_attn_b_dt_bias)[cid].reshape(1, 1).astype(np.float32))
        in_maps.append(m)
    res = run_bass_kernel_spmd(nc, in_maps, core_ids=list(range(NCORE)))
    return unshard_tokens([r["y"] for r in res.results]).astype(np.float32)


# version 09 (final)
# speedup vs baseline: 1.1553x; 1.0034x over previous
import math
import os
import numpy as np
from contextlib import ExitStack

import ml_dtypes
import concourse.bass as bass
import concourse.mybir as mybir
from concourse.bass_utils import run_bass_kernel_spmd
from concourse.bass import IndirectOffsetOnAxis

F32 = mybir.dt.float32
BF16 = mybir.dt.bfloat16
I32 = mybir.dt.int32
AF = mybir.ActivationFunctionType
ALU = mybir.AluOpType
AX = mybir.AxisListType

B, T, D = 2, 8192, 1024
NCORE = 8
NSLOT = 8
NTILE = 2 * NSLOT
TOK = NTILE * 128
SHIFT_A = 20.0
HEADS_A = 16
NEG = -1e30
FFN_E, FFN_H = 8, 4096
EPS_LN = 1e-5
EPS_B = 1e-5
RG8 = [list(range(8))]

ATTN_DBG = 3
A_MAXJJ = 99
A_SLIM = int(os.environ.get("A_SLIM", "0"))
A_SKIP = int(os.environ.get("A_SKIP", "1"))
DBG_TT = 15
ALL_PHASES = ("attn_a", "ffn1", "attn_b", "ffn2")


def _pair_windows():
    sl = _slopes(HEADS_A)
    return [int((97.3 / sl[2 * pr + 1] + 127.0) // 128.0) for pr in range(8)]


def ABW():
    return 2 * sum(8 * i + 7 for i in range(NSLOT)) * 16


def GMAXW():
    return (8 * (NSLOT - 1) + 7) * 16


def _slopes(n):
    start = 2.0 ** (-(8.0 / n))
    return np.asarray([start * start ** i for i in range(n)], dtype=np.float32)


class _EngProxy:
    def __init__(self, prog, name):
        self._p = prog
        self._n = name

    def __getattr__(self, fn):
        def emit(*a, waits=(), inc=None, chan=None, **k):
            return self._p._add(self._n, fn, a, k, waits, inc, chan)
        return emit

    def mark(self):
        return (self._n, self._p.vals[self._n])


class Prog:
    SERIAL = {"v": True, "s": True, "t": False, "sd": False, "g": False}
    STEP = {"v": 1, "s": 1, "t": 1, "sd": 16, "g": 16}

    def __init__(self):
        self.ops = {n: [] for n in self.STEP}
        self.vals = dict.fromkeys(self.STEP, 0)
        self.chan_eng = {n: n for n in self.STEP}
        self.v = _EngProxy(self, "v")
        self.s = _EngProxy(self, "s")
        self.t = _EngProxy(self, "t")
        self.sd = _EngProxy(self, "sd")
        self.g = _EngProxy(self, "g")
        self._max_waited = {}

    def cmark(self, chan):
        return (chan, self.vals.get(chan, 0))

    def _add(self, eng, fn, a, k, waits, inc, chan):
        chan = chan or eng
        if chan not in self.vals:
            self.vals[chan] = 0
        step = self.STEP[eng] if inc is None else inc
        ws = []
        for (pc, pv) in waits:
            if pc == eng and self.SERIAL.get(eng):
                continue
            key = (eng, pc)
            if pv > 0 and pv > self._max_waited.get(key, 0):
                self._max_waited[key] = pv
                ws.append((pc, pv))
        self.ops[eng].append((fn, a, k, ws, step, chan, self.vals[eng] if chan == eng else -1,
                              self.vals[eng]))
        self.vals[chan] += step
        return (chan, self.vals[chan])

    def barrier(self, exclude=("wsh",)):
        snap = {cn: v for cn, v in self.vals.items() if v > 0 and cn not in exclude}
        for e in self.STEP:
            for pc, pv in snap.items():
                key = (e, pc)
                if pv > self._max_waited.get(key, 0):
                    self._max_waited[key] = pv
                    self.ops[e].append(("__wait__", (pc, pv), {}, [], 0, None, -1, 0))

    def finalize(self, block, nc, es):
        sems = {}
        for cn in self.vals:
            sems[cn] = es.enter_context(nc.semaphore("sem_" + cn))
        secmap = {"v": block.vector, "s": block.scalar, "t": block.tensor,
                  "sd": block.sync, "g": block.gpsimd}

        def runner(eng_name):
            def run(e):
                serial = self.SERIAL[eng_name]
                engsem = sems[eng_name]
                for (fn, a, k, ws, step, chan, selfwait_val, _ev) in self.ops[eng_name]:
                    if fn == "__wait__":
                        pc, pv = a
                        e.wait_ge(sems[pc], pv)
                        continue
                    for (pc, pv) in ws:
                        e.wait_ge(sems[pc], pv)
                    r = getattr(e, fn)(*a, **k)
                    if serial and chan == eng_name and selfwait_val > 0:
                        r._wait_ge(engsem, selfwait_val)
                    if step:
                        r.then_inc(sems[chan], step)
            return run
        for n, sec in secmap.items():
            sec(runner(n))


class Ctx:
    pass


def emit_ln_tile(p, c, src_ap, out_ap, extra_dep=()):
    v, s = p.v, p.s
    m = v.tensor_reduce(c.ln_nm[:], src_ap, AX.X, ALU.add, waits=extra_dep)
    v.tensor_scalar(c.ln_nm[:], c.ln_nm[:], -1.0 / D, None, ALU.mult)
    mnm = v.mark()
    s.activation(c.ln_dump[:], src_ap, AF.Square, accum_out=c.ln_ssq[:], waits=[mnm])
    mssq = s.mark()
    v.tensor_mul(c.ln_m2[:], c.ln_nm[:], c.ln_nm[:], waits=[mssq])
    v.tensor_scalar(c.ln_ssq[:], c.ln_ssq[:], 1.0 / D, None, ALU.mult)
    v.tensor_sub(c.ln_ssq[:], c.ln_ssq[:], c.ln_m2[:])
    mvar = v.mark()
    s.activation(c.ln_ssq[:], c.ln_ssq[:], AF.Ln, bias=c.cpool[:, 0:1], scale=1.0, waits=[mvar])
    s.activation(c.ln_ssq[:], c.ln_ssq[:], AF.Exp, scale=-0.5)
    mrstd = s.mark()
    v.tensor_scalar(out_ap, src_ap, c.ln_nm[:, 0:1], c.ln_ssq[:, 0:1],
                    ALU.add, ALU.mult, waits=[mrstd])
    return v.mark()


def emit_transpose8(p, c, src_tile_bf, dst_fn, dep):
    marks = []
    prev_evac = [None, None]
    for dc in range(8):
        bank = dc % 2
        w = [dep]
        if prev_evac[bank] is not None:
            w.append(prev_evac[bank])
        p.t.transpose(c.pT[bank][:], src_tile_bf[:, dc * 128:(dc + 1) * 128],
                      c.ident_bf[:], waits=w)
        mt = p.t.mark()
        p.v.tensor_copy(dst_fn(dc), c.pT[bank][:], waits=[mt])
        prev_evac[bank] = p.v.mark()
        marks.append(prev_evac[bank])
    return marks[-1]


def emit_ffn_prologue(p, c):
    p.sd.dma_start(out=c.f_rw[:].rearrange("p (dc e) -> p dc e", dc=8),
                   in_=c.d_rw.rearrange("(dc p) e -> p dc e", p=128), chan="fpro")
    p.sd.dma_start(out=c.f_rb_row[:], in_=c.d_rb, chan="fpro")
    mrw = p.cmark("fpro")
    mrb = mrw
    p.t.matmul(c.pR8[:], c.ones_row[:], c.f_rb_row[:], start=True, stop=True,
               waits=[mrb])
    mrbb = p.t.mark()
    p.v.tensor_copy(c.f_rb_rep[:], c.pR8[:], waits=[mrbb])
    mrbrep = p.v.mark()

    tp_done = []
    for tt in range(NTILE):
        mln = emit_ln_tile(p, c, c.x_res[:, tt * 1024:(tt + 1) * 1024], c.f_xn[:],
                           extra_dep=())
        p.v.tensor_scalar(c.f_xn32[:], c.x_res[:, tt * 1024:(tt + 1) * 1024],
                          c.ln_nm[:, 0:1], c.ln_ssq[:, 0:1], ALU.add, ALU.mult)
        mln32 = p.v.mark()
        mtp = emit_transpose8(
            p, c, c.f_xn[:],
            lambda dc, tt=tt: c.f_xnT[:, (dc * NTILE + tt) * 128:(dc * NTILE + tt) * 128 + 128],
            mln)
        tp_done.append(mtp)
        prev = [None, None]
        for dc in range(8):
            bk = dc % 2
            w = [mln32]
            if prev[bk] is not None:
                w.append(prev[bk])
            p.t.transpose(c.pT32[bk][:], c.f_xn32[:, dc * 128:(dc + 1) * 128],
                          c.ident32[:], waits=w)
            mtt = p.t.mark()
            p.v.tensor_copy(c.f_xnT32[:, dc * 128:(dc + 1) * 128], c.pT32[bk][:], waits=[mtt])
            prev[bk] = p.v.mark()
        mtp32 = p.v.mark()
        for dc in range(8):
            w = [mtp32, mrw] if dc == 0 else []
            if dc == 0 and tt == 0:
                w.append(mrbrep)
            p.t.matmul(c.pR8[:], c.f_xnT32[:, dc * 128:(dc + 1) * 128],
                       c.f_rw[:, dc * FFN_E:(dc + 1) * FFN_E],
                       start=(dc == 0), stop=(dc == 7), waits=w)
        mrl = p.t.mark()
        p.v.tensor_add(c.f_rlog[:], c.pR8[:], c.f_rb_rep[:], waits=[mrl])
        p.v.reduce_max(c.f_rmax[:], c.f_rlog[:], AX.X)
        p.v.tensor_scalar(c.f_mask[:], c.f_rlog[:], c.f_rmax[:, 0:1], None, ALU.is_equal)
        p.v.tensor_scalar(c.f_rmax[:], c.f_rmax[:], -1.0, None, ALU.mult)
        mrm = p.v.mark()
        p.s.activation(c.f_rlog[:], c.f_rlog[:], AF.Exp, bias=c.f_rmax[:, 0:1],
                       accum_out=c.f_esum[:], waits=[mrm])
        mre = p.s.mark()
        p.v.reciprocal(c.f_esum[:], c.f_esum[:], waits=[mre])
        p.v.tensor_scalar(c.gates[:, tt * FFN_E:(tt + 1) * FFN_E],
                          c.f_mask[:], c.f_esum[:, 0:1], None, ALU.mult)

    return p.v.mark()


def emit_ffn_experts(p, c, m_gates, grp_tiles=4, img_ready=False, tile_store=None):
    NG = NTILE // grp_tiles
    w1_use = [[], []]
    w2_use = [[], []]
    pa_use = [None, None]
    pg_use = [None, None]
    y_use = None
    act_ready = {}
    wb1 = {}
    wb2 = {}

    for grp in range(NG):
        for e in range(FFN_E):
            for j in range(8):
                buf = (e * 8 + j) % 2
                w = list(w1_use[buf])
                if img_ready or (e, j) in wb1:
                    if (e, j) in wb1:
                        w.append(wb1[(e, j)])
                    p.g.dma_start(out=c.f_w1a[buf][:], in_=c.d_w1sh[e, j, 0],
                                  waits=w, chan=f"w1_{buf}")
                    p.g.dma_start(out=c.f_w1g[buf][:], in_=c.d_w1sh[e, j, 1],
                                  chan=f"w1_{buf}")
                    mw1 = p.cmark(f"w1_{buf}")
                    w1_use[buf] = []
                else:
                    p.g.dma_start(out=c.f_w1a[buf][:].rearrange("p (dc n) -> p dc n", dc=8),
                                  in_=c.d_w1[e, :, j * 512:(j + 1) * 512]
                                  .rearrange("(dc p) n -> p dc n", p=128), waits=w,
                                  chan=f"w1_{buf}")
                    p.g.dma_start(out=c.f_w1g[buf][:].rearrange("p (dc n) -> p dc n", dc=8),
                                  in_=c.d_w1[e, :, FFN_H + j * 512:FFN_H + (j + 1) * 512]
                                  .rearrange("(dc p) n -> p dc n", p=128), chan=f"w1_{buf}")
                    mw1 = p.cmark(f"w1_{buf}")
                    p.sd.dma_start(out=c.d_w1sh[e, j, 0], in_=c.f_w1a[buf][:],
                                   waits=[mw1], chan="w1wb")
                    p.sd.dma_start(out=c.d_w1sh[e, j, 1], in_=c.f_w1g[buf][:],
                                   chan="w1wb")
                    wb1[(e, j)] = p.cmark("w1wb")
                    w1_use[buf] = [wb1[(e, j)]]
                for hc in range(4):
                    ch = j * 4 + hc
                    pb = ch % 2
                    w = [mw1]
                    if ch < 2 and grp == 0 and e == 0:
                        w.append(m_gates)
                    if pa_use[pb] is not None:
                        w.append(pa_use[pb])
                    if pg_use[pb] is not None:
                        w.append(pg_use[pb])
                    xs = c.f_xnT
                    base = lambda dc: (dc * NTILE + grp * grp_tiles) * 128
                    for dc in range(8):
                        p.t.matmul(c.pA[pb][:],
                                   c.f_w1a[buf][:, dc * 512 + hc * 128: dc * 512 + hc * 128 + 128],
                                   xs[:, base(dc): base(dc) + grp_tiles * 128],
                                   start=(dc == 0), stop=(dc == 7),
                                   waits=w if dc == 0 else [])
                    for dc in range(8):
                        p.t.matmul(c.pG[pb][:],
                                   c.f_w1g[buf][:, dc * 512 + hc * 128: dc * 512 + hc * 128 + 128],
                                   xs[:, base(dc): base(dc) + grp_tiles * 128],
                                   start=(dc == 0), stop=(dc == 7))
                    mag = p.t.mark()
                    w1_use[buf] = ([wb1[(e, j)]] if (e, j) in wb1 else []) + [mag]
                    p.s.activation(c.f_silu[pb][:], c.pA[pb][:], AF.Silu, waits=[mag])
                    pa_use[pb] = p.s.mark()
                    p.v.tensor_mul(c.f_actT[:, ch * (grp_tiles * 128):(ch + 1) * (grp_tiles * 128)],
                                   c.pG[pb][:], c.f_silu[pb][:], waits=[pa_use[pb]])
                    pg_use[pb] = p.v.mark()
                    act_ready[ch] = pg_use[pb]

            GT = grp_tiles
            for nh in range(2):
                for cb in range(4):
                    buf = (nh * 4 + cb) % 2
                    w = list(w2_use[buf])
                    key = (e, nh, cb)
                    if img_ready or key in wb2:
                        if key in wb2:
                            w.append(wb2[key])
                        p.g.dma_start(out=c.f_w2c[buf][:], in_=c.d_w2sh[e, nh, cb],
                                      waits=w, chan=f"w2_{buf}")
                        mw2 = p.cmark(f"w2_{buf}")
                        wbdep2 = []
                    else:
                        p.g.dma_start(out=c.f_w2c[buf][:].rearrange("p (cc n) -> p cc n", cc=8),
                                      in_=c.d_w2[e, cb * 1024:(cb + 1) * 1024,
                                                 nh * 512:(nh + 1) * 512]
                                      .rearrange("(cc p) n -> p cc n", p=128),
                                      waits=w, chan=f"w2_{buf}")
                        mw2 = p.cmark(f"w2_{buf}")
                        p.sd.dma_start(out=c.d_w2sh[e, nh, cb], in_=c.f_w2c[buf][:],
                                       waits=[mw2], chan="w2wb")
                        wb2[key] = p.cmark("w2wb")
                        wbdep2 = [wb2[key]]
                    for ch8 in range(8):
                        ch = cb * 8 + ch8
                        w = [mw2, act_ready[ch]]
                        if ch == 0 and y_use is not None:
                            w.append(y_use)
                        for ti in range(4):
                            p.t.matmul(c.pY[ti][:],
                                       c.f_actT[:, ch * (GT * 128) + ti * 128: ch * (GT * 128) + ti * 128 + 128],
                                       c.f_w2c[buf][:, ch8 * 512:(ch8 + 1) * 512],
                                       start=(ch == 0), stop=(ch == 31),
                                       waits=w if ti == 0 else [])
                        w2_use[buf] = wbdep2 + [p.t.mark()]
                my = p.t.mark()
                for ti in range(4):
                    tt = grp * grp_tiles + ti
                    p.v.scalar_tensor_tensor(
                        c.x_res[:, tt * 1024 + nh * 512: tt * 1024 + nh * 512 + 512],
                        c.pY[ti][:],
                        c.gates[:, tt * FFN_E + e: tt * FFN_E + e + 1],
                        c.x_res[:, tt * 1024 + nh * 512: tt * 1024 + nh * 512 + 512],
                        ALU.mult, ALU.add, waits=[my])
                y_use = p.v.mark()
            if tile_store is not None and e == FFN_E - 1:
                for ti in range(grp_tiles):
                    tile_store(grp * grp_tiles + ti, y_use)




def emit_attn_a_proj(p, c):
    p.v.memset(c.a_QTz[0][64:128, :], 0.0)
    p.v.memset(c.a_QTz[1][0:64, :], 0.0)
    xst_use = [None, None]
    for tt in range(NTILE):
        xb = tt % 2
        w = [] if xst_use[xb] is None else [xst_use[xb]]
        p.sd.dma_start(out=c.a_xst[xb][:], in_=c.x_in[tt * 128:(tt + 1) * 128, :],
                       waits=w, chan=f"xst{xb}")
        mx = p.cmark(f"xst{xb}")
        mln = emit_ln_tile(p, c, c.a_xst[xb][:], c.a_xn[:], extra_dep=[mx])
        emit_transpose8(
            p, c, c.a_xn[:],
            lambda dc, tt=tt: c.a_xnT[:, (dc * NTILE + tt) * 128:(dc * NTILE + tt) * 128 + 128],
            mln)
        xst_use[xb] = p.s.mark()
    mtp_all = p.v.mark()

    wb_use = [None, None]
    pq_use = [None, None]
    pkv_use = None
    ktmp_use = None
    ktst_use = None
    for cg in range(6):
        buf = cg % 2
        w = []
        if wb_use[buf] is not None:
            w.append(wb_use[buf])
        p.g.dma_start(out=c.a_wbuf[buf][:].rearrange("p (dc n) -> p dc n", dc=8),
                      in_=c.d_wqkv[:, cg * 512:(cg + 1) * 512]
                      .rearrange("(dc p) n -> p dc n", p=128), waits=w, chan=f"wq{buf}")
        mw = p.cmark(f"wq{buf}")
        if cg < 2:
            QW = min(512, TOK)
            NQG = TOK // QW
            for hh in range(8):
                h = cg * 8 + hh
                pr, odd = h // 2, h % 2
                for tg in range(NQG):
                    pb = (hh * NQG + tg) % 2
                    w = [mw]
                    if cg == 0 and hh == 0 and tg == 0:
                        w.append(mtp_all)
                    if pq_use[pb] is not None:
                        w.append(pq_use[pb])
                    for dc in range(8):
                        p.t.matmul(c.pQ[pb][:],
                                   c.a_wbuf[buf][:, dc * 512 + hh * 64: dc * 512 + hh * 64 + 64],
                                   c.a_xnT[:, dc * NTILE * 128 + tg * QW: dc * NTILE * 128 + tg * QW + QW],
                                   start=(dc == 0), stop=(dc == 7),
                                   waits=w if dc == 0 else [])
                    mq = p.t.mark()
                    wb_use[buf] = mq
                    dst = c.a_QTz[odd][odd * 64:(odd + 1) * 64,
                                       pr * TOK + tg * QW: pr * TOK + tg * QW + QW]
                    wq2 = [mq]
                    p.v.tensor_scalar(dst, c.pQ[pb][:], 0.125, None, ALU.mult, waits=wq2)
                    pq_use[pb] = p.v.mark()
        else:
            for tt in range(NTILE):
                w = [mw]
                if cg == 2 and tt == 0:
                    w.append(mtp_all)
                if pkv_use is not None:
                    w.append(pkv_use)
                for dc in range(8):
                    p.t.matmul(c.pKV[:],
                               c.a_xnT[:, (dc * NTILE + tt) * 128:(dc * NTILE + tt) * 128 + 128],
                               c.a_wbuf[buf][:, dc * 512:(dc + 1) * 512],
                               start=(dc == 0), stop=(dc == 7),
                               waits=w if dc == 0 else [])
                mkv = p.t.mark()
                wb_use[buf] = mkv
                ww = [mkv]
                if ktmp_use is not None:
                    ww.append(ktmp_use)
                p.v.tensor_copy(c.a_ktmp[:], c.pKV[:], waits=ww)
                pkv_use = p.v.mark()
                if cg >= 4:
                    p.sd.dma_start(out=c.cc_v_in[tt * 128:(tt + 1) * 128,
                                                 (cg - 4) * 512:(cg - 3) * 512],
                                   in_=c.a_ktmp[:], waits=[pkv_use], chan="kvst")
                    ktmp_use = p.cmark("kvst")
                else:
                    prevq = [None, None]
                    for qq in range(4):
                        wq = [pkv_use]
                        if prevq[qq % 2] is not None:
                            wq.append(prevq[qq % 2])
                        p.t.transpose(c.pT[qq % 2][:],
                                      c.a_ktmp[:, qq * 128:(qq + 1) * 128],
                                      c.ident_bf[:], waits=wq)
                        mtt2 = p.t.mark()
                        p.v.tensor_copy(c.a_ktT[:, qq * 128:(qq + 1) * 128],
                                        c.pT[qq % 2][:], waits=[mtt2])
                        prevq[qq % 2] = p.v.mark()
                    mktT = p.v.mark()
                    ktmp_use = mktT
                    wst = [mktT]
                    if ktst_use is not None:
                        wst.append(ktst_use)
                    p.sd.dma_start(
                        out=c.cc_kt_in[tt * 1024 + (cg - 2) * 512:
                                       tt * 1024 + (cg - 2) * 512 + 512, :]
                        .rearrange("(pr d) kp -> d pr kp", d=128),
                        in_=c.a_ktT[:].rearrange("p (pr kp) -> p pr kp", pr=4),
                        waits=wst, chan="ktst")
                    ktst_use = p.cmark("ktst")
    c.m_proj_s = p.s.mark()
    return (p.cmark("kvst"), p.cmark("ktst"))


def emit_attn_a(p, c, m_store):
    NR = 8
    m_store_kv, m_store_kt = m_store
    xst_use = [c.m_proj_s, c.m_proj_s]
    p.g.dma_start(out=c.a_wout[:].rearrange("p (dc n) -> p dc n", dc=8),
                  in_=c.d_wout.rearrange("(dc p) n -> p dc n", p=128), chan="wout",
                  waits=[p.t.mark(), p.v.mark(), c.m_proj_s, m_store_kv, m_store_kt])
    m_wout = p.cmark("wout")
    p.g.collective_compute("AllGather", ALU.bypass, RG8,
                           [c.cc_v_in.opt()], [c.cc_v_all.opt()],
                           waits=[m_store_kv], inc=1, chan="cc")
    p.g.collective_compute("AllGather", ALU.bypass, RG8,
                           [c.cc_kt_in.opt()], [c.cc_kt_all.opt()],
                           waits=[m_store_kt], inc=1, chan="cc")
    m_ag = p.cmark("cc")

    for r in range(NR):
        p.v.memset(c.a_vring[r].rearrange("p (k s) -> p k s", s=66)[:, :, 64:65], 1.0,
                   waits=[m_store_kv, m_store_kt] if r == 0 else [])
    m_ones = p.v.mark()

    m_proj_t = p.t.mark()
    ring_use = [m_proj_t] * NR
    bias_use = [[m_store_kv, m_store_kt], [m_store_kv, m_store_kt]]
    pl_use = [None, None]
    psb_use = {}
    psb_ready = {}
    po_flush = {}
    pb2 = [0]
    pout_use = None
    abias_off = 0
    pb = 0

    if ATTN_DBG < 1:
        return
    CH = 4
    acc_first = {}
    accO_use = None
    A_WIN = _pair_windows()
    for tt in range(NTILE):
        bnum, i = tt // NSLOT, tt % NSLOT
        gmax = 8 * i + 7

        def active(jj, pr):
            if not A_SKIP:
                return True
            return jj == 0 or (8 * i - (jj - 1)) <= A_WIN[pr]

        def prmin_of(jj):
            for pr in range(8):
                if active(jj, pr):
                    return pr
            return 8
        bb = tt % 2
        bu = bias_use[bb]
        w = [] if bu is None else (list(bu) if isinstance(bu, list) else [bu])
        p.sd.dma_start(out=c.a_bias[bb][:, 0:gmax * 16],
                       in_=c.d_abias[:, abias_off:abias_off + gmax * 16], waits=w,
                       chan=f"ab{bb}")
        m_bias = p.cmark(f"ab{bb}")
        abias_off += gmax * 16

        njj = min(gmax + 1, A_MAXJJ)
        nch = (njj + CH - 1) // CH
        for chi in range(nch):
            jlo, jhi = chi * CH, min(njj, chi * CH + CH)
            for jj in range(jlo, jhi):
                r = jj % NR
                sl = jj % (2 * CH)
                if jj == 0:
                    vbuf, vrow = c.cc_v_in, tt * 128
                    ktbuf, ktrow = c.cc_kt_in, tt * 1024
                else:
                    j = jj - 1
                    vbuf = c.cc_v_all
                    vrow = (j % 8) * TOK + (bnum * NSLOT + j // 8) * 128
                    ktbuf = c.cc_kt_all
                    ktrow = (j % 8) * TOK * 8 + (bnum * NSLOT + j // 8) * 1024
                w = [] if ring_use[r] is None else [ring_use[r]]
                if jj == 1:
                    w.append(m_ag)
                if jj == 0:
                    w.append(m_store_kv)
                    w.append(m_store_kt)
                pm = prmin_of(jj)
                pml = pm if A_SLIM else 0
                p.sd.dma_start(
                    out=c.a_kring[r][:, pml * 128:].rearrange("p (pr kp) -> p pr kp", pr=8 - pml),
                    in_=ktbuf[ktrow + pml * 128:ktrow + 1024, :]
                    .rearrange("(pr d) kp -> d pr kp", d=128),
                    waits=w, chan=f"kv{r}")
                for hh in range(2):
                    p.g.dma_start(
                        out=c.a_vring[r].rearrange("p (k m) -> p k m", m=132)[:, pml:8, hh * 66:hh * 66 + 64],
                        in_=vbuf[vrow:vrow + 128, :]
                        .rearrange("p (k n) -> p k n", n=128)[:, pml:8, hh * 64:hh * 64 + 64],
                        waits=w if hh == 0 else [], chan=f"kv{r}")
                m_ring = p.cmark(f"kv{r}")
                m_ring_w = [m_ring, m_ones] if jj == 0 else [m_ring]

                if ATTN_DBG < 2.2:
                    ring_use[r] = m_ring
                    continue
                for pri, pr in enumerate(range(pm, 8)):
                    w1 = list(m_ring_w) if pri == 0 else []
                    if pl_use[pb] is not None:
                        w1.append(pl_use[pb])
                    for odd in range(2):
                        p.t.matmul(c.pL[pb][:, odd * 128:(odd + 1) * 128],
                                   c.a_kring[r][:, pr * 128:(pr + 1) * 128],
                                   c.a_QTz[odd][:, pr * TOK + tt * 128: pr * TOK + tt * 128 + 128],
                                   start=True, stop=True,
                                   waits=w1 if odd == 0 else [])
                    m1 = p.t.mark()
                    ring_use[r] = m1
                    if ATTN_DBG < 2.3:
                        pb ^= 1
                        continue
                    ws = [m1]
                    if jj > 0:
                        ws.append(m_bias)
                    mpsb = psb_use.get((sl, pr))
                    if mpsb is not None:
                        ws.append(mpsb)
                    dst = c.a_P[sl]
                    for odd in range(2):
                        h = pr * 2 + odd
                        bias_ap = (c.diag_bias[:, h:h + 1] if jj == 0
                                   else c.a_bias[bb][:, (jj - 1) * 16 + h:(jj - 1) * 16 + h + 1])
                        p.s.activation(dst[:, pr * 256 + odd * 128: pr * 256 + odd * 128 + 128],
                                       c.pL[pb][:, odd * 128:(odd + 1) * 128],
                                       AF.Exp, bias=bias_ap,
                                       waits=ws if odd == 0 else [])
                    mp = p.s.mark()
                    pl_use[pb] = mp
                    if jj == 0 and ATTN_DBG >= 2.4:
                        p.v.tensor_mul(dst[:, pr * 256:(pr + 1) * 256],
                                       dst[:, pr * 256:(pr + 1) * 256],
                                       c.mask256[:], waits=[mp])
                    pb ^= 1
                psb_ready[jj % (2 * CH)] = (p.s.mark(), p.v.mark())

            if ATTN_DBG < 2.5:
                continue
            for pr in range(8):
                jlist = [jj for jj in range(jlo, jhi) if active(jj, pr)]
                if not jlist:
                    continue
                for odd in range(2):
                    pb2i = pb2[0]
                    first = True
                    for jj in jlist:
                        r = jj % NR
                        sl = jj % (2 * CH)
                        w2 = []
                        if first:
                            w2 += list(psb_ready[(jhi - 1) % (2 * CH)])
                            mflush = po_flush.get(pb2i)
                            if mflush is not None:
                                w2.append(mflush)
                        p.t.matmul(c.psPO[pb2i][:],
                                   c.a_vring[r][:, pr * 132 + odd * 66: pr * 132 + odd * 66 + 65],
                                   c.a_P[sl][:, pr * 256 + odd * 128: pr * 256 + odd * 128 + 128],
                                   start=first, stop=(jj == jlist[-1]),
                                   waits=w2)
                        first = False
                        psb_use[(sl, pr)] = p.t.mark()
                    m2 = p.t.mark()
                    for jj in jlist:
                        ring_use[jj % NR] = m2
                    aslice = c.a_accO[0:65, (pr * 2 + odd) * 128:(pr * 2 + odd) * 128 + 128]
                    wv = [m2]
                    if chi == 0 and accO_use is not None:
                        wv.append(accO_use)
                    if chi == 0:
                        p.v.tensor_copy(aslice, c.psPO[pb2i][:], waits=wv)
                    else:
                        p.v.tensor_add(aslice, aslice, c.psPO[pb2i][:], waits=wv)
                    po_flush[pb2i] = p.v.mark()
                    pb2[0] ^= 1
        m_attn = p.v.mark()
        if ATTN_DBG < 2.95:
            continue

        for pr in range(8):
            for odd in range(2):
                cslice = slice((pr * 2 + odd) * 128, (pr * 2 + odd) * 128 + 128)
                p.v.reciprocal(c.a_recip[:], c.a_accO[64:65, cslice], waits=[m_attn])
                mrc = p.v.mark()
                w = [mrc]
                if pl_use[0] is not None:
                    w.append(pl_use[0])
                p.t.matmul(c.pL[0][0:64, 0:128], c.ones_row[:, 0:64], c.a_recip[:],
                           start=True, stop=True, waits=w)
                mrb = p.t.mark()
                p.s.copy(c.a_rb[:], c.pL[0][0:64, 0:128], waits=[mrb])
                mrbe = p.s.mark()
                pl_use[0] = mrbe
                p.v.tensor_mul(c.a_aT[odd * 64:(odd + 1) * 64, pr * 128:(pr + 1) * 128],
                               c.a_accO[0:64, cslice], c.a_rb[:], waits=[mrbe])
        accO_use = p.v.mark()
        m_norm = p.v.mark()
        bias_use[bb] = p.s.mark()

        if ATTN_DBG < 2.98:
            continue
        xb = tt % 2
        w = [] if xst_use[xb] is None else [xst_use[xb]]
        p.sd.dma_start(out=c.a_xst[xb][:], in_=c.x_in[tt * 128:(tt + 1) * 128, :],
                       waits=w, chan=f"xst{xb}")
        mx = p.cmark(f"xst{xb}")
        for nh in range(2):
            first = True
            for pr in range(8):
                w = [m_norm, m_wout] if first else []
                if first and pout_use is not None:
                    w.append(pout_use)
                p.t.matmul(c.pOut[nh][:],
                           c.a_aT[:, pr * 128:(pr + 1) * 128],
                           c.a_wout[:, pr * 1024 + nh * 512: pr * 1024 + nh * 512 + 512],
                           start=first, stop=(pr == 7), waits=w)
                first = False
            mo = p.t.mark()
            dst = c.a_xst[xb][:, nh * 512:(nh + 1) * 512]
            p.v.tensor_add(dst, dst, c.pOut[nh][:], waits=[mo, mx])
            pout_use = p.v.mark()
        p.sd.dma_start(out=c.x1_dram[tt * 128:(tt + 1) * 128, :], in_=c.a_xst[xb][:],
                       waits=[pout_use], chan="x1st")
        xst_use[xb] = p.cmark("x1st")
        if tt == DBG_TT:
            p.sd.dma_start(out=c.dbg_aT, in_=c.a_aT[:], waits=[m_norm], chan="dbg")
            p.sd.dma_start(out=c.dbg_acc, in_=c.a_accO[:], waits=[m_attn], chan="dbg")
            p.sd.dma_start(out=c.dbg_kv[:, 0:1024], in_=c.cc_v_in[TOK - 256:TOK, :], waits=[m_store_kv], chan="dbg")
            p.sd.dma_start(out=c.dbg_vr, in_=c.a_vring[0][:], waits=[m_attn], chan="dbg")




def emit_attn_b(p, c):
    NCH = T // 128
    p.v.memset(c.b_S[0][:], 0.0)
    p.v.memset(c.b_S[1][:], 0.0)
    xnst = [None, None]
    for tt in range(NTILE):
        xb = tt % 2
        dep = [] if xnst[xb] is None else [xnst[xb]]
        mln = emit_ln_tile(p, c, c.x_res[:, tt * 1024:(tt + 1) * 1024], c.b_xn[xb][:],
                           extra_dep=dep)
        p.sd.dma_start(out=c.cc_xn_in[tt * 128:(tt + 1) * 128, :], in_=c.b_xn[xb][:],
                       waits=[mln], chan="bxnst")
        xnst[xb] = p.cmark("bxnst")
    m_xnst = p.cmark("bxnst")
    p.g.collective_compute("AllGather", ALU.bypass, RG8,
                           [c.cc_xn_in.opt()], [c.cc_xn_all.opt()],
                           waits=[m_xnst], inc=1, chan="cc")
    m_ag = p.cmark("cc")

    p.g.dma_start(out=c.b_wqkvr[:].rearrange("p (dc n) -> p dc n", dc=8),
                  in_=c.d_wbqkvr.rearrange("(dc p) n -> p dc n", p=128), chan="bw")
    p.g.dma_start(out=c.b_wa[:].rearrange("p (dc n) -> p dc n", dc=8),
                  in_=c.d_wba.rearrange("(dc p) n -> p dc n", p=128), chan="bw",
                  allow_slow_non_contiguous=True)
    p.g.dma_start(out=c.b_wo[:].rearrange("p (dc n) -> p dc n", dc=8),
                  in_=c.d_wbo.rearrange("(dc p) n -> p dc n", p=128), chan="bw")
    m_bw = p.cmark("bw")
    p.sd.dma_start(out=c.b_rows[0:1, 0:128], in_=c.d_qnw, chan="brow")
    p.sd.dma_start(out=c.b_rows[0:1, 128:256], in_=c.d_knw, chan="brow")
    p.sd.dma_start(out=c.b_rows[0:1, 256:384], in_=c.d_gnw, chan="brow")
    p.sd.dma_start(out=c.b_rows[0:1, 384:385], in_=c.d_alog, chan="brow")
    p.sd.dma_start(out=c.b_rows[0:1, 385:386], in_=c.d_dtb, chan="brow")
    p.sd.dma_start(out=c.b_ucum[:], in_=c.ucum_d.ap(), chan="brow")
    m_rows = p.cmark("brow")
    p.s.activation(c.b_rows[0:1, 384:385], c.b_rows[0:1, 384:385], AF.Exp,
                   waits=[m_rows])
    m_ea = p.s.mark()
    p.v.tensor_scalar(c.b_rows[0:1, 384:385], c.b_rows[0:1, 384:385], -1.0, None,
                      ALU.mult, waits=[m_ea])
    m_ea2 = p.v.mark()
    p.t.matmul(c.pQKVR[:], c.ones_row[:], c.b_rows[0:1, :], start=True, stop=True,
               waits=[m_ea2])
    m_rep = p.t.mark()
    p.v.tensor_copy(c.b_rep[:], c.pQKVR[:], waits=[m_rep])
    m_repv = p.v.mark()
    qnw_rep = c.b_rep[:, 0:128]
    knw_rep = c.b_rep[:, 128:256]
    gnw_rep = c.b_rep[:, 256:384]
    neg_ea = c.b_rep[:, 384:385]
    dtb_rep = c.b_rep[:, 385:386]

    RB, W = 7, 4
    NST = NCH * B
    st = {"pO": None, "pS": None, "pF": None, "pSm": None, "pQK": m_repv,
          "pA2": None, "pT0": None, "pT1": None}
    xr_use = [None] * RB
    st_done = {}
    scan_rd = {}
    pmk = {}

    def prep(i):
        n, b = i // B, i % B
        ri = i % RB
        war = scan_rd.get(i - RB, [])
        rb = (n % 8) * TOK + (b * NSLOT + n // 8) * 128
        w = [m_ag] + list(war)
        if xr_use[ri] is not None:
            w.append(xr_use[ri])
        for dc in range(8):
            p.sd.dma_start_transpose(
                c.b_xT[ri][:, dc * 128:(dc + 1) * 128],
                c.cc_xn_all[rb:rb + 128, dc * 128:(dc + 1) * 128],
                waits=w if dc == 0 else [], chan=f"bx{ri}")
        m_xT = p.cmark(f"bx{ri}")

        w = [m_xT, m_bw]
        if st["pQK"] is not None:
            w.append(st["pQK"])
        for dc in range(8):
            p.t.matmul(c.pQKVR[:], c.b_xT[ri][:, dc * 128:(dc + 1) * 128],
                       c.b_wqkvr[:, dc * 512:(dc + 1) * 512],
                       start=(dc == 0), stop=(dc == 7), waits=w if dc == 0 else [])
        m_proj = p.t.mark()
        w = [m_proj]
        if st["pA2"] is not None:
            w.append(st["pA2"])
        for dc in range(8):
            p.t.matmul(c.pA2[:], c.b_xT[ri][:, dc * 128:(dc + 1) * 128],
                       c.b_wa[:, dc:dc + 1],
                       start=(dc == 0), stop=(dc == 7), waits=w if dc == 0 else [])
        m_aproj = p.t.mark()
        xr_use[ri] = m_aproj

        p.s.activation(c.b_sqkv[ri][:], c.pQKVR[:, 0:384], AF.Silu,
                       waits=[m_proj, m_repv] + list(war))
        p.s.activation(c.b_scr[ri][:], c.pQKVR[:, 384:512], AF.Sigmoid)
        st["pQK"] = p.s.mark()
        pQK_use = st["pQK"]
        p.v.tensor_add(c.b_z[ri][:], c.pA2[:], dtb_rep, waits=[m_aproj])
        st["pA2"] = p.v.mark()
        p.s.activation(c.b_zab[ri][:], c.b_z[ri][:], AF.Abs, waits=[st["pA2"]])
        p.s.activation(c.b_zab[ri][:], c.b_zab[ri][:], AF.Exp, scale=-1.0)
        p.s.activation(c.b_zab[ri][:], c.b_zab[ri][:], AF.Ln, bias=c.cpool[:, 1:2])
        m_sp1 = p.s.mark()
        p.v.tensor_scalar(c.b_z[ri][:], c.b_z[ri][:], 0.0, None, ALU.max, waits=[m_sp1])
        p.v.tensor_add(c.b_z[ri][:], c.b_z[ri][:], c.b_zab[ri][:])
        p.v.tensor_mul(c.b_z[ri][:], c.b_z[ri][:], neg_ea)

        for qk in range(2):
            sl_ = slice(qk * 128, qk * 128 + 128)
            p.v.tensor_mul(c.b_dmp[ri][:], c.b_sqkv[ri][:, sl_], c.b_sqkv[ri][:, sl_],
                           waits=[pQK_use])
            p.v.reduce_sum(c.b_ss[ri][:], c.b_dmp[ri][:], AX.X)
            mss = p.v.mark()
            p.s.activation(c.b_ss[ri][:], c.b_ss[ri][:], AF.Ln, bias=c.cpool[:, 2:3],
                           scale=1.0 / 128, waits=[mss])
            p.s.activation(c.b_ss[ri][:], c.b_ss[ri][:], AF.Exp, scale=-0.5)
            mrs = p.s.mark()
            p.v.scalar_tensor_tensor(
                c.b_qn[ri][:, sl_], c.b_sqkv[ri][:, sl_], c.b_ss[ri][:, 0:1],
                qnw_rep if qk == 0 else knw_rep, ALU.mult, ALU.mult, waits=[mrs])
        m_qn = p.v.mark()

        seq = [c.b_scr[ri], c.b_buf[ri], c.b_mh[ri], c.b_buf[ri], c.b_mh[ri]]
        for r4 in range(4):
            p.v.max(c.b_v8[ri][:], seq[r4][:],
                    waits=[pQK_use] if r4 == 0 else [])
            p.v.match_replace(seq[r4 + 1][:], c.b_v8[ri][:], seq[r4][:], -1.0)
        p.v.tensor_scalar(c.b_mh[ri][:], c.b_mh[ri][:], -1.0, None, ALU.is_equal)
        p.v.tensor_mul(c.b_mh[ri][:], c.b_scr[ri][:], c.b_mh[ri][:])
        p.v.reduce_sum(c.b_ss[ri][:], c.b_mh[ri][:], AX.X)
        p.v.tensor_scalar(c.b_ss[ri][:], c.b_ss[ri][:], 1e-9, None, ALU.add)
        p.v.reciprocal(c.b_ss[ri][:], c.b_ss[ri][:])
        p.v.tensor_scalar(c.b_mh[ri][:], c.b_mh[ri][:], c.b_ss[ri][:, 0:1], None, ALU.mult)
        p.v.tensor_scalar(c.b_f[ri][:], c.b_mh[ri][:], c.b_z[ri][:, 0:1], None, ALU.mult)
        m_f = p.v.mark()

        w = [m_f, m_rows]
        if st["pF"] is not None:
            w.append(st["pF"])
        p.t.matmul(c.pF[:], c.b_ucum[:], c.b_f[ri][:], start=True, stop=True, waits=w)
        m_F = p.t.mark()
        p.s.activation(c.b_E[ri][:], c.pF[:], AF.Exp, waits=[m_F])
        p.s.activation(c.b_Ei[ri][:], c.pF[:], AF.Exp, scale=-1.0)
        st["pF"] = p.s.mark()
        w = [m_f]
        if st["pSm"] is not None:
            w.append(st["pSm"])
        p.t.matmul(c.pSmall[0:1, 0:128], c.ones_col[:], c.b_f[ri][:],
                   start=True, stop=True, waits=w)
        m_fc = p.t.mark()
        p.v.tensor_copy(c.b_fc[ri][:], c.pSmall[0:1, 0:128], waits=[m_fc])
        m_fcv = p.v.mark()
        p.t.transpose(c.pSmall[:, 0:1], c.b_fc[ri][:], c.ident32[0:1, 0:1],
                      waits=[m_fcv])
        m_fct = p.t.mark()
        p.s.activation(c.b_efc[ri][:], c.pSmall[:, 0:1], AF.Exp,
                       waits=[m_fct] + list(war))
        st["pSm"] = p.s.mark()
        m_efc = st["pSm"]

        p.v.tensor_mul(c.b_qt[ri][:], c.b_qn[ri][:, 0:128], c.b_E[ri][:],
                       waits=[st["pF"], m_qn])
        p.s.activation(c.b_ef[ri][:], c.b_f[ri][:], AF.Exp, waits=[m_f])
        m_ef = p.s.mark()
        p.v.tensor_scalar(c.b_ef[ri][:], c.b_ef[ri][:], -1.0, c.cpool[0:128, 1:2],
                          ALU.mult, ALU.add, waits=[m_ef])
        p.v.tensor_mul(c.b_kt[ri][:], c.b_qn[ri][:, 128:256], c.b_ef[ri][:])
        p.v.tensor_mul(c.b_kt[ri][:], c.b_kt[ri][:], c.b_Ei[ri][:])
        m_ktil = p.v.mark()

        w = [m_ktil]
        if st["pT0"] is not None:
            w.append(st["pT0"])
        p.t.transpose(c.pT32b[0][:], c.b_qt[ri][:], c.ident32[:], waits=w)
        mtq = p.t.mark()
        p.v.tensor_copy(c.b_qtT[ri][:], c.pT32b[0][:], waits=[mtq] + list(war))
        st["pT0"] = p.v.mark()
        m_qtT = st["pT0"]
        w = [m_ktil]
        if st["pT1"] is not None:
            w.append(st["pT1"])
        p.t.transpose(c.pT32b[1][:], c.b_kt[ri][:], c.ident32[:], waits=w)
        mtk = p.t.mark()
        p.v.tensor_copy(c.b_ktT[ri][:], c.pT32b[1][:], waits=[mtk])
        st["pT1"] = p.v.mark()

        w = [st["pT0"], st["pT1"], st["pF"]]
        p.t.matmul(c.pF[:], c.b_ktT[ri][:], c.b_qtT[ri][:], start=True, stop=True,
                   waits=w)
        m_at = p.t.mark()
        p.v.tensor_mul(c.b_am[ri][:], c.pF[:], c.b_ucum[:], waits=[m_at] + list(war))
        st["pF"] = p.v.mark()
        pmk[i] = dict(qtT=m_qtT, am=st["pF"], ktil=m_ktil, efc=m_efc, sqkv=pQK_use)

    def scan(i):
        n, b = i // B, i % B
        ri = i % RB
        mk = pmk.pop(i)
        w = [mk["qtT"], mk["am"], mk["sqkv"]]
        if b in st_done:
            w.append(st_done[b])
        if st["pO"] is not None:
            w.append(st["pO"])
        p.t.matmul(c.pO[:], c.b_qtT[ri][:], c.b_S[b][:], start=True, stop=False,
                   waits=w)
        p.t.matmul(c.pO[:], c.b_am[ri][:], c.b_sqkv[ri][:, 256:384],
                   start=False, stop=True)
        m_O = p.t.mark()

        w = [mk["ktil"]]
        if st["pS"] is not None:
            w.append(st["pS"])
        p.t.matmul(c.pS[:], c.b_kt[ri][:], c.b_sqkv[ri][:, 256:384],
                   start=True, stop=True, waits=w)
        m_S = p.t.mark()
        p.v.tensor_add(c.b_S[b][:], c.b_S[b][:], c.pS[:], waits=[m_S, mk["efc"]])
        p.v.tensor_scalar(c.b_S[b][:], c.b_S[b][:], c.b_efc[ri][:, 0:1], None, ALU.mult)
        st_done[b] = p.v.mark()
        st["pS"] = st_done[b]

        p.v.tensor_copy(c.b_o32[:], c.pO[:], waits=[m_O])
        p.v.tensor_mul(c.b_dmp[ri][:], c.b_o32[:], c.b_o32[:])
        p.v.reduce_sum(c.b_oss[:, (b * NCH + n):(b * NCH + n) + 1], c.b_dmp[ri][:], AX.X)
        p.v.tensor_copy(c.b_obf[:, (b * NCH + n) * 128:(b * NCH + n) * 128 + 128],
                        c.b_o32[:])
        st["pO"] = p.v.mark()
        scan_rd[i] = [m_S, st_done[b]]

    for step in range(NST + W):
        if step < NST:
            prep(step)
        if step >= W:
            scan(step - W)

    m_chunks = p.v.mark()
    p.sd.dma_start(out=c.cc_oss_in, in_=c.b_oss[:], waits=[m_chunks], chan="bosst")
    m_oss = p.cmark("bosst")
    p.g.collective_compute("AllReduce", ALU.add, RG8,
                           [c.cc_oss_in.opt()], [c.cc_oss_out.opt()],
                           waits=[m_oss], inc=1, chan="cc")
    m_ar = p.cmark("cc")
    p.sd.dma_start(out=c.b_oss[:], in_=c.cc_oss_out, waits=[m_ar], chan="bossl")
    m_ossl = p.cmark("bossl")
    p.s.activation(c.b_oss[:], c.b_oss[:], AF.Ln, bias=c.cpool[:, 2:3],
                   scale=1.0 / 1024, waits=[m_ossl])
    p.s.activation(c.b_oss[:], c.b_oss[:], AF.Exp, scale=-0.5)
    m_rstd = p.s.mark()
    return m_rstd, gnw_rep, m_bw


def emit_attn_b_out(p, c, m_rstd, gnw_rep, m_bw):
    NCH = T // 128
    ot_use = None
    pOut_use = None
    for n in range(NCH):
        for b in range(B):
            col = b * NCH + n
            rb = (n % 8) * TOK + (b * NSLOT + n // 8) * 128
            p.v.scalar_tensor_tensor(
                c.b_on[:], c.b_obf[:, col * 128: col * 128 + 128],
                c.b_oss[:, col:col + 1], gnw_rep, ALU.mult, ALU.mult,
                waits=[m_rstd])
            m_on = p.v.mark()
            w = [m_on]
            if ot_use is not None:
                w.append(ot_use)
            p.t.transpose(c.pTbf[:], c.b_on[:], c.ident_bf[:], waits=w)
            m_ot = p.t.mark()
            p.v.tensor_copy(c.b_onT[:], c.pTbf[:], waits=[m_ot])
            m_otv = p.v.mark()
            ot_use = m_otv
            w = [m_otv, m_bw]
            if pOut_use is not None:
                w.append(pOut_use)
            for nh in range(2):
                p.t.matmul(c.pWo[nh][:], c.b_onT[:],
                           c.b_wo[:, nh * 512:(nh + 1) * 512],
                           start=True, stop=True, waits=w if nh == 0 else [])
            m_wo = p.t.mark()
            wv = [m_wo]
            if n + b > 0:
                wv.append(p.cmark("brsst"))
            p.v.tensor_copy(c.b_part[:, 0:512], c.pWo[0][:], waits=wv)
            p.v.tensor_copy(c.b_part[:, 512:1024], c.pWo[1][:])
            pOut_use = p.v.mark()
            p.sd.dma_start(out=c.cc_rs_in[rb:rb + 128, :], in_=c.b_part[:],
                           waits=[pOut_use], chan="brsst")
    m_rsst = p.cmark("brsst")
    p.g.collective_compute("ReduceScatter", ALU.add, RG8,
                           [c.cc_rs_in.opt()], [c.cc_rs_out.opt()],
                           waits=[m_rsst], inc=1, chan="cc")
    m_rs = p.cmark("cc")
    xadd_use = None
    for tt in range(NTILE):
        w = [m_rs]
        if xadd_use is not None:
            w.append(xadd_use)
        p.sd.dma_start(out=c.b_stage[:], in_=c.cc_rs_out[tt * 128:(tt + 1) * 128, :],
                       waits=w, chan="brsl")
        m_st = p.cmark("brsl")
        dst = c.x_res[:, tt * 1024:(tt + 1) * 1024]
        p.v.tensor_add(dst, dst, c.b_stage[:], waits=[m_st])
        xadd_use = p.v.mark()


def build_program(phases=ALL_PHASES):
    nc = bass.Bass("TRN2", target_bir_lowering=False)
    es = ExitStack()
    c = Ctx()
    c.nc = nc

    def sbt(name, shape, dtype=F32):
        return es.enter_context(nc.sbuf_tensor(name, shape, dtype))

    def pst(name, shape, dtype=F32):
        return es.enter_context(nc.psum_tensor(name, shape, dtype))

    x_in = nc.dram_tensor("x_c", [TOK, D], F32, kind="ExternalInput").ap()
    y_out = nc.dram_tensor("y", [TOK, D], F32, kind="ExternalOutput").ap()
    c.dbg_aT = nc.dram_tensor("dbg_aT", [128, 1024], BF16, kind="ExternalOutput").ap()
    c.dbg_acc = nc.dram_tensor("dbg_acc", [65, 2048], F32, kind="ExternalOutput").ap()
    c.dbg_kv = nc.dram_tensor("dbg_kv", [256, 2048], BF16, kind="ExternalOutput").ap()
    c.dbg_vr = nc.dram_tensor("dbg_vr", [128, 1056], BF16, kind="ExternalOutput").ap()
    need_ffn = ("ffn1" in phases) or ("ffn2" in phases)
    if "attn_b" in phases:
        c.d_wbqkvr = nc.dram_tensor("wbqkvr", [D, 512], F32, kind="ExternalInput").ap()
        c.d_wba = nc.dram_tensor("wba", [D, 1], F32, kind="ExternalInput").ap()
        c.d_wbo = nc.dram_tensor("wbo", [128, D], F32, kind="ExternalInput").ap()
        c.d_qnw = nc.dram_tensor("qnw", [1, 128], F32, kind="ExternalInput").ap()
        c.d_knw = nc.dram_tensor("knw", [1, 128], F32, kind="ExternalInput").ap()
        c.d_gnw = nc.dram_tensor("gnw", [1, 128], F32, kind="ExternalInput").ap()
        c.d_alog = nc.dram_tensor("alog", [1, 1], F32, kind="ExternalInput").ap()
        c.d_dtb = nc.dram_tensor("dtb", [1, 1], F32, kind="ExternalInput").ap()
        c.cc_xn_in = nc.dram_tensor("cc_xn_in", [TOK, D], BF16, kind="Internal").ap()
        c.cc_xn_all = nc.dram_tensor("cc_xn_all", [NCORE * TOK, D], BF16, kind="Internal").ap()
        c.cc_oss_in = nc.dram_tensor("cc_oss_in", [128, 128], F32, kind="Internal").ap()
        c.cc_oss_out = nc.dram_tensor("cc_oss_out", [128, 128], F32, kind="Internal").ap()
        c.cc_rs_in = nc.dram_tensor("cc_rs_in", [NCORE * TOK, D], BF16, kind="Internal").ap()
        c.cc_rs_out = nc.dram_tensor("cc_rs_out", [TOK, D], BF16, kind="Internal").ap()
        c.ucum_d = nc.inline_tensor(np.triu(np.ones((128, 128), np.float32)), "ucum32")
    if "attn_a" in phases:
        c.d_wqkv = nc.dram_tensor("wqkv", [D, 3 * D], F32, kind="ExternalInput").ap()
        c.d_wout = nc.dram_tensor("wout", [D, D], F32, kind="ExternalInput").ap()
        c.d_abias = nc.dram_tensor("abias", [128, ABW()], F32, kind="ExternalInput").ap()
        c.cc_v_in = nc.dram_tensor("cc_v_in", [TOK, 1024], BF16, kind="Internal").ap()
        c.cc_v_all = nc.dram_tensor("cc_v_all", [NCORE * TOK, 1024], BF16, kind="Internal").ap()
        c.cc_kt_in = nc.dram_tensor("cc_kt_in", [TOK * 8, 128], BF16, kind="Internal").ap()
        c.cc_kt_all = nc.dram_tensor("cc_kt_all", [NCORE * TOK * 8, 128], BF16, kind="Internal").ap()
        c.x1_dram = nc.dram_tensor("x1_dram", [TOK, D], F32, kind="Internal").ap()
    if need_ffn:
        c.d_rw = nc.dram_tensor("rw", [D, FFN_E], F32, kind="ExternalInput").ap()
        c.d_rb = nc.dram_tensor("rb", [1, FFN_E], F32, kind="ExternalInput").ap()
        c.d_w1 = nc.dram_tensor("w1", [FFN_E, D, 2 * FFN_H], F32, kind="ExternalInput").ap()
        c.d_w2 = nc.dram_tensor("w2", [FFN_E, FFN_H, D], F32, kind="ExternalInput").ap()
        c.d_w1sh = nc.dram_tensor("w1sh", [FFN_E, 8, 2, 128, 4096], BF16, kind="Internal").ap()
        c.d_w2sh = nc.dram_tensor("w2sh", [FFN_E, 2, 4, 128, 4096], BF16, kind="Internal").ap()

    id_bf_d = nc.inline_tensor(np.eye(128, dtype=np.float32).astype(ml_dtypes.bfloat16), "id_bf")
    id_32_d = nc.inline_tensor(np.eye(128, dtype=np.float32), "id_32")
    if "attn_a" in phases:
        _pk = np.arange(128, dtype=np.float32)
        _sl = _slopes(HEADS_A)
        diag_b = (_sl[None, :] * _pk[:, None] - SHIFT_A).astype(np.float32)
        diag_b_d = nc.inline_tensor(diag_b, "diag_b")
        um = (np.tril(np.ones((128, 128), np.float32)).T)
        mask256_np = np.concatenate([um, um], axis=1).astype(ml_dtypes.bfloat16)
        mask256_d = nc.inline_tensor(mask256_np, "mask256")
    ones_row_d = nc.inline_tensor(np.ones((1, 128), np.float32), "ones_row_c")
    ones_col_d = nc.inline_tensor(np.ones((128, 1), np.float32), "ones_col_c")

    c.x_res = None
    c.ident_bf = sbt("ident_bf", [128, 128], BF16)
    c.ident32 = sbt("ident32", [128, 128])
    c.ones_row = sbt("ones_row", [1, 128], F32)
    c.ones_col = sbt("ones_col", [128, 1], F32)
    c.cpool = sbt("cpool", [128, 4], F32)
    c.ln_nm = sbt("ln_nm", [128, 1])
    c.ln_ssq = sbt("ln_ssq", [128, 1])
    c.ln_m2 = sbt("ln_m2", [128, 1])
    c.ln_dump = sbt("ln_dump", [128, 1024])
    c.gates = sbt("gates", [128, NTILE * FFN_E])

    p = Prog()
    c.x_in = x_in

    def alloc_x_res(src_dram):
        c.x_res = es.enter_context(nc.sbuf_tensor("x_res", [128, NTILE * 1024], F32))
        for tt in range(NTILE):
            p.sd.dma_start(out=c.x_res[:, tt * 1024:(tt + 1) * 1024],
                           in_=src_dram[tt * 128:(tt + 1) * 128, :], chan="xload")
        p.barrier()

    p.sd.dma_start(out=c.ident_bf[:], in_=id_bf_d.ap())
    p.sd.dma_start(out=c.ident32[:], in_=id_32_d.ap())
    p.sd.dma_start(out=c.ones_col[:], in_=ones_col_d.ap())
    p.sd.dma_start(out=c.ones_row[:], in_=ones_row_d.ap())
    p.v.memset(c.cpool[:, 0:1], EPS_LN)
    p.v.memset(c.cpool[:, 1:2], 1.0)
    p.v.memset(c.cpool[:, 2:3], EPS_B)
    p.v.memset(c.cpool[:, 3:4], 0.0)
    p.barrier()


    phase_scopes = []
    for ph in phases:
        if c.x_res is None and ph != "attn_a":
            alloc_x_res(x_in)
        if ph == "attn_a":
            es2 = ExitStack()
            phase_scopes.append(es2)

            def sA(name, shape, dtype=F32, _es=es2):
                return _es.enter_context(nc.sbuf_tensor(name, shape, dtype))

            c.a_QTz = [sA(f"a_QTz{i}", [128, 8 * TOK], BF16) for i in range(2)]
            c.a_xst = [sA(f"a_xst{i}", [128, 1024]) for i in range(2)]
            c.diag_bias = sA("diag_bias", [128, HEADS_A])
            c.mask256 = sA("mask256s", [128, 256], BF16)
            p.sd.dma_start(out=c.diag_bias[:], in_=diag_b_d.ap())
            p.sd.dma_start(out=c.mask256[:], in_=mask256_d.ap())
            p.barrier()

            es3 = ExitStack()
            c.a_xn = es3.enter_context(nc.sbuf_tensor("a_xn", [128, 1024], BF16))
            c.a_xnT = es3.enter_context(nc.sbuf_tensor("a_xnT", [128, NTILE * 8 * 128], BF16))
            c.a_wbuf = [es3.enter_context(nc.sbuf_tensor(f"a_wbuf{i}", [128, 8 * 512], BF16)) for i in range(2)]
            c.a_ktmp = es3.enter_context(nc.sbuf_tensor("a_ktmp", [128, 512], BF16))
            c.a_ktT = es3.enter_context(nc.sbuf_tensor("a_ktT", [128, 512], BF16))
            c.pT = [es3.enter_context(nc.psum_tensor(f"a_pT{i}", [128, 128], BF16)) for i in range(2)]
            c.pQ = [es3.enter_context(nc.psum_tensor(f"a_pQ{i}", [64, min(512, TOK)])) for i in range(2)]
            c.pKV = es3.enter_context(nc.psum_tensor("a_pKV", [128, 512]))
            m_store = emit_attn_a_proj(p, c)
            es3.close()

            c.a_wout = sA("a_wout", [128, 8 * 1024], BF16)
            c.a_bias = [sA(f"a_bias{i}", [128, GMAXW()]) for i in range(2)]
            c.a_kring = [sA(f"a_kring{i}", [128, 1024], BF16) for i in range(8)]
            c.a_vring = [sA(f"a_vring{i}", [128, 1056], BF16) for i in range(8)]
            c.a_P = [sA(f"a_P{i}", [128, 2048], BF16) for i in range(8)]
            c.a_accO = sA("a_accO", [65, 2048])
            c.a_aT = sA("a_aT", [128, 1024], BF16)
            c.a_recip = sA("a_recip", [1, 128])
            c.a_rb = sA("a_rb", [64, 128])

            es4 = ExitStack()
            c.psPO = [es4.enter_context(nc.psum_tensor(f"a_psPO{i}", [65, 128])) for i in range(2)]
            c.pL = [es4.enter_context(nc.psum_tensor(f"a_pL{i}", [128, 256])) for i in range(2)]
            c.pOut = [es4.enter_context(nc.psum_tensor(f"a_pOut{i}", [128, 512])) for i in range(2)]
            emit_attn_a(p, c, m_store)
            es4.close()
            es2.close()
            p.barrier()
            alloc_x_res(c.x1_dram)
        if ph == "attn_b":
            es2 = ExitStack()
            phase_scopes.append(es2)

            def sB(name, shape, dtype=F32, _es=es2):
                return _es.enter_context(nc.sbuf_tensor(name, shape, dtype))

            RB = 7
            c.b_xn = [sB(f"b_xn{i}", [128, 1024], BF16) for i in range(2)]
            c.b_xT = [sB(f"b_xT{i}", [128, 1024], BF16) for i in range(RB)]
            c.b_wqkvr = sB("b_wqkvr", [128, 8 * 512], BF16)
            c.b_wa = sB("b_wa", [128, 8], BF16)
            c.b_wo = sB("b_wo", [128, 1024], BF16)
            c.b_rows = sB("b_rows", [1, 512])
            c.b_rep = sB("b_rep", [128, 512])
            c.b_ucum = sB("b_ucum", [128, 128])
            c.b_sqkv = [sB(f"b_sqkv{i}", [128, 384]) for i in range(RB)]
            c.b_scr = [sB(f"b_scr{i}", [128, 128]) for i in range(RB)]
            c.b_z = [sB(f"b_z{i}", [128, 1]) for i in range(RB)]
            c.b_zab = [sB(f"b_zab{i}", [128, 1]) for i in range(RB)]
            c.b_ss = [sB(f"b_ss{i}", [128, 1]) for i in range(RB)]
            c.b_qn = [sB(f"b_qn{i}", [128, 256]) for i in range(RB)]
            c.b_buf = [sB("b_buf0", [128, 128])] * RB
            c.b_v8 = [sB("b_v80", [128, 8])] * RB
            c.b_mh = [sB("b_mh0", [128, 128])] * RB
            c.b_f = [sB(f"b_f{i}", [128, 128]) for i in range(RB)]
            c.b_E = [sB(f"b_E{i}", [128, 128]) for i in range(RB)]
            c.b_Ei = [sB(f"b_Ei{i}", [128, 128]) for i in range(RB)]
            c.b_fc = [sB(f"b_fc{i}", [1, 128]) for i in range(RB)]
            c.b_efc = [sB(f"b_efc{i}", [128, 1]) for i in range(RB)]
            c.b_ef = [sB(f"b_ef{i}", [128, 128]) for i in range(RB)]
            c.b_dmp = [sB("b_dmp0", [128, 128])] * RB
            c.b_qt = [sB(f"b_qt{i}", [128, 128]) for i in range(RB)]
            c.b_kt = [sB(f"b_kt{i}", [128, 128]) for i in range(RB)]
            c.b_qtT = [sB(f"b_qtT{i}", [128, 128]) for i in range(RB)]
            c.b_ktT = [sB(f"b_ktT{i}", [128, 128]) for i in range(RB)]
            c.b_am = [sB(f"b_am{i}", [128, 128]) for i in range(RB)]
            c.b_S = [sB(f"b_S{i}", [128, 128]) for i in range(2)]
            c.b_o32 = sB("b_o32", [128, 128])
            c.b_obf = sB("b_obf", [128, 128 * 128], BF16)
            c.b_oss = sB("b_oss", [128, 128])
            c.b_on = sB("b_on", [128, 128], BF16)
            c.b_onT = sB("b_onT", [128, 128], BF16)
            c.b_part = sB("b_part", [128, 1024], BF16)
            c.b_stage = sB("b_stage", [128, 1024], BF16)

            es5 = ExitStack()
            c.pQKVR = es5.enter_context(nc.psum_tensor("b_pQKVR", [128, 512]))
            c.pA2 = es5.enter_context(nc.psum_tensor("b_pA2", [128, 1]))
            c.pF = es5.enter_context(nc.psum_tensor("b_pF", [128, 128]))
            c.pSmall = es5.enter_context(nc.psum_tensor("b_pSmall", [128, 128]))
            c.pT32b = [es5.enter_context(nc.psum_tensor(f"b_pT32{i}", [128, 128])) for i in range(2)]
            c.pO = es5.enter_context(nc.psum_tensor("b_pO", [128, 128]))
            c.pS = es5.enter_context(nc.psum_tensor("b_pS", [128, 128]))
            m_rstd, gnw_rep, m_bw = emit_attn_b(p, c)
            es5.close()
            es6 = ExitStack()
            c.pTbf = es6.enter_context(nc.psum_tensor("b_pTbf", [128, 128], BF16))
            c.pWo = [es6.enter_context(nc.psum_tensor(f"b_pWo{i}", [128, 512])) for i in range(2)]
            emit_attn_b_out(p, c, m_rstd, gnw_rep, m_bw)
            es6.close()
            es2.close()
            p.barrier()
        if ph in ("ffn1", "ffn2"):
            es2 = ExitStack()
            phase_scopes.append(es2)

            def s2(name, shape, dtype=F32, _es=es2, _ph=ph):
                return _es.enter_context(nc.sbuf_tensor(name + "_" + _ph, shape, dtype))

            def p2(name, shape, dtype=F32, _es=es2, _ph=ph):
                return _es.enter_context(nc.psum_tensor(name + "_" + _ph, shape, dtype))

            c.f_xn = s2("f_xn", [128, 1024], BF16)
            c.f_xnT = s2("f_xnT", [128, 8 * NTILE * 128], BF16)
            c.f_actT = s2("f_actT", [128, 32 * 4 * 128], BF16)
            c.f_w1a = [s2(f"f_w1a{i}", [128, 8 * 512], BF16) for i in range(2)]
            c.f_w1g = [s2(f"f_w1g{i}", [128, 8 * 512], BF16) for i in range(2)]
            c.f_w2c = [s2(f"f_w2c{i}", [128, 8 * 512], BF16) for i in range(2)]
            c.f_rw = s2("f_rw", [128, 8 * FFN_E])
            c.f_xn32 = s2("f_xn32", [128, 1024])
            c.f_xnT32 = s2("f_xnT32", [128, 1024])
            c.f_rb_row = s2("f_rb_row", [1, FFN_E])
            c.f_rb_rep = s2("f_rb_rep", [128, FFN_E])
            c.f_rlog = s2("f_rlog", [128, FFN_E])
            c.f_rmax = s2("f_rmax", [128, 1])
            c.f_esum = s2("f_esum", [128, 1])
            c.f_silu = [s2(f"f_silu{i}", [128, 512]) for i in range(2)]
            c.f_mask = s2("f_mask", [128, FFN_E])
            es3 = ExitStack()
            c.pT = [es3.enter_context(nc.psum_tensor(f"pT{i}_" + ph, [128, 128], BF16)) for i in range(2)]
            c.pT32 = [es3.enter_context(nc.psum_tensor(f"pT32{i}_" + ph, [128, 128])) for i in range(2)]
            c.pR8 = es3.enter_context(nc.psum_tensor("pR8_" + ph, [128, FFN_E]))
            m_gates = emit_ffn_prologue(p, c)
            es3.close()
            c.pA = [p2(f"pA{i}", [128, 512]) for i in range(2)]
            c.pG = [p2(f"pG{i}", [128, 512]) for i in range(2)]
            c.pY = [p2(f"pY{i}", [128, 512]) for i in range(4)]
            last_ffn = (ph == "ffn2") or (ph == phases[-1])
            tstore = None
            if last_ffn:
                def tstore(tt, mk, _p=p):
                    _p.sd.dma_start(out=y_out[tt * 128:(tt + 1) * 128, :],
                                    in_=c.x_res[:, tt * 1024:(tt + 1) * 1024],
                                    waits=[mk], chan="yst")
            emit_ffn_experts(p, c, m_gates, grp_tiles=4,
                             img_ready=(ph == "ffn2" and "ffn1" in phases),
                             tile_store=tstore)
            if last_ffn:
                c.y_stored = True
            es2.close()
            p.barrier()

    if not getattr(c, "y_stored", False):
        for tt in range(NTILE):
            p.sd.dma_start(out=y_out[tt * 128:(tt + 1) * 128, :],
                           in_=c.x_res[:, tt * 1024:(tt + 1) * 1024], chan="yst")
    p.barrier()

    block = es.enter_context(nc.Block())
    with es:
        p.finalize(block, nc, es)
    return nc


def shard_tokens(x):
    xt = np.ascontiguousarray(x.reshape(B, T // 128, 128, D))
    out = []
    for cid in range(NCORE):
        sl = xt[:, cid::8]
        out.append(np.ascontiguousarray(sl.reshape(TOK, D)))
    return out

def unshard_tokens(parts):
    y = np.empty((B, T // 128, 128, D), np.float32)
    for cid in range(NCORE):
        y[:, cid::8] = parts[cid].reshape(B, NSLOT, 128, D)
    return y.reshape(B, T, D)


def make_abias(cid):
    sl = _slopes(HEADS_A)
    pk = np.arange(128, dtype=np.float32)
    blocks = []
    for tt in range(NTILE):
        i = tt % NSLOT
        g = cid + 8 * i
        gmax = 8 * i + 7
        blk = np.full((128, gmax * 16), NEG, np.float32)
        for j in range(gmax):
            if j < g:
                for h in range(HEADS_A):
                    blk[:, j * 16 + h] = sl[h] * (128.0 * (j - g) + pk) - SHIFT_A
        blocks.append(blk)
    return np.ascontiguousarray(np.concatenate(blocks, axis=1))


_NC_CACHE = {}


def kernel(L0_attn_a_x, L0_attn_a_Wqkv, L0_attn_a_Wout, L0_attn_b_Wq, L0_attn_b_Wk,
           L0_attn_b_Wv, L0_attn_b_Wr, L0_attn_b_Wa, L0_attn_b_A_log, L0_attn_b_dt_bias,
           L0_attn_b_q_norm_w, L0_attn_b_k_norm_w, L0_attn_b_g_norm_w, L0_attn_b_Wo,
           L0_ffn_router_w, L0_ffn_router_b, L0_ffn_w1, L0_ffn_w2):
    phases = ALL_PHASES
    key = phases
    if key not in _NC_CACHE:
        _NC_CACHE[key] = build_program(phases)
    nc = _NC_CACHE[key]
    xs = shard_tokens(np.asarray(L0_attn_a_x, np.float32))
    in_maps = []
    for cid in range(NCORE):
        m = dict(x_c=xs[cid],
                 rw=np.ascontiguousarray(L0_ffn_router_w),
                 rb=np.ascontiguousarray(L0_ffn_router_b).reshape(1, FFN_E),
                 w1=np.ascontiguousarray(L0_ffn_w1),
                 w2=np.ascontiguousarray(L0_ffn_w2),
                 wqkv=np.ascontiguousarray(L0_attn_a_Wqkv),
                 wout=np.ascontiguousarray(L0_attn_a_Wout),
                 abias=make_abias(cid),
                 wbqkvr=np.ascontiguousarray(np.concatenate(
                     [np.asarray(W)[:, cid * 128:(cid + 1) * 128] for W in
                      (L0_attn_b_Wq, L0_attn_b_Wk, L0_attn_b_Wv, L0_attn_b_Wr)],
                     axis=1)),
                 wba=np.ascontiguousarray(np.asarray(L0_attn_b_Wa)[:, cid:cid + 1]),
                 wbo=np.ascontiguousarray(np.asarray(L0_attn_b_Wo)[cid * 128:(cid + 1) * 128, :]),
                 qnw=np.ascontiguousarray(np.asarray(L0_attn_b_q_norm_w).reshape(1, 128)),
                 knw=np.ascontiguousarray(np.asarray(L0_attn_b_k_norm_w).reshape(1, 128)),
                 gnw=np.ascontiguousarray(np.asarray(L0_attn_b_g_norm_w)[cid * 128:(cid + 1) * 128].reshape(1, 128)),
                 alog=np.asarray(L0_attn_b_A_log)[cid].reshape(1, 1).astype(np.float32),
                 dtb=np.asarray(L0_attn_b_dt_bias)[cid].reshape(1, 1).astype(np.float32))
        in_maps.append(m)
    res = run_bass_kernel_spmd(nc, in_maps, core_ids=list(range(NCORE)))
    return unshard_tokens([r["y"] for r in res.results]).astype(np.float32)
